# version 05
# baseline (speedup 1.0000x reference)
import os
import numpy as np
import ml_dtypes
import concourse.bass as bass
import concourse.mybir as mybir
from concourse.bass_utils import run_bass_kernel_spmd

F32 = mybir.dt.float32
BF16 = mybir.dt.bfloat16
AF = mybir.ActivationFunctionType
ALU = mybir.AluOpType
AX = mybir.AxisListType.X

B, T, D = 2, 8192, 1024
SEG = 2048
NEG = -1.0e38


def _f32(x):
    return np.ascontiguousarray(np.asarray(x), dtype=np.float32)


def ctile(vec):
    return _f32(np.asarray(vec).reshape(8, 128).T)



class Sched:

    def __init__(self, nc):
        self.nc = nc
        self.items = []
        self.src_cnt = {}

    def mark(self):
        return dict(self.src_cnt)

    def cur(self, eng, kind="c"):
        src = (eng, kind)
        return (src, self.src_cnt.get(src, 0))

    def add(self, eng, fn, *, dma=False, after=None, deps=None, stream="0"):
        src = (eng, "d:" + stream) if dma else (eng, "c")
        k = 16 if dma else 1
        if deps is not None:
            req = {}
            for t in deps:
                if t is None:
                    continue
                (s2, v) = t
                if req.get(s2, 0) < v:
                    req[s2] = v
            if not dma and eng != "tensor":
                v = self.src_cnt.get((eng, "c"), 0)
                if v > 0:
                    req[(eng, "c")] = max(req.get((eng, "c"), 0), v)
        else:
            req = dict(after) if after is not None else dict(self.src_cnt)
            if not dma and eng == "tensor":
                req.pop((eng, "c"), None)
        req = {s2: v for s2, v in req.items() if v > 0}
        self.items.append((eng, src, k, fn, req))
        self.src_cnt[src] = self.src_cnt.get(src, 0) + k
        return (src, self.src_cnt[src])

    def emit(self, name=""):
        nc = self.nc
        sems = {s2: nc.alloc_semaphore(f"ch{name}_{s2[0]}_{s2[1].replace(':', '')}")
                for s2 in self.src_cnt}
        used = []
        for it in self.items:
            if it[0] not in used:
                used.append(it[0])
        with nc.Block() as blk:
            for eng_name in used:
                def mk(eng_name):
                    def body(e):
                        waited = {}
                        for (en, src, k, fn, req) in self.items:
                            if en != eng_name:
                                continue
                            for s2, v in req.items():
                                if waited.get(s2, 0) < v:
                                    e.wait_ge(sems[s2], v)
                                    waited[s2] = v
                            fn(e).then_inc(sems[src], k)
                    return body
                getattr(blk, eng_name)(mk(eng_name))
        with nc.Block() as blk2:
            def fin(e):
                for s2, v in self.src_cnt.items():
                    e.wait_ge(sems[s2], v)
            blk2.sync(fin)


class Chain:

    def __init__(self, s, deps=None):
        self.s = s
        self.deps = list(deps or [])
        self.prev = None

    def step(self, eng, fn, *, dma=False, extra=None):
        d = list(self.deps)
        if self.prev is not None:
            d.append(self.prev)
        if extra:
            d.extend(extra)
        self.prev = self.s.add(eng, fn, dma=dma, deps=d)
        self.deps = []
        return self.prev


class Ctx:
    pass



def host_consts_l01():
    return dict(
        c_ident=_f32(np.eye(128)),
        c_trrev=_f32(np.broadcast_to((2047.0 - np.arange(2048))[None, :], (128, 2048))),
        c_iota8=_f32(np.broadcast_to(np.arange(8)[None, :], (128, 8))),
    )



def alloc_front(nc, want_acc):
    c = Ctx()
    c.nc = nc
    c.ident = nc.alloc_sbuf_tensor("ident", [128, 128], F32)
    c.XS = 2064
    c.xnF = nc.alloc_sbuf_tensor("xnF", [128, 8 * c.XS], BF16)
    c.accF = nc.alloc_sbuf_tensor("accF", [128, 8 * SEG], F32) if want_acc else None
    c.xtm = nc.alloc_sbuf_tensor("xtm", [128, 2048], F32)
    c.lnt = nc.alloc_sbuf_tensor("lnt", [128, 1024], F32)
    c.lns = nc.alloc_sbuf_tensor("lns", [128, 1024], F32)
    c.lnm = nc.alloc_sbuf_tensor("lnm", [128, 4], F32)
    c.row1 = nc.alloc_sbuf_tensor("row1", [1, 1024], F32)
    c.rowm = nc.alloc_sbuf_tensor("rowm", [1, 4], F32)
    c.cst = nc.alloc_sbuf_tensor("cst", [128, 4], F32)
    c.pT = nc.alloc_psum_tensor("pT", [128, 128], F32)
    c.pT2 = nc.alloc_psum_tensor("pT2", [128, 128], F32)
    c.pM = nc.alloc_psum_tensor("pM", [128, 512], F32)
    c.pM2 = nc.alloc_psum_tensor("pM2", [128, 512], F32)
    return c


def xn_col(c, j, t):
    return c.XS * j + 1 + t


def emit_consts(s, c):
    for (k, v) in ((0, 1e-5), (1, 1e-9), (2, 1e-6)):
        s.add("vector", lambda e, k=k, v=v: e.memset(c.cst[:, k:k + 1], v))


def emit_ln_tile(s, c, xv, out, mcol):
    s.add("vector", lambda e: e.reduce_sum(mcol[:, 0:1], xv, AX))
    s.add("vector", lambda e: e.tensor_scalar_mul(mcol[:, 0:1], mcol[:, 0:1], 1.0 / 1024.0))
    s.add("vector", lambda e: e.tensor_scalar_sub(c.lnt[0:xv.shape[0], :], xv, mcol[:, 0:1]))
    s.add("scalar", lambda e: e.square(c.lns[0:xv.shape[0], :], c.lnt[0:xv.shape[0], :]))
    s.add("vector", lambda e: e.reduce_sum(mcol[:, 1:2], c.lns[0:xv.shape[0], :], AX))
    s.add("vector", lambda e: e.tensor_scalar_mul(mcol[:, 1:2], mcol[:, 1:2], 1.0 / 1024.0))
    s.add("scalar", lambda e: e.activation(mcol[:, 2:3], mcol[:, 1:2], AF.Ln,
                                        bias=c.cst[0:xv.shape[0], 0:1]))
    s.add("scalar", lambda e: e.activation(mcol[:, 3:4], mcol[:, 2:3], AF.Exp, scale=-0.5))
    s.add("vector", lambda e: e.tensor_scalar_mul(out, c.lnt[0:xv.shape[0], :], mcol[:, 3:4]))


def emit_front_end(s, c, x_ap, want_acc):
    for i in range(16):
        half = (i % 2) * 1024
        xv = c.xtm[:, half:half + 1024]
        s.add("sync", lambda e, i=i, xv=xv: e.dma_start(
            out=xv, in_=x_ap[1 + 128 * i:1 + 128 * (i + 1), :]), dma=True)
        emit_ln_tile(s, c, xv, c.lns[:, :], c.lnm)
        for j in range(8):
            s.add("tensor", lambda e, j=j: e.transpose(
                c.pT[:, :], c.lns[:, 128 * j:128 * (j + 1)], c.ident[:, :]))
            s.add("scalar", lambda e, i=i, j=j: e.copy(
                c.xnF[:, xn_col(c, j, 128 * i): xn_col(c, j, 128 * (i + 1))],
                c.pT[:, :]))
            if want_acc:
                s.add("tensor", lambda e, j=j, xv=xv: e.transpose(
                    c.pT2[:, :], xv[:, 128 * j:128 * (j + 1)], c.ident[:, :]))
                s.add("vector", lambda e, i=i, j=j: e.tensor_copy(
                    c.accF[:, SEG * j + 128 * i: SEG * j + 128 * (i + 1)],
                    c.pT2[:, :]))
    s.add("sync", lambda e: e.dma_start(out=c.row1[:, :], in_=x_ap[0:1, :]), dma=True)
    emit_ln_tile(s, c, c.row1[:, :], c.row1[:, :], c.rowm)
    for j in range(8):
        s.add("tensor", lambda e, j=j: e.transpose(
            c.pT[:, 0:1], c.row1[0:1, 128 * j:128 * (j + 1)], c.ident[0:1, 0:1]))
        s.add("scalar", lambda e, j=j: e.copy(
            c.xnF[:, c.XS * j: c.XS * j + 1], c.pT[:, 0:1]))


def alloc_wkv_small(nc, c):
    c.td = nc.alloc_sbuf_tensor("td", [128, 8], F32)
    c.w8 = nc.alloc_sbuf_tensor("w8", [128, 8], F32)
    c.tf = nc.alloc_sbuf_tensor("tf", [128, 8], F32)
    c.tmk = nc.alloc_sbuf_tensor("tmk", [128, 8], F32)
    c.tmv = nc.alloc_sbuf_tensor("tmv", [128, 8], F32)
    c.tmr = nc.alloc_sbuf_tensor("tmr", [128, 8], F32)
    c.tmk1 = nc.alloc_sbuf_tensor("tmk1", [128, 8], F32)
    c.tmv1 = nc.alloc_sbuf_tensor("tmv1", [128, 8], F32)
    c.tmr1 = nc.alloc_sbuf_tensor("tmr1", [128, 8], F32)
    c.trrev = nc.alloc_sbuf_tensor("trrev", [128, 2048], F32)
    c.wbuf = nc.alloc_sbuf_tensor("wbuf", [128, 8192], BF16)


def emit_wkv_prep(s, c, with_r=True):
    s.add("scalar", lambda e: e.activation(c.w8[:, :], c.td[:, :], AF.Exp))
    s.add("vector", lambda e: e.tensor_scalar_mul(c.w8[:, :], c.w8[:, :], -1.0))
    pairs = [(c.tmk1, c.tmk), (c.tmv1, c.tmv)] + ([(c.tmr1, c.tmr)] if with_r else [])
    for (dst, src) in pairs:
        s.add("vector", lambda e, dst=dst, src=src: e.tensor_scalar(
            dst[:, :], src[:, :], -1.0, 1.0, ALU.mult, ALU.add))


def emit_timemix(s, c, t0, ntok, mix, mix1, out_bf, scratch):
    for j in range(8):
        xn_v = c.xnF[:, xn_col(c, j, t0): xn_col(c, j, t0 + ntok)]
        xx_v = c.xnF[:, xn_col(c, j, t0 - 1): xn_col(c, j, t0 - 1 + ntok)]
        ob = out_bf[:, ntok * j:ntok * (j + 1)]
        s.add("vector", lambda e, xx_v=xx_v, j=j: e.tensor_scalar_mul(
            scratch[:, 0:ntok], xx_v, mix1[:, j:j + 1]))
        s.add("vector", lambda e, xn_v=xn_v, j=j, ob=ob: e.scalar_tensor_tensor(
            ob, xn_v, mix[:, j:j + 1], scratch[:, 0:ntok], ALU.mult, ALU.add))


def emit_wstream(s, c, w_ap, dst=None, cols=1024, nck=8, deps=None, stream="0"):
    dst = dst if dst is not None else c.wbuf
    src3 = w_ap.rearrange("(kc p) n -> p kc n", p=128)
    dst3 = dst[:, 0:nck * cols].rearrange("p (kc n) -> p kc n", n=cols)
    return s.add("gpsimd", lambda e: e.dma_start(out=dst3, in_=src3),
                 dma=True, deps=deps, stream=stream)


def emit_mm8(s, c, psum, out_cb, rhs_fn, n, wbuf=None, wcols=1024, nck=8,
             deps=None):
    wbuf = wbuf if wbuf is not None else c.wbuf

    def fn(e):
        last = None
        for kc in range(nck):
            lhsT = wbuf[:, kc * wcols + out_cb * 128: kc * wcols + out_cb * 128 + 128]
            last = e.matmul(psum[:, 0:n], lhsT, rhs_fn(kc), start=(kc == 0),
                            stop=(kc == nck - 1))
        return last
    return s.add("tensor", fn, deps=deps)



def _mknc():
    rc = os.environ.get("MK_NORC") != "1"
    return bass.Bass("TRN2", target_bir_lowering=False, detect_race_conditions=rc)


def build_launch0():
    nc = _mknc()
    x_in = nc.dram_tensor("x", [2049, 1024], F32, kind="ExternalInput")
    wkT_in = nc.dram_tensor("wkT", [1024, 1024], F32, kind="ExternalInput")
    wvT_in = nc.dram_tensor("wvT", [1024, 1024], F32, kind="ExternalInput")
    td_in = nc.dram_tensor("td_ct", [128, 8], F32, kind="ExternalInput")
    tmk_in = nc.dram_tensor("tmk_ct", [128, 8], F32, kind="ExternalInput")
    tmv_in = nc.dram_tensor("tmv_ct", [128, 8], F32, kind="ExternalInput")
    ident_in = nc.dram_tensor("c_ident", [128, 128], F32, kind="ExternalInput")
    trrev_in = nc.dram_tensor("c_trrev", [128, 2048], F32, kind="ExternalInput")
    wsum_out = nc.dram_tensor("wsum", [128, 24], F32, kind="ExternalOutput")

    c = alloc_front(nc, want_acc=False)
    alloc_wkv_small(nc, c)
    wbuf2 = nc.alloc_sbuf_tensor("wbuf2", [128, 8192], BF16)
    kq = nc.alloc_sbuf_tensor("kq", [128, 2048], F32)
    vq = nc.alloc_sbuf_tensor("vq", [128, 2048], F32)
    sc1 = nc.alloc_sbuf_tensor("sc1", [128, 2048], F32)
    sc2 = nc.alloc_sbuf_tensor("sc2", [128, 2048], F32)
    xkf = nc.alloc_sbuf_tensor("xkf", [128, 8 * 2048], BF16)
    xvf = nc.alloc_sbuf_tensor("xvf", [128, 8 * 2048], BF16)
    out_sb = nc.alloc_sbuf_tensor("out_sb", [128, 24], F32)
    mixsc = nc.alloc_sbuf_tensor("mixsc", [128, 2048], F32)

    s = Sched(nc)
    for (dst, src) in ((c.ident, ident_in), (c.td, td_in), (c.tmk, tmk_in),
                       (c.tmv, tmv_in), (c.trrev, trrev_in)):
        s.add("sync", lambda e, dst=dst, src=src: e.dma_start(
            out=dst[:, :], in_=src.ap()), dma=True)
    emit_consts(s, c)
    emit_front_end(s, c, x_in.ap(), want_acc=False)
    emit_wkv_prep(s, c, with_r=False)
    emit_timemix(s, c, 0, 2048, c.tmk, c.tmk1, xkf, mixsc)
    emit_timemix(s, c, 0, 2048, c.tmv, c.tmv1, xvf, mixsc)
    emit_wstream(s, c, wkT_in.ap(), dst=c.wbuf)
    emit_wstream(s, c, wvT_in.ap(), dst=wbuf2)
    for j in range(8):
        for tc in range(4):
            emit_mm8(s, c, c.pM, j,
                     lambda kc, tc=tc: xkf[:, 2048 * kc + 512 * tc: 2048 * kc + 512 * (tc + 1)],
                     512, wbuf=c.wbuf)
            s.add("vector", lambda e, tc=tc: e.tensor_copy(
                kq[:, 512 * tc:512 * (tc + 1)], c.pM[:, :]))
            emit_mm8(s, c, c.pM2, j,
                     lambda kc, tc=tc: xvf[:, 2048 * kc + 512 * tc: 2048 * kc + 512 * (tc + 1)],
                     512, wbuf=wbuf2)
            s.add("vector", lambda e, tc=tc: e.tensor_copy(
                vq[:, 512 * tc:512 * (tc + 1)], c.pM2[:, :]))
        s.add("vector", lambda e, j=j: e.scalar_tensor_tensor(
            sc1[:, :], c.trrev[:, :], c.w8[:, j:j + 1], kq[:, :], ALU.mult, ALU.add))
        s.add("vector", lambda e, j=j: e.reduce_max(out_sb[:, j:j + 1], sc1[:, :], AX))
        s.add("vector", lambda e, j=j: e.tensor_scalar_sub(
            sc1[:, :], sc1[:, :], out_sb[:, j:j + 1]))
        s.add("scalar", lambda e: e.activation(sc2[:, :], sc1[:, :], AF.Exp))
        s.add("vector", lambda e, j=j: e.reduce_sum(
            out_sb[:, 16 + j:16 + j + 1], sc2[:, :], AX))
        s.add("vector", lambda e: e.tensor_mul(sc1[:, :], sc2[:, :], vq[:, :]))
        s.add("vector", lambda e, j=j: e.reduce_sum(
            out_sb[:, 8 + j:8 + j + 1], sc1[:, :], AX))
    s.add("sync", lambda e: e.dma_start(out=wsum_out.ap(), in_=out_sb[:, :]), dma=True)
    s.emit("L0")
    return nc


def host_inputs_launch0(inputs, consts):
    x = np.asarray(inputs["L0_attn_x"], np.float32)
    wkT = _f32(np.asarray(inputs["L0_attn_Wk"]).T)
    wvT = _f32(np.asarray(inputs["L0_attn_Wv"]).T)
    td = ctile(inputs["L0_attn_time_decay"])
    tmk = ctile(np.asarray(inputs["L0_attn_time_mix_k"]).reshape(-1))
    tmv = ctile(np.asarray(inputs["L0_attn_time_mix_v"]).reshape(-1))
    maps = []
    for core in range(8):
        b, sq = core // 4, core % 4
        xe = np.zeros((2049, 1024), np.float32)
        xe[1:] = x[b, sq * SEG:(sq + 1) * SEG]
        if sq > 0:
            xe[0] = x[b, sq * SEG - 1]
        maps.append(dict(x=xe, wkT=wkT, wvT=wvT, td_ct=td, tmk_ct=tmk,
                         tmv_ct=tmv, c_ident=consts["c_ident"],
                         c_trrev=consts["c_trrev"]))
    return maps



CH = 512
NQ = 4


def build_launch1(debug=False):
    nc = _mknc()
    inp = {}
    def I(name, shape):
        inp[name] = nc.dram_tensor(name, shape, F32, kind="ExternalInput")
        return inp[name]
    x_in = I("x", [2049, 1024])
    wkT_in = I("wkT", [1024, 1024]); wvT_in = I("wvT", [1024, 1024])
    wrT_in = I("wrT", [1024, 1024]); woT_in = I("woT", [1024, 1024])
    td_in = I("td_ct", [128, 8]); tf_in = I("tf_ct", [128, 8])
    tmk_in = I("tmk_ct", [128, 8]); tmv_in = I("tmv_ct", [128, 8])
    tmr_in = I("tmr_ct", [128, 8])
    carry_in = I("carry", [128, 72])
    wg_in = I("wg", [1024, 8])
    w1_in = I("w1", [8, 1024, 4096]); b1_in = I("b1ct", [128, 288])
    w2_in = I("w2", [8, 4096, 1024])
    b2_in = I("b2", [8, 1024]); bs2_in = I("bs2", [1, 1024])
    ws1_in = I("ws1", [1024, 4096]); ws2_in = I("ws2", [4096, 1024])
    wrkv_in = I("wrkv", [1024, 3072]); brkv_in = I("brkv_ct", [128, 24])
    nul_in = I("nul_ct", [128, 8])
    ident_in = I("c_ident", [128, 128]); trrev_in = I("c_trrev", [128, 2048])
    iota_in = I("c_iota8", [128, 8])
    sel_in = nc.dram_tensor("c_sel", [8, 1024], BF16, kind="ExternalInput")
    x1_out = nc.dram_tensor("x1", [2048, 1024], F32, kind="ExternalOutput")
    usum_out = nc.dram_tensor("usum", [128, 16], F32, kind="ExternalOutput")
    x1_acc = nc.dram_tensor("x1_acc", [128, 16384], F32, kind="Internal")
    if debug:
        dbg_x = nc.dram_tensor("dbg_x", [128, 16384], F32, kind="ExternalOutput")
        dbg_p2 = nc.dram_tensor("dbg_p2", [4, 128, 8192], F32, kind="ExternalOutput")
        dbg_kvr = nc.dram_tensor("dbg_kvr", [4, 4, 128, 4096], F32, kind="ExternalOutput")
        dbg_st = nc.dram_tensor("dbg_st", [4, 128, 24], F32, kind="ExternalOutput")
    act_hbm = nc.dram_tensor("act_hbm", [9, 8, 4, 128, 2048], BF16, kind="Internal")

    AR = nc.alloc_sbuf_tensor("AR", [128, 20480], F32)
    ARB = nc.alloc_sbuf_tensor("ARB", [128, 41088], BF16)
    def Fv(a, n):
        return AR[:, a:a + n]
    def Bv(a, n):
        return ARB[:, a:a + n]

    c = Ctx(); c.nc = nc; c.XS = 2064
    c.ident = nc.alloc_sbuf_tensor("ident", [128, 128], F32)
    c.cst = nc.alloc_sbuf_tensor("cst", [128, 4], F32)
    c.xnF = ARB[:, 0:16512]
    c.xtm = AR[:, 0:2048]
    c.lnt = AR[:, 2048:3072]
    c.lns = AR[:, 3072:4096]
    c.lnsB = AR[:, 12288:13312]
    c.lnm = nc.alloc_sbuf_tensor("lnm", [128, 8], F32)
    c.rowm = nc.alloc_sbuf_tensor("rowm", [1, 4], F32)
    c.row1 = AR[0:1, 4096:5120]
    c.pT = nc.alloc_psum_tensor("pT", [128, 128], F32)
    c.pT2 = nc.alloc_psum_tensor("pT2", [128, 128], F32)
    c.pM = nc.alloc_psum_tensor("pM", [128, 512], F32)
    c.pM2 = nc.alloc_psum_tensor("pM2", [128, 512], F32)
    c.pCW = nc.alloc_psum_tensor("pCW", [128, 512], F32)
    c.td = nc.alloc_sbuf_tensor("td", [128, 8], F32)
    c.w8 = nc.alloc_sbuf_tensor("w8", [128, 8], F32)
    c.tf = nc.alloc_sbuf_tensor("tf", [128, 8], F32)
    c.tmk = nc.alloc_sbuf_tensor("tmk", [128, 8], F32)
    c.tmv = nc.alloc_sbuf_tensor("tmv", [128, 8], F32)
    c.tmr = nc.alloc_sbuf_tensor("tmr", [128, 8], F32)
    c.tmk1 = nc.alloc_sbuf_tensor("tmk1", [128, 8], F32)
    c.tmv1 = nc.alloc_sbuf_tensor("tmv1", [128, 8], F32)
    c.tmr1 = nc.alloc_sbuf_tensor("tmr1", [128, 8], F32)
    c.trrev = AR[:, 17408:19456]
    c.wbuf = ARB[:, 16512:24704]

    STG = Fv(4096, 1024)
    STG2 = Fv(13312, 1024)
    KQ = Fv(5120, 4096)
    VQ = Fv(9216, 4096)
    PE_ = Fv(13312, 513); SN_ = Fv(13825, 513); SD_ = Fv(14338, 513)
    T1 = Fv(14851, 512); T2 = Fv(15363, 512); T3 = Fv(15875, 512)
    T4 = Fv(16387, 512); AL = Fv(16899, 512); CD = Fv(17411, 512)
    CN = Fv(17923, 512); E1B = Fv(18435, 512); E2B = Fv(18947, 512)
    WREP = Fv(19459, 512)
    XQ = Bv(24704, 4096)
    RQ = Bv(28800, 4096)
    RWQ = Bv(32896, 4096)
    st = nc.alloc_sbuf_tensor("st", [128, 24], F32)
    w2048 = nc.alloc_sbuf_tensor("w2048", [128, 8], F32)
    fold8 = nc.alloc_sbuf_tensor("fold8", [128, 40], F32)
    carry_sb = nc.alloc_sbuf_tensor("carry_sb", [128, 72], F32)
    lgt = AR[0:8, 9216:11264]
    cw_tm = nc.alloc_sbuf_tensor("cw_tm", [128, 128], F32)
    cm = nc.alloc_sbuf_tensor("cm", [128, 8], F32)
    ptile = nc.alloc_sbuf_tensor("ptile", [128, 8], F32)
    pt2_ = nc.alloc_sbuf_tensor("pt2_", [128, 8], F32)
    eqt = nc.alloc_sbuf_tensor("eqt", [128, 8], F32)
    eq2t = nc.alloc_sbuf_tensor("eq2t", [128, 8], F32)
    p2t = nc.alloc_sbuf_tensor("p2t", [128, 8], F32)
    iota8 = nc.alloc_sbuf_tensor("iota8", [128, 8], F32)
    wgb = nc.alloc_sbuf_tensor("wgb", [128, 64], BF16)
    cwT8 = nc.alloc_sbuf_tensor("cwT8", [8, 2048], BF16)
    selb = nc.alloc_sbuf_tensor("selb", [8, 1024], BF16)
    cwB4 = nc.alloc_sbuf_tensor("cwB4", [128, 2048], BF16)
    ones_row_bf = nc.alloc_sbuf_tensor("ones_row_bf", [1, 128], BF16)
    ones_col = nc.alloc_sbuf_tensor("ones_col", [128, 1], F32)
    ones_rowf = nc.alloc_sbuf_tensor("ones_rowf", [1, 128], F32)
    b2b = nc.alloc_sbuf_tensor("b2b", [8, 1024], BF16)
    bs2b = nc.alloc_sbuf_tensor("bs2b", [1, 1024], BF16)
    row01 = nc.alloc_sbuf_tensor("row01", [1, 512], BF16)
    b1sb = nc.alloc_sbuf_tensor("b1sb", [128, 288], F32)
    nu8 = nc.alloc_sbuf_tensor("nu8", [128, 8], F32)
    nun8 = nc.alloc_sbuf_tensor("nun8", [128, 8], F32)
    usum_sb = nc.alloc_sbuf_tensor("usum_sb", [128, 16], F32)
    srow = nc.alloc_sbuf_tensor("srow", [1, 1536], F32)
    brkv = nc.alloc_sbuf_tensor("brkv", [128, 24], F32)

    s = Sched(nc)
    for (dst, src) in ((c.ident, ident_in), (c.td, td_in), (c.tf, tf_in),
                       (c.tmk, tmk_in), (c.tmv, tmv_in), (c.tmr, tmr_in),
                       (carry_sb, carry_in), (iota8, iota_in)):
        s.add("sync", lambda e, dst=dst, src=src: e.dma_start(
            out=dst[:, :], in_=src.ap()), dma=True, stream="sm")
    s.add("gpsimd", lambda e: e.dma_start(out=b2b[:, :], in_=b2_in.ap()), dma=True, stream="sm")
    s.add("sync", lambda e: e.dma_start(out=b1sb[:, :], in_=b1_in.ap()), dma=True, stream="sm")
    s.add("gpsimd", lambda e: e.dma_start(out=bs2b[:, :], in_=bs2_in.ap()), dma=True, stream="sm")
    s.add("sync", lambda e: e.dma_start(out=selb[:, :], in_=sel_in.ap()), dma=True, stream="sm")
    emit_consts(s, c)
    s.add("vector", lambda e: e.memset(ones_row_bf[:, :], 1.0))
    s.add("vector", lambda e: e.memset(ones_col[:, :], 1.0))
    s.add("vector", lambda e: e.memset(ones_rowf[:, :], 1.0))
    s.add("vector", lambda e: e.memset(row01[:, :], 0.1))
    tok_smalls_s = s.cur("sync", "d:sm")
    tok_smalls_g = s.cur("gpsimd", "d:sm")
    tok_smalls_v = s.cur("vector")

    x1a = x1_acc.ap().rearrange("p (j t) -> p j t", t=2048)
    ld_tok = [None, None]
    tp_done = [None, None]
    stg_dma = [None, None]
    init_toks = []
    for i in range(16):
        par = i % 2
        xv = c.xtm[:, par * 1024:par * 1024 + 1024]
        lnsv = c.lns if par == 0 else c.lnsB
        ld_tok[par] = s.add("sync", lambda e, i=i, xv=xv: e.dma_start(
            out=xv, in_=x_in.ap()[1 + 128 * i:1 + 128 * (i + 1), :]),
            dma=True, deps=[tp_done[par], tok_smalls_s], stream=f"ld{par}")
        ch = Chain(s, deps=[ld_tok[par], tp_done[par], tok_smalls_v])
        mc = c.lnm[:, 4 * par:4 * par + 4]
        ch.step("vector", lambda e, xv=xv, mc=mc: e.reduce_sum(mc[:, 0:1], xv, AX))
        ch.step("vector", lambda e, mc=mc: e.tensor_scalar_mul(mc[:, 0:1], mc[:, 0:1], 1.0 / 1024.0))
        ch.step("vector", lambda e, xv=xv, mc=mc: e.tensor_scalar_sub(c.lnt[:, :], xv, mc[:, 0:1]))
        ch.step("scalar", lambda e: e.square(KQ[:, 0:1024], c.lnt[:, :]))
        ch.step("vector", lambda e, mc=mc: e.reduce_sum(mc[:, 1:2], KQ[:, 0:1024], AX))
        ch.step("vector", lambda e, mc=mc: e.tensor_scalar_mul(mc[:, 1:2], mc[:, 1:2], 1.0 / 1024.0))
        ch.step("scalar", lambda e, mc=mc: e.activation(mc[:, 2:3], mc[:, 1:2], AF.Ln,
                                                        bias=c.cst[:, 0:1]))
        ch.step("scalar", lambda e, mc=mc: e.activation(mc[:, 3:4], mc[:, 2:3], AF.Exp, scale=-0.5))
        ln_tok = ch.step("vector", lambda e, lnsv=lnsv, mc=mc: e.tensor_scalar_mul(
            lnsv[:, :], c.lnt[:, :], mc[:, 3:4]))
        prev_s = None
        prev_v = None
        for j in range(8):
            tt = s.add("tensor", lambda e, j=j, lnsv=lnsv: e.transpose(
                c.pT[:, :], lnsv[:, 128 * j:128 * (j + 1)], c.ident[:, :]),
                deps=[ln_tok, prev_s])
            prev_s = s.add("scalar", lambda e, i=i, j=j: e.copy(
                c.xnF[:, xn_col(c, j, 128 * i): xn_col(c, j, 128 * (i + 1))],
                c.pT[:, :]), deps=[tt])
            tt2 = s.add("tensor", lambda e, j=j, xv=xv: e.transpose(
                c.pT2[:, :], xv[:, 128 * j:128 * (j + 1)], c.ident[:, :]),
                deps=[ld_tok[par], prev_v])
            prev_v = s.add("vector", lambda e, j=j, par=par: e.tensor_copy(
                (STG if par == 0 else STG2)[:, 128 * j:128 * (j + 1)], c.pT2[:, :]),
                deps=[tt2, stg_dma[par]])
        tp_done[par] = s.add("vector", lambda e: e.engine_nop(), deps=[prev_s, prev_v])
        stg_dma[par] = s.add("sync", lambda e, i=i, par=par: e.dma_start(
            out=x1a[:, :, 128 * i:128 * (i + 1)],
            in_=(STG if par == 0 else STG2).rearrange("p (j t) -> p j t", t=128)),
            dma=True, deps=[prev_v], stream=f"st{par}")
        init_toks.append(stg_dma[par])
    hl = s.add("sync", lambda e: e.dma_start(out=c.row1[:, :], in_=x_in.ap()[0:1, :]),
               dma=True, deps=[tp_done[0], tp_done[1], stg_dma[0], stg_dma[1],
                               tok_smalls_s], stream="sm")
    ch = Chain(s, deps=[hl, tok_smalls_v])
    ch.step("vector", lambda e: e.reduce_sum(c.rowm[:, 0:1], c.row1[:, :], AX))
    ch.step("vector", lambda e: e.tensor_scalar_mul(c.rowm[:, 0:1], c.rowm[:, 0:1], 1.0 / 1024.0))
    ch.step("vector", lambda e: e.tensor_scalar_sub(c.row1[:, :], c.row1[:, :], c.rowm[:, 0:1]))
    ch.step("scalar", lambda e: e.square(c.lnt[0:1, :], c.row1[:, :]))
    ch.step("vector", lambda e: e.reduce_sum(c.rowm[:, 1:2], c.lnt[0:1, :], AX))
    ch.step("vector", lambda e: e.tensor_scalar_mul(c.rowm[:, 1:2], c.rowm[:, 1:2], 1.0 / 1024.0))
    ch.step("scalar", lambda e: e.activation(c.rowm[:, 2:3], c.rowm[:, 1:2], AF.Ln,
                                             bias=c.cst[0:1, 0:1]))
    ch.step("scalar", lambda e: e.activation(c.rowm[:, 3:4], c.rowm[:, 2:3], AF.Exp, scale=-0.5))
    hn = ch.step("vector", lambda e: e.tensor_scalar_mul(c.row1[:, :], c.row1[:, :], c.rowm[:, 3:4]))
    pvh = None
    for j in range(8):
        tt = s.add("tensor", lambda e, j=j: e.transpose(
            c.pT[:, 0:1], c.row1[0:1, 128 * j:128 * (j + 1)], c.ident[0:1, 0:1]),
            deps=[hn, pvh])
        pvh = s.add("scalar", lambda e, j=j: e.copy(
            c.xnF[:, c.XS * j: c.XS * j + 1], c.pT[:, 0:1]), deps=[tt])
    xn_ready = pvh

    emit_wkv_prep(s, c, with_r=True)
    s.add("vector", lambda e: e.tensor_scalar_mul(w2048[:, :], c.w8[:, :], 2048.0))
    s.add("vector", lambda e: e.memset(st[:, 0:8], NEG))
    s.add("vector", lambda e: e.memset(st[:, 8:24], 0.0))
    TM = fold8[:, 0:8]; MM_ = fold8[:, 8:16]; M1 = fold8[:, 16:24]
    M2 = fold8[:, 24:32]; M3 = fold8[:, 32:40]
    for kslot in range(3):
        sp = carry_sb[:, 24 * kslot + 0:24 * kslot + 8]
        sa = carry_sb[:, 24 * kslot + 8:24 * kslot + 16]
        sb_ = carry_sb[:, 24 * kslot + 16:24 * kslot + 24]
        s.add("vector", lambda e: e.tensor_add(TM, st[:, 0:8], w2048[:, :]))
        s.add("vector", lambda e, sp=sp: e.tensor_max(MM_, TM, sp))
        s.add("vector", lambda e: e.tensor_sub(M1, TM, MM_))
        s.add("scalar", lambda e: e.activation(M1, M1, AF.Exp))
        s.add("vector", lambda e, sp=sp: e.tensor_sub(M2, sp, MM_))
        s.add("scalar", lambda e: e.activation(M2, M2, AF.Exp))
        s.add("vector", lambda e: e.tensor_mul(M3, M1, st[:, 8:16]))
        s.add("vector", lambda e, sa=sa: e.tensor_mul(TM, M2, sa))
        s.add("vector", lambda e: e.tensor_add(st[:, 8:16], M3, TM))
        s.add("vector", lambda e: e.tensor_mul(M3, M1, st[:, 16:24]))
        s.add("vector", lambda e, sb_=sb_: e.tensor_mul(TM, M2, sb_))
        s.add("vector", lambda e: e.tensor_add(st[:, 16:24], M3, TM))
        s.add("vector", lambda e: e.tensor_copy(st[:, 0:8], MM_))
    tok_prep_v = s.cur("vector")
    tok_prep_a = s.cur("scalar")

    wkv_w_tok = None
    xq_mm_tok = None
    kq_ev = {}
    vq_ev = {}
    rq_ev = {}
    rwq_toks = [None] * 8
    wo_mm_last = None
    psA = [c.pM, c.pM2]
    ps_free = {0: None, 1: None}
    acc_last = {}
    for h in range(8):
        for tc in range(4):
            acc_last[(h, tc)] = [init_toks[4 * tc + 2], init_toks[4 * tc + 3]]
    acc_last[("stg", 0)] = stg_dma[0]
    acc_last[("stg", 1)] = stg_dma[1]
    if debug:
        dbgx_tok = s.add("sync", lambda e: e.dma_start(
            out=dbg_x.ap(), in_=x1_acc.ap()), dma=True,
            deps=init_toks, stream="dbgx")
        for h in range(8):
            for tc in range(4):
                acc_last[(h, tc)] = acc_last[(h, tc)] + [dbgx_tok]
    mmi = 0
    dbg_toks = []
    for q in range(4):
        wk_dma = emit_wstream(s, c, wkT_in.ap(),
                              deps=[wkv_w_tok, tok_smalls_g] + dbg_toks, stream="wst")
        tmx = None
        for j in range(8):
            xn_v = c.xnF[:, xn_col(c, j, CH * q): xn_col(c, j, CH * q + CH)]
            xx_v = c.xnF[:, xn_col(c, j, CH * q - 1): xn_col(c, j, CH * q - 1 + CH)]
            t1 = s.add("vector", lambda e, xx_v=xx_v, j=j: e.tensor_scalar_mul(
                c.lnt[:, 0:CH], xx_v, c.tmk1[:, j:j + 1]),
                deps=[xn_ready, tok_prep_v, xq_mm_tok, tmx])
            tmx = s.add("vector", lambda e, xn_v=xn_v, j=j: e.scalar_tensor_tensor(
                XQ[:, CH * j:CH * (j + 1)], xn_v, c.tmk[:, j:j + 1], c.lnt[:, 0:CH],
                ALU.mult, ALU.add), deps=[])
        xq_tok = tmx
        for j in range(8):
            ps = psA[mmi % 2]
            mm = emit_mm8(s, c, ps, j, lambda kc: XQ[:, CH * kc:CH * (kc + 1)], CH,
                          deps=[wk_dma, xq_tok, ps_free[mmi % 2]])
            kq_ev[j] = s.add("vector", lambda e, j=j, ps=ps: e.tensor_copy(
                KQ[:, CH * j:CH * (j + 1)], ps[:, 0:CH]), deps=[mm])
            ps_free[mmi % 2] = kq_ev[j]
            mmi += 1
        wkv_w_tok = mm
        wv_dma = emit_wstream(s, c, wvT_in.ap(), deps=[wkv_w_tok], stream="wst")
        tmx = None
        for j in range(8):
            xn_v = c.xnF[:, xn_col(c, j, CH * q): xn_col(c, j, CH * q + CH)]
            xx_v = c.xnF[:, xn_col(c, j, CH * q - 1): xn_col(c, j, CH * q - 1 + CH)]
            t1 = s.add("vector", lambda e, xx_v=xx_v, j=j: e.tensor_scalar_mul(
                c.lnt[:, 0:CH], xx_v, c.tmv1[:, j:j + 1]), deps=[mm, tmx])
            tmx = s.add("vector", lambda e, xn_v=xn_v, j=j: e.scalar_tensor_tensor(
                XQ[:, CH * j:CH * (j + 1)], xn_v, c.tmv[:, j:j + 1], c.lnt[:, 0:CH],
                ALU.mult, ALU.add), deps=[])
        xq_tok = tmx
        for j in range(8):
            ps = psA[mmi % 2]
            mm = emit_mm8(s, c, ps, j, lambda kc: XQ[:, CH * kc:CH * (kc + 1)], CH,
                          deps=[wv_dma, xq_tok, ps_free[mmi % 2]])
            vq_ev[j] = s.add("vector", lambda e, j=j, ps=ps: e.tensor_copy(
                VQ[:, CH * j:CH * (j + 1)], ps[:, 0:CH]), deps=[mm])
            ps_free[mmi % 2] = vq_ev[j]
            mmi += 1
        wkv_w_tok = mm
        wr_dma = emit_wstream(s, c, wrT_in.ap(), deps=[wkv_w_tok], stream="wst")
        tmx = None
        for j in range(8):
            xn_v = c.xnF[:, xn_col(c, j, CH * q): xn_col(c, j, CH * q + CH)]
            xx_v = c.xnF[:, xn_col(c, j, CH * q - 1): xn_col(c, j, CH * q - 1 + CH)]
            t1 = s.add("vector", lambda e, xx_v=xx_v, j=j: e.tensor_scalar_mul(
                c.lnt[:, 0:CH], xx_v, c.tmr1[:, j:j + 1]), deps=[mm, tmx])
            tmx = s.add("vector", lambda e, xn_v=xn_v, j=j: e.scalar_tensor_tensor(
                XQ[:, CH * j:CH * (j + 1)], xn_v, c.tmr[:, j:j + 1], c.lnt[:, 0:CH],
                ALU.mult, ALU.add), deps=[])
        xq_tok = tmx
        for j in range(8):
            ps = psA[mmi % 2]
            mm = emit_mm8(s, c, ps, j, lambda kc: XQ[:, CH * kc:CH * (kc + 1)], CH,
                          deps=[wr_dma, xq_tok, ps_free[mmi % 2], rwq_toks[j]])
            rq_ev[j] = s.add("scalar", lambda e, j=j, ps=ps: e.activation(
                RQ[:, CH * j:CH * (j + 1)], ps[:, 0:CH], AF.Sigmoid), deps=[mm])
            ps_free[mmi % 2] = rq_ev[j]
            mmi += 1
        wkv_w_tok = mm
        st_fin = {}
        for j in range(8):
            kt = KQ[:, CH * j:CH * (j + 1)]
            vt = VQ[:, CH * j:CH * (j + 1)]
            ch = Chain(s, deps=[kq_ev[j], vq_ev[j], rq_ev[j], tok_prep_a,
                                acc_last[("stg", 0)], acc_last[("stg", 1)]])
            ch.step("vector", lambda e, j=j: e.tensor_copy(PE_[:, 0:1], st[:, j:j + 1]))
            ch.step("vector", lambda e: e.memset(WREP, 1.0))
            ch.step("vector", lambda e, j=j: e.tensor_scalar_mul(WREP, WREP, c.w8[:, j:j + 1]))
            ch.step("vector", lambda e, j=j, kt=kt: e.tensor_tensor_scan(
                PE_[:, 1:CH + 1], WREP, kt, st[:, j:j + 1], ALU.add, ALU.max))
            ch.step("vector", lambda e: e.tensor_sub(T1, PE_[:, 0:CH], PE_[:, 1:CH + 1]))
            ch.step("scalar", lambda e, j=j: e.activation(AL, T1, AF.Exp, bias=c.w8[:, j:j + 1]))
            ch.step("vector", lambda e, kt=kt: e.tensor_sub(T2, kt, PE_[:, 1:CH + 1]))
            ch.step("scalar", lambda e: e.activation(CD, T2, AF.Exp))
            ch.step("vector", lambda e, vt=vt: e.tensor_mul(CN, CD, vt))
            ch.step("vector", lambda e, j=j: e.tensor_copy(SN_[:, 0:1], st[:, 8 + j:9 + j]))
            ch.step("vector", lambda e, j=j: e.tensor_tensor_scan(
                SN_[:, 1:CH + 1], AL, CN, st[:, 8 + j:9 + j], ALU.mult, ALU.add))
            ch.step("vector", lambda e, j=j: e.tensor_copy(SD_[:, 0:1], st[:, 16 + j:17 + j]))
            ch.step("vector", lambda e, j=j: e.tensor_tensor_scan(
                SD_[:, 1:CH + 1], AL, CD, st[:, 16 + j:17 + j], ALU.mult, ALU.add))
            ch.step("vector", lambda e, j=j, kt=kt: e.tensor_scalar_add(T1, kt, c.tf[:, j:j + 1]))
            ch.step("vector", lambda e: e.tensor_max(T2, PE_[:, 0:CH], T1))
            ch.step("vector", lambda e: e.tensor_sub(T3, PE_[:, 0:CH], T2))
            ch.step("scalar", lambda e: e.activation(E1B, T3, AF.Exp))
            ch.step("vector", lambda e: e.tensor_sub(T3, T1, T2))
            ch.step("scalar", lambda e: e.activation(E2B, T3, AF.Exp))
            ch.step("vector", lambda e: e.tensor_mul(T4, E1B, SN_[:, 0:CH]))
            ch.step("vector", lambda e, vt=vt: e.tensor_mul(T3, E2B, vt))
            ch.step("vector", lambda e: e.tensor_add(T4, T4, T3))
            ch.step("vector", lambda e: e.tensor_mul(T3, E1B, SD_[:, 0:CH]))
            ch.step("vector", lambda e: e.tensor_add(T3, T3, E2B))
            ch.step("scalar", lambda e: e.activation(T1, T3, AF.Ln))
            ch.step("scalar", lambda e: e.activation(T2, T1, AF.Exp, scale=-1.0))
            ch.step("vector", lambda e: e.tensor_mul(T4, T4, T2))
            rwq_toks[j] = ch.step("vector", lambda e, j=j: e.tensor_mul(
                RWQ[:, CH * j:CH * (j + 1)], T4, RQ[:, CH * j:CH * (j + 1)]),
                extra=[wo_mm_last])
            ch.step("vector", lambda e, j=j: e.tensor_copy(st[:, j:j + 1], PE_[:, CH:CH + 1]))
            ch.step("vector", lambda e, j=j: e.tensor_copy(st[:, 8 + j:9 + j], SN_[:, CH:CH + 1]))
            st_fin[j] = ch.step("vector", lambda e, j=j: e.tensor_copy(st[:, 16 + j:17 + j], SD_[:, CH:CH + 1]))
        if debug:
            last_ch = st_fin[7]
            dbg_toks = [
                s.add("sync", lambda e, q=q: e.dma_start(
                    out=dbg_kvr.ap()[q, 0], in_=KQ[:, :]), dma=True,
                    deps=[kq_ev[7], last_ch], stream="dbg0"),
                s.add("sync", lambda e, q=q: e.dma_start(
                    out=dbg_kvr.ap()[q, 1], in_=VQ[:, :]), dma=True,
                    deps=[vq_ev[7], last_ch], stream="dbg1"),
                s.add("gpsimd", lambda e, q=q: e.dma_start(
                    out=dbg_kvr.ap()[q, 2], in_=RQ[:, :]), dma=True,
                    deps=[rq_ev[7], last_ch], stream="dbg2"),
                s.add("gpsimd", lambda e, q=q: e.dma_start(
                    out=dbg_kvr.ap()[q, 3], in_=RWQ[:, :]), dma=True,
                    deps=[last_ch], stream="dbg3"),
                s.add("sync", lambda e, q=q: e.dma_start(
                    out=dbg_st.ap()[q], in_=st[:, :]), dma=True,
                    deps=[last_ch], stream="dbg4"),
            ]
        wo_dma = emit_wstream(s, c, woT_in.ap(), deps=[wkv_w_tok], stream="wst")
        for cout in range(8):
            ps = psA[mmi % 2]
            mm = emit_mm8(s, c, ps, cout, lambda kc: RWQ[:, CH * kc:CH * (kc + 1)], CH,
                          deps=[wo_dma, rwq_toks[7], ps_free[mmi % 2]])
            ev = s.add("vector", lambda e, ps=ps, cout=cout: e.tensor_copy(
                STG[:, 0:CH] if cout % 2 == 0 else STG2[:, 0:CH], ps[:, 0:CH]),
                deps=[mm, acc_last.get(("stg", cout % 2))])
            dmat = s.add("gpsimd", lambda e, cout=cout, q=q: e.dma_start(
                out=x1_acc.ap()[:, 2048 * cout + CH * q: 2048 * cout + CH * (q + 1)],
                in_=(STG if cout % 2 == 0 else STG2)[:, 0:CH], accum_op=ALU.add),
                dma=True, deps=[ev] + acc_last[(cout, q)], stream=f"st{cout % 2}")
            acc_last[(cout, q)] = [dmat]
            acc_last[("stg", cout % 2)] = dmat
            ps_free[mmi % 2] = ev
            mmi += 1
            wo_mm_last = mm
            wkv_w_tok = mm

    wg_dma = s.add("gpsimd", lambda e: e.dma_start(
        out=wgb[:, 0:64].rearrange("p (kc n) -> p kc n", n=8),
        in_=wg_in.ap().rearrange("(kc p) n -> p kc n", p=128)), dma=True,
        deps=[tok_smalls_g], stream="sm")
    lg_toks = []
    for tc in range(4):
        def fn(e, tc=tc):
            last = None
            for kc in range(8):
                last = e.matmul(c.pCW[0:8, 0:512], wgb[:, 8 * kc:8 * (kc + 1)],
                                c.xnF[:, xn_col(c, kc, 512 * tc): xn_col(c, kc, 512 * (tc + 1))],
                                start=(kc == 0), stop=(kc == 7))
            return last
        prev_cp = lg_toks[-1] if lg_toks else None
        mm = s.add("tensor", fn, deps=[wg_dma, xn_ready, prev_cp, wo_mm_last] + dbg_toks)
        lg_toks.append(s.add("vector", lambda e, tc=tc: e.tensor_copy(
            lgt[:, 512 * tc:512 * (tc + 1)], c.pCW[0:8, 0:512]), deps=[mm]))
    cw_toks = []
    prev_t = None
    for i in range(16):
        ch = Chain(s, deps=[lg_toks[i // 4], prev_t])
        tt = ch.step("tensor", lambda e, i=i: e.transpose(
            c.pT[:, 0:8], lgt[:, 128 * i:128 * (i + 1)], c.ident[0:8, 0:8]))
        ch.step("vector", lambda e: e.tensor_copy(ptile[:, :], c.pT[:, 0:8]))
        ch.step("vector", lambda e: e.reduce_max(cm[:, 0:1], ptile[:, :], AX))
        ch.step("vector", lambda e: e.tensor_scalar_mul(cm[:, 1:2], cm[:, 0:1], -1.0))
        ch.step("scalar", lambda e: e.activation(pt2_[:, :], ptile[:, :], AF.Exp, bias=cm[:, 1:2]))
        ch.step("vector", lambda e: e.reduce_sum(cm[:, 2:3], pt2_[:, :], AX))
        ch.step("scalar", lambda e: e.activation(cm[:, 3:4], cm[:, 2:3], AF.Ln))
        ch.step("scalar", lambda e: e.activation(cm[:, 3:4], cm[:, 3:4], AF.Exp, scale=-1.0))
        ch.step("vector", lambda e: e.tensor_scalar_mul(pt2_[:, :], pt2_[:, :], cm[:, 3:4]))
        ch.step("vector", lambda e: e.reduce_max(cm[:, 0:1], pt2_[:, :], AX))
        ch.step("vector", lambda e: e.tensor_single_scalar(eqt[:, :], pt2_[:, :], cm[:, 0:1], ALU.is_equal))
        ch.step("vector", lambda e: e.scalar_tensor_tensor(p2t[:, :], eqt[:, :], -1.0e30, pt2_[:, :], ALU.mult, ALU.add))
        ch.step("vector", lambda e: e.reduce_max(cm[:, 2:3], p2t[:, :], AX))
        ch.step("vector", lambda e: e.tensor_single_scalar(eq2t[:, :], pt2_[:, :], cm[:, 2:3], ALU.is_equal))
        ch.step("vector", lambda e: e.tensor_add(eqt[:, :], eqt[:, :], eq2t[:, :]))
        ch.step("vector", lambda e: e.tensor_add(cm[:, 5:6], cm[:, 0:1], cm[:, 2:3]))
        ch.step("scalar", lambda e: e.activation(cm[:, 6:7], cm[:, 5:6], AF.Ln, bias=c.cst[:, 1:2]))
        ch.step("scalar", lambda e: e.activation(cm[:, 6:7], cm[:, 6:7], AF.Exp, scale=-1.0))
        ch.step("vector", lambda e: e.tensor_mul(ptile[:, :], pt2_[:, :], eqt[:, :]))
        ch.step("vector", lambda e: e.tensor_scalar_mul(ptile[:, :], ptile[:, :], cm[:, 6:7]))
        prev_t = ch.step("vector", lambda e, i=i: e.tensor_copy(cw_tm[:, 8 * i:8 * (i + 1)], ptile[:, :]))
        cw_toks.append(prev_t)
    cwt_toks = []
    pv = None
    for i in range(16):
        tt = s.add("tensor", lambda e, i=i: e.transpose(
            c.pT2[0:8, 0:128], cw_tm[:, 8 * i:8 * (i + 1)], c.ident[:, :]),
            deps=[cw_toks[i], pv])
        pv = s.add("scalar", lambda e, i=i: e.copy(
            cwT8[:, 128 * i:128 * (i + 1)], c.pT2[0:8, 0:128]), deps=[tt])
        cwt_toks.append(pv)
    cw_ready = cwt_toks[-1]

    PNL = [Bv(16512, 4096), Bv(20608, 4096)]
    ACT1 = Bv(24704, 2048)
    pnl_dma = [None, None]
    pnl_last_mm = [wo_mm_last, wo_mm_last]
    cwb_toks = {}
    act_dma_slot = [None] * 4
    ev_tok = None
    cwm_tok = None
    gi = 0
    for ee in range(9):
        for tc in range(4):
            if ee < 8:
                prev_cwb = cwb_toks.get((ee, tc - 1), cwb_toks.get((ee - 1, 3)))
                mmb = s.add("tensor", lambda e, ee=ee, tc=tc: e.matmul(
                    c.pCW[:, 0:512], selb[:, 128 * ee:128 * (ee + 1)],
                    cwT8[:, 512 * tc:512 * (tc + 1)], start=True, stop=True),
                    deps=[cw_ready, tok_smalls_s, cwb_toks.get((ee - 1, tc)),
                          prev_cwb, cwm_tok])
                cwb_toks[(ee, tc)] = s.add("scalar", lambda e, tc=tc: e.copy(
                    cwB4[:, 512 * tc:512 * (tc + 1)], c.pCW[:, 0:512]),
                    deps=[mmb, cwm_tok])
            else:
                cwb_toks[(ee, tc)] = s.add("vector", lambda e, tc=tc: e.memset(
                    cwB4[:, 512 * tc:512 * (tc + 1)], 0.1),
                    deps=[cwb_toks.get((ee - 1, tc)), cwm_tok])
        for g in range(8):
            buf = gi % 2
            src_ap = (w1_in.ap()[ee].rearrange("(kc p) f -> p kc f", p=128)[:, :, 512 * g:512 * (g + 1)]
                      if ee < 8 else
                      ws1_in.ap().rearrange("(kc p) f -> p kc f", p=128)[:, :, 512 * g:512 * (g + 1)])
            pnl_dma[buf] = s.add("gpsimd", lambda e, buf=buf, src_ap=src_ap: e.dma_start(
                out=PNL[buf].rearrange("p (kc n) -> p kc n", n=512), in_=src_ap),
                dma=True, deps=[pnl_last_mm[buf], tok_smalls_g], stream=f"p{buf}")
            for fb in range(4):
                fc = 4 * g + fb
                for tc in range(4):
                    ps = psA[mmi % 2]
                    mm = emit_mm8(s, c, ps, fb,
                                  lambda kc, tc=tc: c.xnF[:, xn_col(c, kc, 512 * tc): xn_col(c, kc, 512 * (tc + 1))],
                                  512, wbuf=PNL[buf], wcols=512,
                                  deps=[pnl_dma[buf], xn_ready, ps_free[mmi % 2]])
                    ev_tok = s.add("scalar", lambda e, ee=ee, fc=fc, fb=fb, ps=ps: e.activation(
                        ACT1[:, 512 * fb:512 * (fb + 1)], ps[:, 0:512],
                        AF.Silu, bias=b1sb[:, 32 * ee + fc:32 * ee + fc + 1]),
                        deps=[mm, act_dma_slot[fb]])
                    cwm_tok = s.add("vector", lambda e, fb=fb, tc=tc: e.tensor_mul(
                        ACT1[:, 512 * fb:512 * (fb + 1)],
                        ACT1[:, 512 * fb:512 * (fb + 1)],
                        cwB4[:, 512 * tc:512 * (tc + 1)]),
                        deps=[ev_tok, cwb_toks[(ee, tc)]])
                    act_dma_slot[fb] = s.add("gpsimd", lambda e, ee=ee, g=g, tc=tc, fb=fb: e.dma_start(
                        out=act_hbm.ap()[ee, g, tc, :, 512 * fb:512 * (fb + 1)],
                        in_=ACT1[:, 512 * fb:512 * (fb + 1)]), dma=True, deps=[cwm_tok], stream=f"a{fb}")
                    ps_free[mmi % 2] = ev_tok
                    mmi += 1
                    pnl_last_mm[buf] = mm
            gi += 1

    W2H = [Bv(0, 8192), Bv(8192, 8192)]
    ACTF = [Bv(24704, 8192), Bv(32896, 8192)]
    p1_done = s.cur("tensor")
    p2_dbg = []
    w2h_dma = [None, None]
    w2h_last_mm = [p1_done, p1_done]
    actf_dma = [None, None]
    actf_last_mm = [None, None]
    last_ps2_ev = {("sdma", 0): acc_last.get(("stg", 0)),
                   ("sdma", 1): acc_last.get(("stg", 1))}
    for ee in range(9):
        for hh in range(2):
            for half in range(2):
                src_ap = (w2_in.ap()[ee].rearrange("(kc p) n -> p kc n", p=128)[:, 16 * half:16 * (half + 1), 512 * hh:512 * (hh + 1)]
                          if ee < 8 else
                          ws2_in.ap().rearrange("(kc p) n -> p kc n", p=128)[:, 16 * half:16 * (half + 1), 512 * hh:512 * (hh + 1)])
                w2h_dma[half] = s.add("gpsimd", lambda e, half=half, src_ap=src_ap: e.dma_start(
                    out=W2H[half].rearrange("p (kc n) -> p kc n", n=512), in_=src_ap),
                    dma=True, deps=[w2h_last_mm[half]] + p2_dbg, stream=f"w2{half}")
            for tc in range(4):
                for half in range(2):
                    actf_dma[half] = s.add("gpsimd", lambda e, ee=ee, tc=tc, half=half: e.dma_start(
                        out=ACTF[half].rearrange("p (g n) -> p g n", n=2048),
                        in_=act_hbm.ap()[ee, 4 * half:4 * (half + 1), tc, :, :]
                        .rearrange("g p n -> p g n")),
                        dma=True, stream=f"af{half}",
                        deps=[actf_last_mm[half]] + p2_dbg
                             + [act_dma_slot[fb] for fb in range(4)])
                if debug and ee == 2 and hh == 1 and tc == 2:
                    p2_toks = [
                        s.add("gpsimd", lambda e: e.dma_start(
                            out=dbg_p2.ap()[0], in_=W2H[0][:, :]), dma=True,
                            deps=[w2h_dma[0]], stream="dp0"),
                        s.add("gpsimd", lambda e: e.dma_start(
                            out=dbg_p2.ap()[1], in_=W2H[1][:, :]), dma=True,
                            deps=[w2h_dma[1]], stream="dp1"),
                        s.add("gpsimd", lambda e: e.dma_start(
                            out=dbg_p2.ap()[2], in_=ACTF[0][:, :]), dma=True,
                            deps=[actf_dma[0]], stream="dp2"),
                        s.add("gpsimd", lambda e: e.dma_start(
                            out=dbg_p2.ap()[3], in_=ACTF[1][:, :]), dma=True,
                            deps=[actf_dma[1]], stream="dp3"),
                    ]
                    p2_dbg.extend(p2_toks)
                for h4 in range(4):
                    h = 4 * hh + h4
                    ps = psA[mmi % 2]
                    def fn2(e, ee=ee, tc=tc, h4=h4, h=h, ps=ps):
                        last = None
                        extra = (ee == 0)
                        for fc in range(32):
                            half = fc // 16
                            fcl = fc % 16
                            is_last = (fc == 31) and not extra
                            last = e.matmul(
                                ps[:, 0:512],
                                W2H[half][:, 512 * fcl + 128 * h4: 512 * fcl + 128 * (h4 + 1)],
                                ACTF[half][:, 512 * fcl:512 * (fcl + 1)],
                                start=(fc == 0), stop=is_last)
                        if extra:
                            last = e.matmul(ps[:, 0:512], b2b[:, 128 * h:128 * (h + 1)],
                                            cwT8[:, 512 * tc:512 * (tc + 1)],
                                            start=False, stop=False)
                            last = e.matmul(ps[:, 0:512], bs2b[:, 128 * h:128 * (h + 1)],
                                            row01[:, :], start=False, stop=True)
                        return last
                    mm = s.add("tensor", fn2,
                               deps=[w2h_dma[0], w2h_dma[1], actf_dma[0], actf_dma[1],
                                     ps_free[mmi % 2], cw_ready, tok_smalls_g])
                    w2h_last_mm[0] = w2h_last_mm[1] = mm
                    actf_last_mm[0] = actf_last_mm[1] = mm
                    ev = s.add("vector", lambda e, ps=ps, h4=h4: e.tensor_copy(
                        (STG if h4 % 2 == 0 else STG2)[:, 0:512], ps[:, 0:512]),
                        deps=[mm, last_ps2_ev.get(("sdma", h4 % 2))])
                    dmat = s.add("gpsimd", lambda e, h=h, tc=tc, h4=h4: e.dma_start(
                        out=x1_acc.ap()[:, 2048 * h + 512 * tc: 2048 * h + 512 * (tc + 1)],
                        in_=(STG if h4 % 2 == 0 else STG2)[:, 0:512], accum_op=ALU.add),
                        dma=True, deps=[ev] + acc_last[(h, tc)], stream=f"st{h4 % 2}")
                    acc_last[(h, tc)] = [dmat]
                    last_ps2_ev[("sdma", h4 % 2)] = dmat
                    ps_free[mmi % 2] = ev
                    mmi += 1
    moe_done = [t for h in range(8) for tc in range(4) for t in acc_last[(h, tc)]]

    x1_rd = [None, None]
    x1_wr = [None, None]
    x1ev = [None, None]
    for i in range(16):
        par = i % 2
        bufv = STG if par == 0 else STG2
        x1_rd[par] = s.add("sync", lambda e, i=i, bufv=bufv: e.dma_start(
            out=bufv.rearrange("p (j t) -> p j t", t=128),
            in_=x1a[:, :, 128 * i:128 * (i + 1)]),
            dma=True, deps=moe_done if i < 2 else [x1ev[par]], stream=f"rd{par}")
        lastv = None
        for j in range(8):
            tt = s.add("tensor", lambda e, j=j, bufv=bufv: e.transpose(
                c.pT[:, :], bufv[:, 128 * j:128 * (j + 1)], c.ident[:, :]),
                deps=[x1_rd[par], lastv])
            lastv = s.add("vector", lambda e, j=j, par=par: e.tensor_copy(
                (c.xtm[:, 0:1024] if par == 0 else c.xtm[:, 1024:2048])[:, 128 * j:128 * (j + 1)],
                c.pT[:, :]), deps=[tt, x1_wr[par]])
        x1ev[par] = lastv
        x1_wr[par] = s.add("sync", lambda e, i=i, par=par: e.dma_start(
            out=x1_out.ap()[128 * i:128 * (i + 1), :],
            in_=c.xtm[:, 0:1024] if par == 0 else c.xtm[:, 1024:2048]),
            dma=True, deps=[lastv], stream=f"wr{par}")

    X1T = Fv(5120, 4096)
    s.add("sync", lambda e: e.dma_start(out=c.trrev, in_=trrev_in.ap()), dma=True)
    s.add("sync", lambda e: e.dma_start(out=nu8[:, :], in_=nul_in.ap()), dma=True)
    s.add("sync", lambda e: e.dma_start(out=brkv[:, :], in_=brkv_in.ap()), dma=True)
    s.add("scalar", lambda e: e.activation(nu8[:, :], nu8[:, :], AF.Exp))
    s.add("vector", lambda e: e.tensor_scalar_mul(nun8[:, :], nu8[:, :], -1.0))
    for tc in range(4):
        for f in range(8):
            s.add("sync", lambda e, f=f, tc=tc: e.dma_start(
                out=X1T[:, 512 * f:512 * (f + 1)],
                in_=x1_acc.ap()[:, 2048 * f + 512 * tc: 2048 * f + 512 * (tc + 1)]), dma=True)
        def fsum(e, tc=tc):
            last = None
            for f in range(8):
                last = e.matmul(c.pM[0:1, 0:512], ones_col[:, :],
                                X1T[:, 512 * f:512 * (f + 1)],
                                start=(f == 0), stop=(f == 7))
            return last
        s.add("tensor", fsum)
        for f in range(8):
            s.add("scalar", lambda e, f=f: e.square(c.lnt[:, 0:512], X1T[:, 512 * f:512 * (f + 1)]))
            def fsq(e, f=f):
                return e.matmul(c.pM2[0:1, 0:512], ones_col[:, :], c.lnt[:, 0:512],
                                start=(f == 0), stop=(f == 7))
            s.add("tensor", fsq)
        s.add("vector", lambda e: e.tensor_scalar_mul(srow[0:1, 0:512], c.pM[0:1, 0:512], 1.0 / 1024.0))
        s.add("vector", lambda e: e.tensor_scalar_mul(srow[0:1, 512:1024], c.pM2[0:1, 0:512], 1.0 / 1024.0))
        s.add("scalar", lambda e: e.square(srow[0:1, 1024:1536], srow[0:1, 0:512]))
        s.add("vector", lambda e: e.tensor_sub(srow[0:1, 512:1024], srow[0:1, 512:1024], srow[0:1, 1024:1536]))
        s.add("scalar", lambda e: e.activation(srow[0:1, 512:1024], srow[0:1, 512:1024], AF.Ln, bias=c.cst[0:1, 0:1]))
        s.add("scalar", lambda e: e.activation(srow[0:1, 512:1024], srow[0:1, 512:1024], AF.Exp, scale=-0.5))
        s.add("tensor", lambda e: e.matmul(c.pM[:, 0:512], ones_rowf[:, :], srow[0:1, 0:512], start=True, stop=True))
        s.add("vector", lambda e: e.tensor_copy(STG[:, 0:512], c.pM[:, 0:512]))
        s.add("tensor", lambda e: e.matmul(c.pM2[:, 0:512], ones_rowf[:, :], srow[0:1, 512:1024], start=True, stop=True))
        s.add("vector", lambda e: e.tensor_copy(STG[:, 512:1024], c.pM2[:, 0:512]))
        for f in range(8):
            s.add("vector", lambda e, f=f: e.tensor_sub(c.lnt[:, 0:512], X1T[:, 512 * f:512 * (f + 1)], STG[:, 0:512]))
            s.add("vector", lambda e, f=f, tc=tc: e.tensor_mul(
                c.xnF[:, xn_col(c, f, 512 * tc): xn_col(c, f, 512 * (tc + 1))],
                c.lnt[:, 0:512], STG[:, 512:1024]))
    KT = Fv(9216, 2048)
    VT = Fv(11264, 2048)
    SA = Fv(13312, 2048)
    SB2 = Fv(15360, 2048)
    KWB = Bv(16512, 1024)
    VWB = Bv(17536, 1024)
    for j in range(8):
        s.add("gpsimd", lambda e, j=j: e.dma_start(
            out=KWB.rearrange("p (kc n) -> p kc n", n=128),
            in_=wrkv_in.ap().rearrange("(kc p) n -> p kc n", p=128)[:, :, 1024 + 128 * j:1024 + 128 * (j + 1)]), dma=True)
        s.add("gpsimd", lambda e, j=j: e.dma_start(
            out=VWB.rearrange("p (kc n) -> p kc n", n=128),
            in_=wrkv_in.ap().rearrange("(kc p) n -> p kc n", p=128)[:, :, 2048 + 128 * j:2048 + 128 * (j + 1)]), dma=True)
        for tc in range(4):
            emit_mm8(s, c, c.pM, 0,
                     lambda kc, tc=tc: c.xnF[:, xn_col(c, kc, 512 * tc): xn_col(c, kc, 512 * (tc + 1))],
                     512, wbuf=KWB, wcols=128)
            s.add("scalar", lambda e, j=j, tc=tc: e.activation(
                KT[:, 512 * tc:512 * (tc + 1)], c.pM[:, 0:512], AF.Exp,
                bias=brkv[:, 8 + j:9 + j]))
            emit_mm8(s, c, c.pM2, 0,
                     lambda kc, tc=tc: c.xnF[:, xn_col(c, kc, 512 * tc): xn_col(c, kc, 512 * (tc + 1))],
                     512, wbuf=VWB, wcols=128)
            s.add("vector", lambda e, j=j, tc=tc: e.tensor_scalar_add(
                VT[:, 512 * tc:512 * (tc + 1)], c.pM2[:, 0:512], brkv[:, 16 + j:17 + j]))
        s.add("vector", lambda e, j=j: e.tensor_scalar_mul(SA[:, :], c.trrev[:, :], nun8[:, j:j + 1]))
        s.add("scalar", lambda e: e.activation(SA[:, :], SA[:, :], AF.Exp))
        s.add("vector", lambda e: e.tensor_mul(SB2[:, :], KT[:, :], VT[:, :]))
        s.add("vector", lambda e: e.tensor_mul(SB2[:, :], SB2[:, :], SA[:, :]))
        s.add("vector", lambda e, j=j: e.reduce_sum(usum_sb[:, j:j + 1], SB2[:, :], AX))
        s.add("vector", lambda e: e.tensor_mul(SB2[:, :], KT[:, :], SA[:, :]))
        s.add("vector", lambda e, j=j: e.reduce_sum(usum_sb[:, 8 + j:9 + j], SB2[:, :], AX))
    s.add("sync", lambda e: e.dma_start(out=usum_out.ap(), in_=usum_sb[:, :]), dma=True)
    s.emit("L1")
    return nc


def host_inputs_launch1(inputs, consts, wsums):
    x = np.asarray(inputs["L0_attn_x"], np.float32)
    wkT = _f32(np.asarray(inputs["L0_attn_Wk"]).T)
    wvT = _f32(np.asarray(inputs["L0_attn_Wv"]).T)
    wrT = _f32(np.asarray(inputs["L0_attn_Wr"]).T)
    woT = _f32(np.asarray(inputs["L0_attn_Wo"]).T)
    td = ctile(inputs["L0_attn_time_decay"])
    tf = ctile(inputs["L0_attn_time_first"])
    tmk = ctile(np.asarray(inputs["L0_attn_time_mix_k"]).reshape(-1))
    tmv = ctile(np.asarray(inputs["L0_attn_time_mix_v"]).reshape(-1))
    tmr = ctile(np.asarray(inputs["L0_attn_time_mix_r"]).reshape(-1))
    wg = _f32(inputs["L0_ffn_Wg"])
    w1 = _f32(inputs["L0_ffn_W1"])
    w2 = _f32(inputs["L0_ffn_W2"])
    ws1 = _f32(inputs["L0_ffn_Ws1"])
    ws2 = _f32(inputs["L0_ffn_Ws2"])
    b1 = np.asarray(inputs["L0_ffn_b1"], np.float32)
    bs1 = np.asarray(inputs["L0_ffn_bs1"], np.float32)
    b1ct = np.zeros((128, 288), np.float32)
    for e in range(8):
        b1ct[:, 32 * e:32 * (e + 1)] = b1[e].reshape(32, 128).T
    b1ct[:, 256:288] = bs1.reshape(32, 128).T
    b2 = _f32(inputs["L0_ffn_b2"])
    bs2 = _f32(np.asarray(inputs["L0_ffn_bs2"]).reshape(1, 1024))
    wrkv = _f32(inputs["L1_attn_W_rkv"])
    brkv = np.asarray(inputs["L1_attn_b_rkv"], np.float32)
    brkv_ct = np.zeros((128, 24), np.float32)
    for part in range(3):
        brkv_ct[:, 8 * part:8 * (part + 1)] = brkv[1024 * part:1024 * (part + 1)].reshape(8, 128).T
    nul = ctile(inputs["L1_attn_nu_log"])
    sel = np.zeros((8, 1024), np.float32)
    for e in range(8):
        sel[e, 128 * e:128 * (e + 1)] = 1.0
    sel = sel.astype(ml_dtypes.bfloat16)
    ident_p = np.zeros((128, 24), np.float32)
    ident_p[:, 0:8] = NEG
    maps = []
    for core in range(8):
        b, sq = core // 4, core % 4
        xe = np.zeros((2049, 1024), np.float32)
        xe[1:] = x[b, sq * SEG:(sq + 1) * SEG]
        if sq > 0:
            xe[0] = x[b, sq * SEG - 1]
        carry = np.zeros((128, 72), np.float32)
        for k in range(3):
            seg = sq - 3 + k
            if seg >= 0:
                carry[:, 24 * k:24 * (k + 1)] = wsums[b * 4 + seg]
            else:
                carry[:, 24 * k:24 * (k + 1)] = ident_p
        maps.append(dict(
            x=xe, wkT=wkT, wvT=wvT, wrT=wrT, woT=woT, td_ct=td, tf_ct=tf,
            tmk_ct=tmk, tmv_ct=tmv, tmr_ct=tmr, carry=carry, wg=wg,
            w1=w1, b1ct=b1ct, w2=w2, b2=b2, bs2=bs2, ws1=ws1, ws2=ws2,
            wrkv=wrkv, brkv_ct=brkv_ct, nul_ct=nul,
            c_ident=consts["c_ident"], c_trrev=consts["c_trrev"],
            c_iota8=consts["c_iota8"], c_sel=sel))
    return maps



XS2 = 4224
XSA = 3584
XSB = 2560
HID = 2730


def _mlp_tiles():
    out = []
    for i in range(22):
        lo = 128 * i
        hi = min(HID, 128 * (i + 1))
        out.append((lo, hi))
    return out


def w1_perm():
    perm = []
    for (lo, hi) in _mlp_tiles():
        perm.extend(range(lo, hi))
        perm.extend(range(HID + lo, HID + hi))
    return np.array(perm)


def build_launch2():
    nc = _mknc()
    def I(name, shape, dt=F32):
        return nc.dram_tensor(name, shape, dt, kind="ExternalInput")
    x_in = I("x1e", [4224, 1024])
    usum_in = I("usum_pred", [128, 48])
    wqkv_in = I("wqkv", [2, 1024, 1536])
    wo_in = I("wo", [2, 512, 1024])
    w1p_in = I("w1p", [2, 1024, 5460])
    b1p_in = I("b1p_ct", [128, 88])
    w2_in = I("w2", [2, 2730, 1024])
    b2_in = I("b2_ct", [128, 16])
    lng_in = I("lng_ct", [128, 16])
    lnb_in = I("lnb_ct", [128, 16])
    wrkv_in = I("wrkv", [1024, 3072])
    brkv_in = I("brkv_ct", [128, 24])
    wo2_in = I("wo2", [1024, 1024])
    bo_in = I("bo_ct", [128, 8])
    nul_in = I("nul_ct", [128, 8])
    gl_in = I("gl_ct", [128, 8])
    qcos_in = I("t_qcos", [128, 4096]); qsin_in = I("t_qsin", [128, 4096])
    kcos_in = I("t_kcos", [128, 4096]); ksin_in = I("t_ksin", [128, 4096])
    maskS_in = I("mask_std", [512, 1024], BF16)
    maskF_in = I("mask_first", [512, 1024], BF16)
    e2c_in = I("c_e2", [128, 2])
    e2v_in = I("c_e2v", [128, 8])
    flag_in = I("haloflag", [128, 1])
    ident_in = I("c_ident", [128, 128])
    x2_out = nc.dram_tensor("x2seg", [2048, 1024], F32, kind="ExternalOutput")
    x2p = nc.dram_tensor("x2p", [128, 16384], F32, kind="Internal")
    actA_hbm = nc.dram_tensor("actA_hbm", [7, 128, 11264], BF16, kind="Internal")
    actB_hbm = nc.dram_tensor("actB_hbm", [4, 128, 11264], BF16, kind="Internal")

    AR = nc.alloc_sbuf_tensor("AR", [128, 6144], F32)
    XN1 = nc.alloc_sbuf_tensor("XN1", [128, 8 * XS2], BF16)
    XFF = nc.alloc_sbuf_tensor("XFF", [128, 8 * XSA], BF16)
    WQB = nc.alloc_sbuf_tensor("WQB", [128, 12288], BF16)
    WOB = nc.alloc_sbuf_tensor("WOB", [128, 4096], BF16)
    KB0 = nc.alloc_sbuf_tensor("KB0", [128, 2048], BF16)
    KB1 = nc.alloc_sbuf_tensor("KB1", [128, 2048], BF16)
    VB0 = nc.alloc_sbuf_tensor("VB0", [128, 2048], BF16)
    VB1 = nc.alloc_sbuf_tensor("VB1", [128, 2048], BF16)
    QBF = nc.alloc_sbuf_tensor("QBF", [128, 2048], BF16)
    OF = nc.alloc_sbuf_tensor("OF", [128, 2048], BF16)
    ASTORE = nc.alloc_sbuf_tensor("ASTORE", [128, 2048], BF16)
    MSK = ASTORE[:, 0:1024]
    ATT = ASTORE[:, 1024:2048]

    c = Ctx(); c.nc = nc; c.XS = XS2
    c.ident = nc.alloc_sbuf_tensor("ident", [128, 128], F32)
    c.cst = nc.alloc_sbuf_tensor("cst", [128, 4], F32)
    c.xnF = XN1
    c.xtm = AR[:, 0:1024]
    c.lnt = AR[:, 1024:2048]
    c.lns = AR[:, 2048:3072]
    c.lnm = nc.alloc_sbuf_tensor("lnm", [128, 8], F32)
    c.row1 = AR[0:1, 3072:4096]
    c.rowm = nc.alloc_sbuf_tensor("rowm", [1, 4], F32)
    c.pT = nc.alloc_psum_tensor("pT", [128, 128], F32)
    c.pM = nc.alloc_psum_tensor("pM", [128, 512], F32)
    c.pM2 = nc.alloc_psum_tensor("pM2", [128, 512], F32)
    c.pM3 = nc.alloc_psum_tensor("pM3", [128, 512], F32)
    c.pSIM = nc.alloc_psum_tensor("pSIM", [128, 1024], F32)
    c.pSIM2 = nc.alloc_psum_tensor("pSIM2", [128, 1024], F32)

    usum_sb = AR[:, 3072:3120]
    brkv = AR[:, 3120:3144]
    lam8 = AR[:, 3152:3160]
    nu8 = AR[:, 3160:3168]
    gam8 = AR[:, 3168:3176]
    lam16 = AR[:, 3176:3200]
    st16 = AR[:, 3200:3216]
    bo8 = nc.alloc_sbuf_tensor("bo8", [128, 8], F32)
    e2c = nc.alloc_sbuf_tensor("e2c", [128, 2], F32)
    e2v = nc.alloc_sbuf_tensor("e2v", [128, 8], F32)
    lng = nc.alloc_sbuf_tensor("lng", [128, 16], F32)
    lnb = nc.alloc_sbuf_tensor("lnb", [128, 16], F32)
    b2c = nc.alloc_sbuf_tensor("b2c", [128, 16], F32)
    b1pc = nc.alloc_sbuf_tensor("b1pc", [128, 88], F32)
    flagc = nc.alloc_sbuf_tensor("flagc", [128, 1], F32)

    s = Sched(nc)
    for (dst, src) in ((c.ident, ident_in),
                       (e2c, e2c_in), (e2v, e2v_in), (lng, lng_in),
                       (lnb, lnb_in), (b2c, b2_in), (b1pc, b1p_in),
                       (flagc, flag_in)):
        s.add("sync", lambda e, dst=dst, src=src: e.dma_start(
            out=dst[:, :], in_=src.ap()), dma=True)
    emit_consts(s, c)

    x2pa = x2p.ap().rearrange("p (j t) -> p j t", t=2048)
    STG = AR[:, 3072:4096]
    for i in range(33):
        xv = c.xtm
        s.add("sync", lambda e, i=i, xv=xv: e.dma_start(
            out=xv, in_=x_in.ap()[128 * i:128 * (i + 1), :]), dma=True)
        emit_ln_tile(s, c, xv, c.lns[:, :], c.lnm)
        for j in range(8):
            s.add("tensor", lambda e, j=j: e.transpose(
                c.pT[:, :], c.lns[:, 128 * j:128 * (j + 1)], c.ident[:, :]))
            s.add("scalar", lambda e, i=i, j=j: e.copy(
                c.xnF[:, XS2 * j + 128 * i: XS2 * j + 128 * (i + 1)], c.pT[:, :]))
        if i >= 17:
            for j in range(8):
                s.add("tensor", lambda e, j=j, xv=xv: e.transpose(
                    c.pT[:, :], xv[:, 128 * j:128 * (j + 1)], c.ident[:, :]))
                s.add("vector", lambda e, j=j: e.tensor_copy(
                    STG[:, 128 * j:128 * (j + 1)], c.pT[:, :]))
            s.add("sync", lambda e, i=i: e.dma_start(
                out=x2pa[:, :, 128 * (i - 17):128 * (i - 16)],
                in_=STG.rearrange("p (j t) -> p j t", t=128)), dma=True)

    for (dst, src) in ((usum_sb, usum_in), (brkv, brkv_in), (bo8, bo_in),
                       (nu8, nul_in), (gam8, gl_in)):
        s.add("sync", lambda e, dst=dst, src=src: e.dma_start(
            out=dst[:, :], in_=src.ap()), dma=True)
    s.add("scalar", lambda e: e.activation(lam8[:, :], nu8[:, :], AF.Exp))
    s.add("vector", lambda e: e.tensor_scalar_mul(lam8[:, :], lam8[:, :], -1.0))
    s.add("scalar", lambda e: e.activation(lam8[:, :], lam8[:, :], AF.Exp))
    s.add("vector", lambda e: e.tensor_add(gam8[:, :], nu8[:, :], gam8[:, :]))
    s.add("scalar", lambda e: e.activation(gam8[:, :], gam8[:, :], AF.Exp))
    s.add("vector", lambda e: e.tensor_scalar(gam8[:, :], gam8[:, :], 1.0, -1.0, ALU.mult, ALU.add))
    s.add("vector", lambda e: e.tensor_copy(lam16[:, 16:24], lam8[:, :]))
    for _ in range(11):
        s.add("vector", lambda e: e.tensor_mul(lam16[:, 16:24], lam16[:, 16:24], lam16[:, 16:24]))
    s.add("vector", lambda e: e.tensor_copy(lam16[:, 0:8], lam16[:, 16:24]))
    s.add("vector", lambda e: e.tensor_copy(lam16[:, 8:16], lam16[:, 16:24]))
    s.add("vector", lambda e: e.memset(st16[:, :], 0.0))
    for kslot in range(3):
        s.add("vector", lambda e: e.tensor_mul(st16[:, :], st16[:, :], lam16[:, 0:16]))
        s.add("vector", lambda e, kslot=kslot: e.tensor_add(
            st16[:, :], st16[:, :], usum_sb[:, 16 * kslot:16 * (kslot + 1)]))

    KT = AR[:, 0:512]; VT = AR[:, 512:1024]; KVT = AR[:, 1024:1536]
    Y1 = AR[:, 1536:2048]; Y2 = AR[:, 2048:2560]; LREP = AR[:, 2560:3072]
    XL1 = XFF[:, 0:16384]
    RT = MSK
    KWB = WQB[:, 0:1024]; VWB = WQB[:, 1024:2048]; RWB = WQB[:, 2048:3072]
    wrkv3 = wrkv_in.ap().rearrange("(kc p) n -> p kc n", p=128)
    for j in range(8):
        for (dstb, coff) in ((RWB, 0), (KWB, 1024), (VWB, 2048)):
            s.add("gpsimd", lambda e, dstb=dstb, coff=coff, j=j: e.dma_start(
                out=dstb.rearrange("p (kc n) -> p kc n", n=128),
                in_=wrkv3[:, :, coff + 128 * j:coff + 128 * (j + 1)]), dma=True)
        s.add("vector", lambda e: e.memset(LREP, 1.0))
        s.add("vector", lambda e, j=j: e.tensor_scalar_mul(LREP, LREP, lam8[:, j:j + 1]))
        for th in range(4):
            t0 = 512 * th
            emit_mm8(s, c, c.pM, 0,
                     lambda kc, t0=t0: c.xnF[:, XS2 * kc + 2176 + t0: XS2 * kc + 2176 + t0 + 512],
                     512, wbuf=KWB, wcols=128)
            s.add("scalar", lambda e, j=j: e.activation(
                KT[:, :], c.pM[:, 0:512], AF.Exp, bias=brkv[:, 8 + j:9 + j]))
            emit_mm8(s, c, c.pM2, 0,
                     lambda kc, t0=t0: c.xnF[:, XS2 * kc + 2176 + t0: XS2 * kc + 2176 + t0 + 512],
                     512, wbuf=VWB, wcols=128)
            s.add("vector", lambda e, j=j: e.tensor_scalar_add(
                VT[:, :], c.pM2[:, 0:512], brkv[:, 16 + j:17 + j]))
            emit_mm8(s, c, c.pM, 0,
                     lambda kc, t0=t0: c.xnF[:, XS2 * kc + 2176 + t0: XS2 * kc + 2176 + t0 + 512],
                     512, wbuf=RWB, wcols=128)
            s.add("scalar", lambda e, j=j: e.activation(
                RT[:, 0:512], c.pM[:, 0:512], AF.Sigmoid, bias=brkv[:, j:j + 1]))
            s.add("vector", lambda e: e.tensor_mul(KVT[:, :], KT[:, :], VT[:, :]))
            s.add("vector", lambda e, j=j: e.tensor_tensor_scan(
                Y1[:, :], LREP, KVT[:, :], st16[:, j:j + 1], ALU.mult, ALU.add))
            s.add("vector", lambda e, j=j: e.tensor_tensor_scan(
                Y2[:, :], LREP, KT[:, :], st16[:, 8 + j:9 + j], ALU.mult, ALU.add))
            s.add("vector", lambda e, j=j: e.tensor_copy(st16[:, j:j + 1], Y1[:, 511:512]))
            s.add("vector", lambda e, j=j: e.tensor_copy(st16[:, 8 + j:9 + j], Y2[:, 511:512]))
            s.add("vector", lambda e, j=j: e.scalar_tensor_tensor(
                KVT[:, :], KVT[:, :], gam8[:, j:j + 1], Y1[:, :], ALU.mult, ALU.add))
            s.add("vector", lambda e, j=j: e.scalar_tensor_tensor(
                KT[:, :], KT[:, :], gam8[:, j:j + 1], Y2[:, :], ALU.mult, ALU.add))
            s.add("scalar", lambda e: e.activation(KT[:, :], KT[:, :], AF.Ln, bias=c.cst[:, 2:3]))
            s.add("scalar", lambda e: e.activation(KT[:, :], KT[:, :], AF.Exp, scale=-1.0))
            s.add("vector", lambda e: e.tensor_mul(KVT[:, :], KVT[:, :], KT[:, :]))
            s.add("vector", lambda e, j=j, th=th: e.tensor_mul(
                XL1[:, 2048 * j + 512 * th: 2048 * j + 512 * (th + 1)],
                KVT[:, :], RT[:, 0:512]))
    WO2B = WQB[:, 3072:11264]
    for kc in range(8):
        s.add("gpsimd", lambda e, kc=kc: e.dma_start(
            out=WO2B[:, 1024 * kc:1024 * (kc + 1)],
            in_=wo2_in.ap()[128 * kc:128 * (kc + 1), :]), dma=True)
    for cout in range(8):
        for tc in range(4):
            def fwo(e, cout=cout, tc=tc):
                last = None
                for kc in range(8):
                    last = e.matmul(c.pM[:, 0:512],
                                    WO2B[:, 1024 * kc + 128 * cout: 1024 * kc + 128 * (cout + 1)],
                                    XL1[:, 2048 * kc + 512 * tc: 2048 * kc + 512 * (tc + 1)],
                                    start=(kc == 0), stop=(kc == 7))
                return last
            s.add("tensor", fwo)
            s.add("vector", lambda e, cout=cout: e.tensor_scalar_add(
                STG[:, 0:512], c.pM[:, 0:512], bo8[:, cout:cout + 1]))
            s.add("gpsimd", lambda e, cout=cout, tc=tc: e.dma_start(
                out=x2p.ap()[:, 2048 * cout + 512 * tc: 2048 * cout + 512 * (tc + 1)],
                in_=STG[:, 0:512], accum_op=ALU.add), dma=True)

    if os.environ.get("MK_L2STOP") != "1":
        emit_ffn_layers(s, c, locals())
    s.emit("L2")
    return nc


def emit_ffn_layers(s, c, L):
    nc = c.nc
    AR = L["AR"]; XN1 = L["XN1"]; XFF = L["XFF"]; WQB = L["WQB"]
    WOB = L["WOB"]; MSK = L["MSK"]; QBF = L["QBF"]; ATT = L["ATT"]; OF = L["OF"]
    ASTORE = L["ASTORE"]
    KB = (L["KB0"], L["KB1"]); VB = (L["VB0"], L["VB1"])
    wqkv_in = L["wqkv_in"]; wo_in = L["wo_in"]; w1p_in = L["w1p_in"]
    w2_in = L["w2_in"]; maskS_in = L["maskS_in"]; maskF_in = L["maskF_in"]
    qcos_in = L["qcos_in"]; qsin_in = L["qsin_in"]
    kcos_in = L["kcos_in"]; ksin_in = L["ksin_in"]
    actA_hbm = L["actA_hbm"]; actB_hbm = L["actB_hbm"]
    x2p = L["x2p"]; x2_out = L["x2_out"]; x2pa = L["x2pa"]
    e2c = L["e2c"]; e2v = L["e2v"]; lng = L["lng"]; lnb = L["lnb"]
    b2c = L["b2c"]; b1pc = L["b1pc"]; flagc = L["flagc"]; STG = L["STG"]

    ones_colb = nc.alloc_sbuf_tensor("ones_colb", [128, 1], BF16)
    ones_colf = nc.alloc_sbuf_tensor("ones_colf", [128, 1], F32)
    ones_rowf = nc.alloc_sbuf_tensor("ones_rowf_a", [1, 128], F32)
    s.add("vector", lambda e: e.memset(ones_colb[:, :], 1.0))
    s.add("vector", lambda e: e.memset(ones_colf[:, :], 1.0))
    s.add("vector", lambda e: e.memset(ones_rowf[:, :], 1.0))
    mskS_src = maskS_in.ap()
    mskF_src = maskF_in.ap()

    QFp_ = [AR[:, 0:512], AR[:, 0:512]]
    SQp_ = [AR[:, 512:1024], AR[:, 512:1024]]
    BCp_ = [AR[:, 1024:1536], AR[:, 1024:1536]]
    ROTp_ = [AR[:, 1536:2048], AR[:, 1536:2048]]
    W1p_ = [AR[:, 2048:2560], AR[:, 2048:2560]]
    W2p_ = [AR[:, 2560:3072], AR[:, 2560:3072]]
    TCOk = AR[:, 3072:3584]; TSIk = AR[:, 3584:4096]
    TCOq = TCOk; TSIq = TSIk
    R2a = AR[0:1, 4096:4608]; R2b = AR[0:1, 4608:5120]
    R2full = AR[0:1, 4096:5120]
    SMp_ = [AR[:, 0:1024], AR[:, 4096:5120]]
    EFp_ = [AR[:, 1024:2048], AR[:, 5120:6144]]
    cps_ = [c.pM2, c.pSIM2]
    pQK_ = [c.pSIM, c.pSIM2]
    pKV_ = [c.pM, c.pM3]

    tiles = _mlp_tiles()

    def stream_view(l, j, t, n, shifted):
        off = -1 if (shifted and j >= 4) else 0
        if l == 0:
            return XN1[:, XS2 * j + 2176 + t + off: XS2 * j + 2176 + t + off + n]
        else:
            return XFF[:, XSA * j + 1536 + t + off: XSA * j + 1536 + t + off + n]

    def dst_view(l, j, t, n):
        if l == 0:
            return XFF[:, XSA * j + 1536 + t: XSA * j + 1536 + t + n]
        else:
            return XN1[:, XSB * j + 512 + t: XSB * j + 512 + t + n]

    bar0 = [(k2, v2) for k2, v2 in s.src_cnt.items()]
    l2dbg = os.environ.get("MK_L2DBG") == "1"
    if l2dbg:
        dbg_xa = nc.dram_tensor("dbg_xa", [128, 28672], BF16, kind="ExternalOutput")
        dbg_hn = nc.dram_tensor("dbg_hn", [128, 33792], BF16, kind="ExternalOutput")
        dbg_as = nc.dram_tensor("dbg_as", [2, 128, 2048], BF16, kind="ExternalOutput")
        dbg_pnl = nc.dram_tensor("dbg_pnl", [128, 1024], BF16, kind="ExternalOutput")

    chain_prev = [None, None]
    tp_lastp = [None, None]
    exp_prev = [None, None]
    tp_ev = None
    qk_prev = None
    fo_prev = None
    ofcopy_prev = [None, None]
    woadd_prev = None
    wo_mm_prev = None
    mlp2_last = None
    p1_last = None
    qk_layer_last = None

    for l in range(2):
        kv_lo = -4 if l == 0 else -2
        q_lo = kv_lo + 1
        wdep = bar0 if l == 0 else [p1_last, mlp2_last]
        wq_tok = s.add("gpsimd", lambda e, l=l: e.dma_start(
            out=WQB[:, 0:12288].rearrange("p (kc n) -> p kc n", n=1536),
            in_=wqkv_in.ap()[l].rearrange("(kc p) n -> p kc n", p=128)),
            dma=True, deps=wdep, stream="wq")
        wol_tok = s.add("gpsimd", lambda e, l=l: e.dma_start(
            out=WOB[:, 0:4096].rearrange("p (icn n) -> p icn n", n=1024),
            in_=wo_in.ap()[l].rearrange("(icn p) n -> p icn n", p=128)),
            dma=True, deps=wdep + [wo_mm_prev], stream="wol")

        def proj_post(p, mmtok, tco, tsi, evcol, out_view, extra=None):
            ch = Chain(s, deps=[mmtok, chain_prev[0], chain_prev[1],
                                tp_lastp[0], tp_lastp[1]] + (extra or []))
            QF, SQ, BC, ROT, W1_, W2_ = (QFp_[p], SQp_[p], BCp_[p], ROTp_[p],
                                         W1p_[p], W2p_[p])
            cps = cps_[p]; src_ps = pKV_[p]
            ch.step("vector", lambda e: e.tensor_copy(QF, src_ps[:, 0:512]))
            ch.step("scalar", lambda e: e.square(SQ, QF))
            ch.step("tensor", lambda e: e.matmul(cps[0:1, 0:512], e2c[:, 0:1], SQ,
                                                 start=True, stop=True))
            ch.step("scalar", lambda e: e.sqrt(R2a, cps[0:1, 0:512]))
            ch.step("tensor", lambda e: e.matmul(cps[0:1, 0:512], e2c[:, 1:2], SQ,
                                                 start=True, stop=True))
            ch.step("scalar", lambda e: e.sqrt(R2b, cps[0:1, 0:512]))
            ch.step("vector", lambda e: e.tensor_single_scalar(R2full, R2full, 1e-12, ALU.max))
            ch.step("scalar", lambda e: e.activation(R2full, R2full, AF.Ln))
            ch.step("scalar", lambda e: e.activation(R2full, R2full, AF.Exp, scale=-1.0))
            ch.step("tensor", lambda e: e.matmul(cps[:, 0:512], ones_rowf[:, :], R2a,
                                                 start=True, stop=True))
            ch.step("vector", lambda e, evcol=evcol: e.tensor_scalar_mul(
                BC, cps[:, 0:512], e2v[:, evcol:evcol + 1]))
            ch.step("tensor", lambda e: e.matmul(cps[:, 0:512], ones_rowf[:, :], R2b,
                                                 start=True, stop=True))
            ch.step("vector", lambda e, evcol=evcol: e.scalar_tensor_tensor(
                BC, cps[:, 0:512], e2v[:, evcol + 1:evcol + 2], BC,
                ALU.mult, ALU.add))
            ch.step("vector", lambda e: e.tensor_mul(W1_, QF, BC))
            ch.step("gpsimd", lambda e: e.tensor_scalar_mul(ROT[0:32, :], W1_[32:64, :], -1.0))
            ch.step("gpsimd", lambda e: e.tensor_copy(ROT[32:64, :], W1_[0:32, :]))
            ch.step("gpsimd", lambda e: e.tensor_scalar_mul(ROT[64:96, :], W1_[96:128, :], -1.0))
            ch.step("gpsimd", lambda e: e.tensor_copy(ROT[96:128, :], W1_[64:96, :]))
            ch.step("vector", lambda e, tco=tco: e.tensor_mul(W2_, W1_, tco))
            ch.step("gpsimd", lambda e, tsi=tsi: e.tensor_mul(ROT, ROT, tsi))
            tok = ch.step("vector", lambda e, out_view=out_view: e.tensor_add(out_view, W2_, ROT))
            chain_prev[p] = tok
            return tok

        msk_tok = None
        sm_last = None
        tco_k_tok = None
        tco_q_tok = None
        vc_free = [None, None]
        for w in range(kv_lo, 4):
            par = (w - kv_lo) % 2
            KBc, VBc = KB[par], VB[par]
            KBp, VBp = KB[1 - par], VB[1 - par]
            t0 = 512 * w
            tco_k_tok = s.add("sync", lambda e, w=w: e.dma_start(
                out=TCOk, in_=kcos_in.ap()[:, 2048 + 512 * w:2048 + 512 * (w + 1)]),
                dma=True, deps=wdep + [chain_prev[0], chain_prev[1]], stream="tk")
            tsi_k_tok = s.add("sync", lambda e, w=w: e.dma_start(
                out=TSIk, in_=ksin_in.ap()[:, 2048 + 512 * w:2048 + 512 * (w + 1)]),
                dma=True, deps=wdep + [chain_prev[0], chain_prev[1]], stream="tk")
            kadd_last = None
            for ot in range(4):
                p = ot % 2
                def fk(e, l=l, ot=ot, t0=t0):
                    last = None
                    for kc in range(8):
                        last = e.matmul(pKV_[ot % 2][:, 0:512],
                                        WQB[:, 1536 * kc + 512 + 128 * ot: 1536 * kc + 512 + 128 * (ot + 1)],
                                        stream_view(l, kc, t0, 512, True),
                                        start=(kc == 0), stop=(kc == 7))
                    return last
                mmk = s.add("tensor", fk, deps=wdep + [
                    wq_tok, chain_prev[p], ofcopy_prev[0], ofcopy_prev[1],
                    woadd_prev, qk_prev, vc_free[ot % 2]])
                kadd_last = proj_post(p, mmk, TCOk, TSIk, 4 * l + 2,
                                      KBc[:, 512 * ot:512 * (ot + 1)],
                                      extra=[tco_k_tok, tsi_k_tok, qk_prev])
            vcopy_last = None
            for tt in range(4):
                def fv(e, l=l, tt=tt, t0=t0):
                    last = None
                    for kc in range(8):
                        last = e.matmul(pKV_[tt % 2][:, 0:512],
                                        stream_view(l, kc, t0 + 128 * tt, 128, True),
                                        WQB[:, 1536 * kc + 1024:1536 * kc + 1536],
                                        start=(kc == 0), stop=(kc == 7))
                    return last
                mmv = s.add("tensor", fv, deps=[wq_tok, chain_prev[tt % 2],
                                                fo_prev, vc_free[tt % 2]])
                vcopy_last = s.add("vector", lambda e, tt=tt, VBc=VBc: e.tensor_copy(
                    VBc[:, 512 * tt:512 * (tt + 1)], pKV_[tt % 2][:, 0:512]),
                    deps=[mmv, fo_prev])
                vc_free[tt % 2] = vcopy_last
            if w < q_lo:
                qk_prev = mmv
                continue
            tco_q_tok = s.add("sync", lambda e, w=w: e.dma_start(
                out=TCOq, in_=qcos_in.ap()[:, 2048 + 512 * w:2048 + 512 * (w + 1)]),
                dma=True, deps=[kadd_last, chain_prev[0], chain_prev[1]], stream="tk")
            tsi_q_tok = s.add("sync", lambda e, w=w: e.dma_start(
                out=TSIq, in_=qsin_in.ap()[:, 2048 + 512 * w:2048 + 512 * (w + 1)]),
                dma=True, deps=[kadd_last, chain_prev[0], chain_prev[1]], stream="tk")
            qadd = [None] * 4
            for ot in range(4):
                p = ot % 2
                def fq(e, l=l, ot=ot, t0=t0):
                    last = None
                    for kc in range(8):
                        last = e.matmul(pKV_[ot % 2][:, 0:512],
                                        WQB[:, 1536 * kc + 128 * ot: 1536 * kc + 128 * (ot + 1)],
                                        stream_view(l, kc, t0, 512, True),
                                        start=(kc == 0), stop=(kc == 7))
                    return last
                mmq = s.add("tensor", fq, deps=[wq_tok, chain_prev[p], qk_prev,
                                                vc_free[ot % 2]])
                qadd[ot] = proj_post(p, mmq, TCOq, TSIq, 4 * l,
                                     QBF[:, 512 * ot:512 * (ot + 1)],
                                     extra=[tco_q_tok, tsi_q_tok, qk_prev])
            msrc = mskF_src if w == 0 else mskS_src
            of_toks = []

            def emit_tpfo(unit):
                nonlocal tp_ev, fo_prev
                p, EF, h, ic, mul_tok = unit
                ev_toks = []
                for jc in range(8):
                    tt2 = s.add("tensor", lambda e, jc=jc, EF=EF: e.transpose(
                        c.pT[:, :], EF[:, 128 * jc:128 * (jc + 1)], c.ident[:, :]),
                        deps=[mul_tok, tp_ev])
                    tp_lastp[p] = tt2
                    if jc < 4:
                        tp_ev = s.add("scalar", lambda e, jc=jc: e.copy(
                            ATT[:, 128 * jc:128 * (jc + 1)], c.pT[:, :]), deps=[tt2])
                    else:
                        tp_ev = s.add("vector", lambda e, jc=jc: e.tensor_copy(
                            ATT[:, 128 * jc:128 * (jc + 1)], c.pT[:, :]), deps=[tt2])
                    ev_toks.append(tp_ev)
                def fo(e, h=h, VBp=VBp, VBc=VBc, p=p):
                    last = None
                    for jc in range(8):
                        vb = VBp if jc < 4 else VBc
                        tt = jc % 4
                        last = e.matmul(c.pM[0:64, 128 * p:128 * (p + 1)],
                                        vb[:, 512 * tt + 64 * h: 512 * tt + 64 * (h + 1)],
                                        ATT[:, 128 * jc:128 * (jc + 1)],
                                        start=(jc == 0), stop=(jc == 7))
                    return last
                fo_prev = s.add("tensor", fo,
                                deps=ev_toks + [vcopy_last, ofcopy_prev[p]])
                ofcopy_prev[p] = s.add("scalar", lambda e, h=h, ic=ic, p=p: e.copy(
                    OF[(h % 2) * 64:(h % 2) * 64 + 64,
                       512 * (h // 2) + 128 * ic:512 * (h // 2) + 128 * (ic + 1)],
                    c.pM[0:64, 128 * p:128 * (p + 1)]), deps=[fo_prev])
                of_toks.append(ofcopy_prev[p])

            pend = None
            for ic in range(4):
                msk_tok = s.add("sync", lambda e, ic=ic, msrc=msrc: e.dma_start(
                    out=MSK[:, :], in_=msrc[128 * ic:128 * (ic + 1), :]),
                    dma=True, stream="msk",
                    deps=wdep + (bar0 if msk_tok is None else [sm_last, msk_tok]))
                for h in range(8):
                    p = h % 2
                    rows = (h % 2) * 64
                    tl = h // 2
                    SM = SMp_[p]; EF = EFp_[p]; pQK = pQK_[p]
                    qsl = QBF[rows:rows + 64, 512 * tl + 128 * ic: 512 * tl + 128 * (ic + 1)]
                    mmA = s.add("tensor", lambda e, qsl=qsl, KBp=KBp, rows=rows, tl=tl, pQK=pQK: e.matmul(
                        pQK[:, 0:512], qsl, KBp[rows:rows + 64, 512 * tl:512 * (tl + 1)],
                        start=True, stop=True),
                        deps=[qadd[3], kadd_last, vcopy_last, exp_prev[p]])
                    mmB = s.add("tensor", lambda e, qsl=qsl, KBc=KBc, rows=rows, tl=tl, pQK=pQK: e.matmul(
                        pQK[:, 512:1024], qsl, KBc[rows:rows + 64, 512 * tl:512 * (tl + 1)],
                        start=True, stop=True), deps=[])
                    sm_last = s.add("vector", lambda e, SM=SM, pQK=pQK: e.tensor_tensor(
                        SM, pQK[:, :], MSK[:, :], ALU.add),
                        deps=[mmA, mmB, msk_tok, exp_prev[p]])
                    ch = Chain(s, deps=[sm_last])
                    ch.step("vector", lambda e, SM=SM, p=p: e.reduce_max(
                        c.lnm[:, 4 * p:4 * p + 1], SM, AX))
                    ch.step("vector", lambda e, p=p: e.tensor_scalar_mul(
                        c.lnm[:, 4 * p + 1:4 * p + 2], c.lnm[:, 4 * p:4 * p + 1], -1.0))
                    exp_prev[p] = ch.step("scalar", lambda e, SM=SM, EF=EF, p=p: e.activation(
                        EF, SM, AF.Exp, bias=c.lnm[:, 4 * p + 1:4 * p + 2],
                        accum_out=c.lnm[:, 4 * p + 2:4 * p + 3]),
                        extra=[tp_lastp[p]])
                    ch.step("scalar", lambda e, p=p: e.activation(
                        c.lnm[:, 4 * p + 3:4 * p + 4], c.lnm[:, 4 * p + 2:4 * p + 3], AF.Ln))
                    ch.step("scalar", lambda e, p=p: e.activation(
                        c.lnm[:, 4 * p + 3:4 * p + 4], c.lnm[:, 4 * p + 3:4 * p + 4], AF.Exp, scale=-1.0))
                    mul_tok = ch.step("gpsimd", lambda e, EF=EF, p=p: e.tensor_scalar_mul(
                        EF, EF, c.lnm[:, 4 * p + 3:4 * p + 4]))
                    unit = (p, EF, h, ic, mul_tok)
                    if pend is not None:
                        emit_tpfo(pend)
                    pend = unit
            emit_tpfo(pend)
            woadds = [None, None]
            for dm in range(8):
                p = dm % 2
                pW = c.pM2 if p == 0 else c.pM3
                def fwo(e, dm=dm, pW=pW):
                    last = None
                    for icn in range(4):
                        last = e.matmul(pW[:, 0:512],
                                        WOB[:, 1024 * icn + 128 * dm: 1024 * icn + 128 * (dm + 1)],
                                        OF[:, 512 * icn:512 * (icn + 1)],
                                        start=(icn == 0), stop=(icn == 3))
                    return last
                wo_mm_prev = s.add("tensor", fwo,
                                   deps=[ofcopy_prev[0], ofcopy_prev[1], wol_tok,
                                         woadds[p], chain_prev[p]])
                woadds[p] = s.add("vector", lambda e, l=l, dm=dm, t0=t0, pW=pW: e.tensor_tensor(
                    dst_view(l, dm, t0, 512), pW[:, 0:512],
                    stream_view(l, dm, t0, 512, False), ALU.add),
                    deps=[wo_mm_prev] + (bar0 if l == 0 else [mlp2_last]))
                woadd_prev = woadds[p]
            qk_prev = wo_mm_prev
        qk_layer_last = qk_prev
        halo_last = None
        for j in range(8):
            if l == 0:
                halo_last = s.add("vector", lambda e, j=j: e.tensor_scalar_mul(
                    XFF[:, XSA * j:XSA * j + 1536], XFF[:, XSA * j:XSA * j + 1536],
                    flagc[:, 0:1]), deps=[woadd_prev])
            else:
                halo_last = s.add("vector", lambda e, j=j: e.tensor_scalar_mul(
                    XN1[:, XSB * j:XSB * j + 512], XN1[:, XSB * j:XSB * j + 512],
                    flagc[:, 0:1]), deps=[woadd_prev])
        if l2dbg and l == 0:
            dxa_tok = s.add("sync", lambda e: e.dma_start(
                out=dbg_xa.ap(), in_=XFF[:, :]), dma=True,
                deps=[halo_last], stream="dbgxa")
        NTC = 7 if l == 0 else 4
        tb = -1536 if l == 0 else 0

        def hn_view(l_, j, t, n):
            if l_ == 0:
                return XN1[:, 3584 * j + 1536 + t: 3584 * j + 1536 + t + n]
            return XFF[:, 2048 * j + t: 2048 * j + t + n]

        def mlp_src(l_, j, t, n, shifted):
            off = -1 if (shifted and j >= 4) else 0
            if l_ == 0:
                return XFF[:, XSA * j + 1536 + t + off: XSA * j + 1536 + t + off + n]
            else:
                return XN1[:, XSB * j + 512 + t + off: XSB * j + 512 + t + off + n]

        SQf_ = [AR[:, 0:512], AR[:, 4096:4608]]
        Tf_ = [AR[:, 512:1024], AR[:, 4608:5120]]
        mB_ = [AR[:, 1024:1536], AR[:, 5120:5632]]
        rB_ = [AR[:, 1536:2048], AR[:, 5632:6144]]
        mrow_ = [AR[0:1, 2048:2560], AR[0:1, 3072:3584]]
        vrow_ = [AR[0:1, 2560:3072], AR[0:1, 3584:4096]]
        qrow_ = [Tf_[0][0:1, :], Tf_[1][0:1, :]]
        hn_toks = {}
        stat_prev = [None, None]
        for tc in range(NTC):
            p = tc % 2
            t0 = tb + 512 * tc
            pS = pQK_[p]
            ch = Chain(s, deps=[halo_last, stat_prev[p]])
            def fsum(e, l=l, t0=t0, pS=pS):
                last = None
                for f in range(8):
                    last = e.matmul(pS[0:1, 0:512], ones_colb[:, :],
                                    mlp_src(l, f, t0, 512, True),
                                    start=(f == 0), stop=(f == 7))
                return last
            ch.step("tensor", fsum)
            for f in range(8):
                ch.step("scalar", lambda e, l=l, f=f, t0=t0, p=p: e.square(
                    SQf_[p], mlp_src(l, f, t0, 512, True)))
                ch.step("tensor", lambda e, f=f, pS=pS, p=p: e.matmul(
                    pS[0:1, 512:1024], ones_colf[:, :], SQf_[p],
                    start=(f == 0), stop=(f == 7)))
            ch.step("vector", lambda e, p=p, pS=pS: e.tensor_scalar_mul(
                mrow_[p], pS[0:1, 0:512], 1.0 / 1024.0))
            ch.step("vector", lambda e, p=p, pS=pS: e.tensor_scalar_mul(
                vrow_[p], pS[0:1, 512:1024], 1.0 / 1024.0))
            ch.step("scalar", lambda e, p=p: e.square(qrow_[p], mrow_[p]))
            ch.step("vector", lambda e, p=p: e.tensor_sub(vrow_[p], vrow_[p], qrow_[p]))
            ch.step("scalar", lambda e, p=p: e.activation(vrow_[p], vrow_[p], AF.Ln,
                                                          bias=c.cst[0:1, 0:1]))
            ch.step("scalar", lambda e, p=p: e.activation(vrow_[p], vrow_[p], AF.Exp, scale=-0.5))
            ch.step("tensor", lambda e, p=p, pS=pS: e.matmul(
                pS[:, 0:512], ones_rowf[:, :], mrow_[p], start=True, stop=True))
            ch.step("vector", lambda e, p=p, pS=pS: e.tensor_copy(mB_[p], pS[:, 0:512]))
            ch.step("tensor", lambda e, p=p, pS=pS: e.matmul(
                pS[:, 512:1024], ones_rowf[:, :], vrow_[p], start=True, stop=True))
            ch.step("vector", lambda e, p=p, pS=pS: e.tensor_copy(rB_[p], pS[0:128, 512:1024]))
            for f in range(8):
                ch.step("vector", lambda e, l=l, f=f, t0=t0, p=p: e.tensor_sub(
                    Tf_[p], mlp_src(l, f, t0, 512, True), mB_[p]))
                ch.step("vector", lambda e, p=p: e.tensor_mul(Tf_[p], Tf_[p], rB_[p]))
                tok = ch.step("vector", lambda e, l=l, f=f, t0=t0, p=p: e.tensor_scalar(
                    hn_view(l, f, t0, 512),
                    Tf_[p], lng[:, 8 * l + f:8 * l + f + 1], lnb[:, 8 * l + f:8 * l + f + 1],
                    ALU.mult, ALU.add))
            hn_toks[tc] = tok
            stat_prev[p] = tok
        if l2dbg and l == 0:
            dhn_tok = s.add("sync", lambda e: e.dma_start(
                out=dbg_hn.ap(), in_=XN1[:, :]), dma=True,
                deps=[hn_toks[tc2] for tc2 in range(NTC)], stream="dbghn")
        PNL_ = [WQB[:, 0:1024], WQB[:, 1024:2048]]
        act_hbm = actA_hbm if l == 0 else actB_hbm
        tc_groups = [(0, 4), (4, 7)] if l == 0 else [(0, 4)]
        w1p3 = w1p_in.ap()[l].rearrange("(kc p) n -> p kc n", p=128)
        pnl_rd = [qk_layer_last, qk_layer_last]
        p1_free = [None, None]
        spill_prev = [None, None]
        spill_by_tc = {}
        gelu_prev = qk_layer_last
        mul_prev = None
        gi = 0
        mmi1 = 0
        p1ser = os.environ.get("MK_P1SER") == "1"
        def dd(lst):
            return None if p1ser else lst
        for i, (lo, hi) in enumerate(tiles):
            rn = hi - lo
            for (g0, g1) in tc_groups:
                buf = gi % 2
                ld_a = s.add("gpsimd", lambda e, lo=lo, rn=rn, buf=buf, w1p3=w1p3: e.dma_start(
                    out=PNL_[buf].rearrange("p (kc n) -> p kc n", n=128)[:, :, 0:rn],
                    in_=w1p3[:, :, 2 * lo:2 * lo + rn]), dma=True,
                    deps=dd([pnl_rd[buf]]), stream=f"p{buf}")
                gi += 1
                a_toks = {}
                for tc in range(g0, g1):
                    pP = pKV_[mmi1 % 2]
                    mm = s.add("tensor", lambda e, l=l, rn=rn, tc=tc, tb=tb, buf=buf, pP=pP: [
                        e.matmul(pP[0:rn, 0:512],
                                 PNL_[buf][:, 128 * kc:128 * kc + rn],
                                 hn_view(l, kc, tb + 512 * tc, 512),
                                 start=(kc == 0), stop=(kc == 7)) for kc in range(8)][-1],
                        deps=dd([ld_a, hn_toks[tc], p1_free[mmi1 % 2]]))
                    pnl_rd[buf] = mm
                    a_toks[tc] = s.add("vector", lambda e, l=l, i=i, rn=rn, tc=tc, g0=g0, pP=pP: e.tensor_scalar_add(
                        ASTORE[0:rn, 512 * (tc - g0):512 * (tc - g0) + 512], pP[0:rn, 0:512],
                        b1pc[0:rn, 44 * l + 2 * i:44 * l + 2 * i + 1]),
                        deps=dd([mm, mul_prev]))
                    p1_free[mmi1 % 2] = a_toks[tc]
                    mmi1 += 1
                buf = gi % 2
                ld_g = s.add("gpsimd", lambda e, lo=lo, rn=rn, buf=buf, w1p3=w1p3: e.dma_start(
                    out=PNL_[buf].rearrange("p (kc n) -> p kc n", n=128)[:, :, 0:rn],
                    in_=w1p3[:, :, 2 * lo + rn:2 * lo + 2 * rn]), dma=True,
                    deps=dd([pnl_rd[buf]]), stream=f"p{buf}")
                gi += 1
                for tc in range(g0, g1):
                    sp = (i + tc) % 2
                    pP = pKV_[mmi1 % 2]
                    mm = s.add("tensor", lambda e, l=l, rn=rn, tc=tc, tb=tb, buf=buf, pP=pP: [
                        e.matmul(pP[0:rn, 0:512],
                                 PNL_[buf][:, 128 * kc:128 * kc + rn],
                                 hn_view(l, kc, tb + 512 * tc, 512),
                                 start=(kc == 0), stop=(kc == 7)) for kc in range(8)][-1],
                        deps=dd([ld_g, hn_toks[tc], p1_free[mmi1 % 2]]))
                    pnl_rd[buf] = mm
                    p1_last = mm
                    gelu_prev = s.add("scalar", lambda e, l=l, i=i, rn=rn, pP=pP: e.activation(
                        QBF[0:rn, 0:512], pP[0:rn, 0:512], AF.Gelu,
                        bias=b1pc[0:rn, 44 * l + 2 * i + 1:44 * l + 2 * i + 2]),
                        deps=dd([mm, gelu_prev, mul_prev]))
                    p1_free[mmi1 % 2] = gelu_prev
                    mmi1 += 1
                    stg = QBF[:, 512 + 512 * sp:1024 + 512 * sp]
                    if rn < 128:
                        mz = s.add("vector", lambda e, stg=stg: e.memset(stg, 0.0),
                                   deps=dd([gelu_prev, spill_prev[sp]]))
                    else:
                        mz = None
                    mul_prev = s.add("vector", lambda e, rn=rn, tc=tc, g0=g0, stg=stg: e.tensor_mul(
                        stg[0:rn, :], ASTORE[0:rn, 512 * (tc - g0):512 * (tc - g0) + 512],
                        QBF[0:rn, 0:512]),
                        deps=dd([gelu_prev, a_toks[tc], spill_prev[sp], mz]))
                    spill_prev[sp] = s.add("gpsimd", lambda e, tc=tc, i=i, act_hbm=act_hbm, stg=stg: e.dma_start(
                        out=act_hbm.ap()[tc, :, 512 * i:512 * (i + 1)],
                        in_=stg), dma=True, deps=dd([mul_prev]), stream=f"sp{sp}")
                    spill_by_tc.setdefault(tc, []).append(spill_prev[sp])
                if l2dbg and l == 0 and i == 0 and g0 == 0:
                    dtok = s.add("sync", lambda e: e.dma_start(
                        out=dbg_as.ap()[0], in_=ASTORE[:, :]), dma=True,
                        deps=[a_toks[3]], stream="dbgas")
                    dptok = s.add("sync", lambda e: e.dma_start(
                        out=dbg_pnl.ap(), in_=PNL_[0][:, :]), dma=True,
                        deps=[a_toks[3]], stream="dbgpnl")
                    mul_prev = dptok
                    dtok2 = s.add("sync", lambda e: e.dma_start(
                        out=dbg_as.ap()[1], in_=QBF[:, 0:2048]), dma=True,
                        deps=[gelu_prev, mul_prev], stream="dbgas2")
                    mul_prev = dtok
                    gelu_prev = dtok2
        if l == 0:
            W2L = XN1[:, 0:22528]
            ACTL = XN1[:, 22528:33792]
        else:
            W2L = XFF[:, 0:22528]
            ACTL = XN1[:, 20480:31744]
        w2l_tok = None
        for kc, (lo, hi) in enumerate(tiles):
            rn = hi - lo
            w2l_tok = s.add("gpsimd", lambda e, l=l, kc=kc, lo=lo, rn=rn, W2L=W2L: e.dma_start(
                out=W2L[0:rn, 1024 * kc:1024 * (kc + 1)],
                in_=w2_in.ap()[l, lo:lo + rn, :]), dma=True,
                deps=[p1_last] + ([dhn_tok, dxa_tok] if l2dbg and l == 0 else []),
                stream="w2l")
        fw2_prev = None
        fw2_free = [None, None]
        actl_tok = None
        for tc in range(NTC):
            actl_tok = s.add("sync", lambda e, tc=tc, act_hbm=act_hbm, ACTL=ACTL: e.dma_start(
                out=ACTL[:, 0:11264], in_=act_hbm.ap()[tc, :, :]), dma=True,
                deps=[p1_last, fw2_prev] + spill_by_tc[tc]
                     + ([dhn_tok, dxa_tok] if l2dbg and l == 0 else []),
                stream="actl")
            for h in range(8):
                p = h % 2
                pP = pKV_[p]
                def fw2(e, h=h, W2L=W2L, ACTL=ACTL, pP=pP):
                    last = None
                    for kc, (lo, hi) in enumerate(tiles):
                        rn = hi - lo
                        last = e.matmul(pP[:, 0:512],
                                        W2L[0:rn, 1024 * kc + 128 * h: 1024 * kc + 128 * (h + 1)],
                                        ACTL[0:rn, 512 * kc:512 * (kc + 1)],
                                        start=(kc == 0), stop=(kc == 21))
                    return last
                fw2_prev = s.add("tensor", fw2,
                                 deps=[actl_tok, w2l_tok, fw2_free[p]])
                mlp2_last = s.add("vector", lambda e, l=l, h=h, tc=tc, tb=tb, pP=pP: e.scalar_tensor_tensor(
                    mlp_src(l, h, tb + 512 * tc, 512, False),
                    pP[:, 0:512], b2c[:, 8 * l + h:8 * l + h + 1],
                    mlp_src(l, h, tb + 512 * tc, 512, False), ALU.add, ALU.add),
                    deps=[fw2_prev])
                fw2_free[p] = mlp2_last

    for i in range(16):
        s.add("sync", lambda e, i=i: e.dma_start(
            out=STG.rearrange("p (j t) -> p j t", t=128),
            in_=x2pa[:, :, 128 * i:128 * (i + 1)]), dma=True)
        for j in range(8):
            s.add("vector", lambda e, i=i, j=j: e.tensor_tensor(
                STG[:, 128 * j:128 * (j + 1)], STG[:, 128 * j:128 * (j + 1)],
                XN1[:, XSB * j + 512 + 128 * i: XSB * j + 512 + 128 * (i + 1)], ALU.add))
            s.add("tensor", lambda e, j=j: e.transpose(
                c.pT[:, :], STG[:, 128 * j:128 * (j + 1)], c.ident[:, :]))
            s.add("vector", lambda e, j=j: e.tensor_copy(
                c.lnt[:, 128 * j:128 * (j + 1)], c.pT[:, :]))
        s.add("sync", lambda e, i=i: e.dma_start(
            out=x2_out.ap()[128 * i:128 * (i + 1), :], in_=c.lnt[:, 0:1024]), dma=True)


def host_consts_l2(s_idx):
    tloc = np.arange(-2048, 2048, dtype=np.float64)
    p = 2048.0 * s_idx + tloc
    half = np.arange(0, 64, 2, dtype=np.float64)
    inv = 1.0 / 10000.0 ** (half / 64)
    fr = p[:, None] * inv[None, :]
    fr = np.concatenate([fr, fr], axis=-1)
    base = (half + 0.4 * 64) / (1.4 * 64)
    power = (p - 4096.0) / 256.0
    sc = base[None, :] ** power[:, None]
    sc = np.concatenate([sc, sc], axis=-1)
    co, si = np.cos(fr), np.sin(fr)
    qcos = (co * sc * 8.0).T
    qsin = (si * sc * 8.0).T
    kcos = (co / sc).T
    ksin = (si / sc).T
    def dup(a):
        return _f32(np.concatenate([a, a], axis=0))
    i = np.arange(512)[:, None]
    j = np.arange(1024)[None, :]
    d = i - (j - 512)
    bad = (d < 0) | (d > 512)
    mstd = np.where(bad, -3.0e38, 0.0).astype(ml_dtypes.bfloat16)
    badf = bad | (j < 512)
    mfirst = np.where(badf, -3.0e38, 0.0).astype(ml_dtypes.bfloat16)
    return dict(t_qcos=dup(qcos), t_qsin=dup(qsin), t_kcos=dup(kcos),
                t_ksin=dup(ksin), mask_std=mstd, mask_first=mfirst)


def host_inputs_launch2(inputs, consts, x1_full, usums):
    x1 = np.asarray(x1_full, np.float32)
    wqkv = _f32(inputs["L1_ffn_wqkv"])
    wo = _f32(inputs["L1_ffn_wo"])
    w1 = np.asarray(inputs["L1_ffn_w1"], np.float32)
    b1 = np.asarray(inputs["L1_ffn_b1"], np.float32)
    perm = w1_perm()
    w1p = _f32(w1[:, :, perm])
    b1p = b1[:, perm]
    tiles = _mlp_tiles()
    b1p_ct = np.zeros((128, 88), np.float32)
    for l in range(2):
        off = 0
        for i, (lo, hi) in enumerate(tiles):
            rn = hi - lo
            b1p_ct[0:rn, 44 * l + 2 * i] = b1p[l, off:off + rn]
            b1p_ct[0:rn, 44 * l + 2 * i + 1] = b1p[l, off + rn:off + 2 * rn]
            off += 2 * rn
    w2 = _f32(inputs["L1_ffn_w2"])
    def ct2(a):
        out = np.zeros((128, 16), np.float32)
        for l in range(2):
            out[:, 8 * l:8 * (l + 1)] = np.asarray(a, np.float32)[l].reshape(8, 128).T
        return out
    b2c = ct2(inputs["L1_ffn_b2"])
    lng = ct2(inputs["L1_ffn_ln_g"])
    lnb = ct2(inputs["L1_ffn_ln_b"])
    wrkv = _f32(inputs["L1_attn_W_rkv"])
    brkv = np.asarray(inputs["L1_attn_b_rkv"], np.float32)
    brkv_ct = np.zeros((128, 24), np.float32)
    for part in range(3):
        brkv_ct[:, 8 * part:8 * (part + 1)] = brkv[1024 * part:1024 * (part + 1)].reshape(8, 128).T
    wo2 = _f32(inputs["L1_attn_W_o"])
    bo_ct = ctile(inputs["L1_attn_b_o"])
    nul = ctile(inputs["L1_attn_nu_log"])
    gl = ctile(inputs["L1_attn_gamma_log"])
    qs = np.asarray(inputs["L1_ffn_q_scale"], np.float32)
    ks = np.asarray(inputs["L1_ffn_k_scale"], np.float32)
    e2vv = np.zeros((128, 8), np.float32)
    for l in range(2):
        e2vv[0:64, 4 * l + 0] = qs[l]
        e2vv[64:128, 4 * l + 1] = qs[l]
        e2vv[0:64, 4 * l + 2] = ks[l]
        e2vv[64:128, 4 * l + 3] = ks[l]
    e2c = np.zeros((128, 2), np.float32)
    e2c[0:64, 0] = 1.0
    e2c[64:128, 1] = 1.0
    maps = []
    for core in range(8):
        b, sq = core // 4, core % 4
        xe = np.zeros((4224, 1024), np.float32)
        xe[2176:4224] = x1[b, sq * SEG:(sq + 1) * SEG]
        lo_g = sq * SEG - 2049
        nvalid = min(2049, 2049 + lo_g) if lo_g < 0 else 2049
        if lo_g >= 0:
            xe[127:2176] = x1[b, lo_g:lo_g + 2049]
        elif nvalid > 0:
            xe[2176 - nvalid:2176] = x1[b, 0:nvalid]
        up = np.zeros((128, 48), np.float32)
        for kslot in range(3):
            seg = sq - 3 + kslot
            if seg >= 0:
                up[:, 16 * kslot:16 * (kslot + 1)] = usums[b * 4 + seg]
        cz = host_consts_l2(sq)
        maps.append(dict(
            x1e=xe, usum_pred=up, wqkv=wqkv, wo=wo, w1p=w1p, b1p_ct=b1p_ct,
            w2=w2, b2_ct=b2c, lng_ct=lng, lnb_ct=lnb, wrkv=wrkv,
            brkv_ct=brkv_ct, wo2=wo2, bo_ct=bo_ct, nul_ct=nul, gl_ct=gl,
            t_qcos=cz["t_qcos"], t_qsin=cz["t_qsin"], t_kcos=cz["t_kcos"],
            t_ksin=cz["t_ksin"], mask_std=cz["mask_std"],
            mask_first=cz["mask_first"] if sq == 0 else cz["mask_std"],
            c_e2=e2c, c_e2v=e2vv,
            haloflag=np.full((128, 1), 0.0 if sq == 0 else 1.0, np.float32),
            c_ident=consts["c_ident"]))
    return maps



_CACHE = {}


def kernel(**inputs):
    consts = host_consts_l01()
    if "nc0" not in _CACHE:
        _CACHE["nc0"] = build_launch0()
        _CACHE["nc1"] = build_launch1()
        _CACHE["nc2"] = build_launch2()
    nc0, nc1, nc2 = _CACHE["nc0"], _CACHE["nc1"], _CACHE["nc2"]

    maps0 = host_inputs_launch0(inputs, consts)
    res0 = run_bass_kernel_spmd(nc0, maps0, core_ids=list(range(8)))
    wsums = [np.asarray(res0.results[c]["wsum"]) for c in range(8)]

    maps1 = host_inputs_launch1(inputs, consts, wsums)
    res1 = run_bass_kernel_spmd(nc1, maps1, core_ids=list(range(8)))
    x1 = np.zeros((2, 8192, 1024), np.float32)
    usums = []
    for c in range(8):
        b, sq = c // 4, c % 4
        x1[b, sq * SEG:(sq + 1) * SEG] = np.asarray(res1.results[c]["x1"])
        usums.append(np.asarray(res1.results[c]["usum"]))

    maps2 = host_inputs_launch2(inputs, consts, x1, usums)
    res2 = run_bass_kernel_spmd(nc2, maps2, core_ids=list(range(8)))
    x2 = np.zeros((2, 8192, 1024), np.float32)
    for c in range(8):
        b, sq = c // 4, c % 4
        x2[b, sq * SEG:(sq + 1) * SEG] = np.asarray(res2.results[c]["x2seg"])
    return x2


# version 06
# speedup vs baseline: 1.1570x; 1.1570x over previous
import os
import numpy as np
import ml_dtypes
import concourse.bass as bass
import concourse.mybir as mybir
from concourse.bass_utils import run_bass_kernel_spmd

F32 = mybir.dt.float32
BF16 = mybir.dt.bfloat16
AF = mybir.ActivationFunctionType
ALU = mybir.AluOpType
AX = mybir.AxisListType.X

B, T, D = 2, 8192, 1024
SEG = 2048
NEG = -1.0e38


def _f32(x):
    return np.ascontiguousarray(np.asarray(x), dtype=np.float32)


def ctile(vec):
    return _f32(np.asarray(vec).reshape(8, 128).T)



class Sched:

    def __init__(self, nc):
        self.nc = nc
        self.items = []
        self.src_cnt = {}

    def mark(self):
        return dict(self.src_cnt)

    def cur(self, eng, kind="c"):
        src = (eng, kind)
        return (src, self.src_cnt.get(src, 0))

    def add(self, eng, fn, *, dma=False, after=None, deps=None, stream="0"):
        src = (eng, "d:" + stream) if dma else (eng, "c")
        k = 16 if dma else 1
        if deps is not None:
            req = {}
            for t in deps:
                if t is None:
                    continue
                (s2, v) = t
                if req.get(s2, 0) < v:
                    req[s2] = v
            if not dma and eng != "tensor":
                v = self.src_cnt.get((eng, "c"), 0)
                if v > 0:
                    req[(eng, "c")] = max(req.get((eng, "c"), 0), v)
        else:
            req = dict(after) if after is not None else dict(self.src_cnt)
            if not dma and eng == "tensor":
                req.pop((eng, "c"), None)
        req = {s2: v for s2, v in req.items() if v > 0}
        self.items.append((eng, src, k, fn, req))
        self.src_cnt[src] = self.src_cnt.get(src, 0) + k
        return (src, self.src_cnt[src])

    def emit(self, name=""):
        nc = self.nc
        sems = {s2: nc.alloc_semaphore(f"ch{name}_{s2[0]}_{s2[1].replace(':', '')}")
                for s2 in self.src_cnt}
        used = []
        for it in self.items:
            if it[0] not in used:
                used.append(it[0])
        with nc.Block() as blk:
            for eng_name in used:
                def mk(eng_name):
                    def body(e):
                        waited = {}
                        for (en, src, k, fn, req) in self.items:
                            if en != eng_name:
                                continue
                            for s2, v in req.items():
                                if waited.get(s2, 0) < v:
                                    e.wait_ge(sems[s2], v)
                                    waited[s2] = v
                            fn(e).then_inc(sems[src], k)
                    return body
                getattr(blk, eng_name)(mk(eng_name))
        with nc.Block() as blk2:
            def fin(e):
                for s2, v in self.src_cnt.items():
                    e.wait_ge(sems[s2], v)
            blk2.sync(fin)


class Chain:

    def __init__(self, s, deps=None):
        self.s = s
        self.deps = list(deps or [])
        self.prev = None

    def step(self, eng, fn, *, dma=False, extra=None):
        d = list(self.deps)
        if self.prev is not None:
            d.append(self.prev)
        if extra:
            d.extend(extra)
        self.prev = self.s.add(eng, fn, dma=dma, deps=d)
        self.deps = []
        return self.prev


class Ctx:
    pass



def host_consts_l01():
    return dict(
        c_ident=_f32(np.eye(128)),
        c_trrev=_f32(np.broadcast_to((2047.0 - np.arange(2048))[None, :], (128, 2048))),
        c_iota8=_f32(np.broadcast_to(np.arange(8)[None, :], (128, 8))),
    )



def alloc_front(nc, want_acc):
    c = Ctx()
    c.nc = nc
    c.ident = nc.alloc_sbuf_tensor("ident", [128, 128], F32)
    c.XS = 2064
    c.xnF = nc.alloc_sbuf_tensor("xnF", [128, 8 * c.XS], BF16)
    c.accF = nc.alloc_sbuf_tensor("accF", [128, 8 * SEG], F32) if want_acc else None
    c.xtm = nc.alloc_sbuf_tensor("xtm", [128, 2048], F32)
    c.lnt = nc.alloc_sbuf_tensor("lnt", [128, 1024], F32)
    c.lns = nc.alloc_sbuf_tensor("lns", [128, 1024], F32)
    c.lnm = nc.alloc_sbuf_tensor("lnm", [128, 4], F32)
    c.row1 = nc.alloc_sbuf_tensor("row1", [1, 1024], F32)
    c.rowm = nc.alloc_sbuf_tensor("rowm", [1, 4], F32)
    c.cst = nc.alloc_sbuf_tensor("cst", [128, 4], F32)
    c.pT = nc.alloc_psum_tensor("pT", [128, 128], F32)
    c.pT2 = nc.alloc_psum_tensor("pT2", [128, 128], F32)
    c.pM = nc.alloc_psum_tensor("pM", [128, 512], F32)
    c.pM2 = nc.alloc_psum_tensor("pM2", [128, 512], F32)
    return c


def xn_col(c, j, t):
    return c.XS * j + 1 + t


def emit_consts(s, c):
    for (k, v) in ((0, 1e-5), (1, 1e-9), (2, 1e-6)):
        s.add("vector", lambda e, k=k, v=v: e.memset(c.cst[:, k:k + 1], v))


def emit_ln_tile(s, c, xv, out, mcol):
    s.add("vector", lambda e: e.reduce_sum(mcol[:, 0:1], xv, AX))
    s.add("vector", lambda e: e.tensor_scalar_mul(mcol[:, 0:1], mcol[:, 0:1], 1.0 / 1024.0))
    s.add("vector", lambda e: e.tensor_scalar_sub(c.lnt[0:xv.shape[0], :], xv, mcol[:, 0:1]))
    s.add("scalar", lambda e: e.square(c.lns[0:xv.shape[0], :], c.lnt[0:xv.shape[0], :]))
    s.add("vector", lambda e: e.reduce_sum(mcol[:, 1:2], c.lns[0:xv.shape[0], :], AX))
    s.add("vector", lambda e: e.tensor_scalar_mul(mcol[:, 1:2], mcol[:, 1:2], 1.0 / 1024.0))
    s.add("scalar", lambda e: e.activation(mcol[:, 2:3], mcol[:, 1:2], AF.Ln,
                                        bias=c.cst[0:xv.shape[0], 0:1]))
    s.add("scalar", lambda e: e.activation(mcol[:, 3:4], mcol[:, 2:3], AF.Exp, scale=-0.5))
    s.add("vector", lambda e: e.tensor_scalar_mul(out, c.lnt[0:xv.shape[0], :], mcol[:, 3:4]))


def emit_front_end(s, c, x_ap, want_acc):
    for i in range(16):
        half = (i % 2) * 1024
        xv = c.xtm[:, half:half + 1024]
        s.add("sync", lambda e, i=i, xv=xv: e.dma_start(
            out=xv, in_=x_ap[1 + 128 * i:1 + 128 * (i + 1), :]), dma=True)
        emit_ln_tile(s, c, xv, c.lns[:, :], c.lnm)
        for j in range(8):
            s.add("tensor", lambda e, j=j: e.transpose(
                c.pT[:, :], c.lns[:, 128 * j:128 * (j + 1)], c.ident[:, :]))
            s.add("scalar", lambda e, i=i, j=j: e.copy(
                c.xnF[:, xn_col(c, j, 128 * i): xn_col(c, j, 128 * (i + 1))],
                c.pT[:, :]))
            if want_acc:
                s.add("tensor", lambda e, j=j, xv=xv: e.transpose(
                    c.pT2[:, :], xv[:, 128 * j:128 * (j + 1)], c.ident[:, :]))
                s.add("vector", lambda e, i=i, j=j: e.tensor_copy(
                    c.accF[:, SEG * j + 128 * i: SEG * j + 128 * (i + 1)],
                    c.pT2[:, :]))
    s.add("sync", lambda e: e.dma_start(out=c.row1[:, :], in_=x_ap[0:1, :]), dma=True)
    emit_ln_tile(s, c, c.row1[:, :], c.row1[:, :], c.rowm)
    for j in range(8):
        s.add("tensor", lambda e, j=j: e.transpose(
            c.pT[:, 0:1], c.row1[0:1, 128 * j:128 * (j + 1)], c.ident[0:1, 0:1]))
        s.add("scalar", lambda e, j=j: e.copy(
            c.xnF[:, c.XS * j: c.XS * j + 1], c.pT[:, 0:1]))


def alloc_wkv_small(nc, c):
    c.td = nc.alloc_sbuf_tensor("td", [128, 8], F32)
    c.w8 = nc.alloc_sbuf_tensor("w8", [128, 8], F32)
    c.tf = nc.alloc_sbuf_tensor("tf", [128, 8], F32)
    c.tmk = nc.alloc_sbuf_tensor("tmk", [128, 8], F32)
    c.tmv = nc.alloc_sbuf_tensor("tmv", [128, 8], F32)
    c.tmr = nc.alloc_sbuf_tensor("tmr", [128, 8], F32)
    c.tmk1 = nc.alloc_sbuf_tensor("tmk1", [128, 8], F32)
    c.tmv1 = nc.alloc_sbuf_tensor("tmv1", [128, 8], F32)
    c.tmr1 = nc.alloc_sbuf_tensor("tmr1", [128, 8], F32)
    c.trrev = nc.alloc_sbuf_tensor("trrev", [128, 2048], F32)
    c.wbuf = nc.alloc_sbuf_tensor("wbuf", [128, 8192], BF16)


def emit_wkv_prep(s, c, with_r=True):
    s.add("scalar", lambda e: e.activation(c.w8[:, :], c.td[:, :], AF.Exp))
    s.add("vector", lambda e: e.tensor_scalar_mul(c.w8[:, :], c.w8[:, :], -1.0))
    pairs = [(c.tmk1, c.tmk), (c.tmv1, c.tmv)] + ([(c.tmr1, c.tmr)] if with_r else [])
    for (dst, src) in pairs:
        s.add("vector", lambda e, dst=dst, src=src: e.tensor_scalar(
            dst[:, :], src[:, :], -1.0, 1.0, ALU.mult, ALU.add))


def emit_timemix(s, c, t0, ntok, mix, mix1, out_bf, scratch):
    for j in range(8):
        xn_v = c.xnF[:, xn_col(c, j, t0): xn_col(c, j, t0 + ntok)]
        xx_v = c.xnF[:, xn_col(c, j, t0 - 1): xn_col(c, j, t0 - 1 + ntok)]
        ob = out_bf[:, ntok * j:ntok * (j + 1)]
        s.add("vector", lambda e, xx_v=xx_v, j=j: e.tensor_scalar_mul(
            scratch[:, 0:ntok], xx_v, mix1[:, j:j + 1]))
        s.add("vector", lambda e, xn_v=xn_v, j=j, ob=ob: e.scalar_tensor_tensor(
            ob, xn_v, mix[:, j:j + 1], scratch[:, 0:ntok], ALU.mult, ALU.add))


def emit_wstream(s, c, w_ap, dst=None, cols=1024, nck=8, deps=None, stream="0"):
    dst = dst if dst is not None else c.wbuf
    src3 = w_ap.rearrange("(kc p) n -> p kc n", p=128)
    dst3 = dst[:, 0:nck * cols].rearrange("p (kc n) -> p kc n", n=cols)
    return s.add("gpsimd", lambda e: e.dma_start(out=dst3, in_=src3),
                 dma=True, deps=deps, stream=stream)


def emit_mm8(s, c, psum, out_cb, rhs_fn, n, wbuf=None, wcols=1024, nck=8,
             deps=None):
    wbuf = wbuf if wbuf is not None else c.wbuf

    def fn(e):
        last = None
        for kc in range(nck):
            lhsT = wbuf[:, kc * wcols + out_cb * 128: kc * wcols + out_cb * 128 + 128]
            last = e.matmul(psum[:, 0:n], lhsT, rhs_fn(kc), start=(kc == 0),
                            stop=(kc == nck - 1))
        return last
    return s.add("tensor", fn, deps=deps)



def _mknc():
    rc = os.environ.get("MK_NORC") != "1"
    return bass.Bass("TRN2", target_bir_lowering=False, detect_race_conditions=rc)


def build_launch0():
    nc = _mknc()
    x_in = nc.dram_tensor("x", [2049, 1024], F32, kind="ExternalInput")
    wkT_in = nc.dram_tensor("wkT", [1024, 1024], F32, kind="ExternalInput")
    wvT_in = nc.dram_tensor("wvT", [1024, 1024], F32, kind="ExternalInput")
    td_in = nc.dram_tensor("td_ct", [128, 8], F32, kind="ExternalInput")
    tmk_in = nc.dram_tensor("tmk_ct", [128, 8], F32, kind="ExternalInput")
    tmv_in = nc.dram_tensor("tmv_ct", [128, 8], F32, kind="ExternalInput")
    ident_in = nc.dram_tensor("c_ident", [128, 128], F32, kind="ExternalInput")
    trrev_in = nc.dram_tensor("c_trrev", [128, 2048], F32, kind="ExternalInput")
    wsum_out = nc.dram_tensor("wsum", [128, 24], F32, kind="ExternalOutput")

    c = alloc_front(nc, want_acc=False)
    alloc_wkv_small(nc, c)
    wbuf2 = nc.alloc_sbuf_tensor("wbuf2", [128, 8192], BF16)
    kq = nc.alloc_sbuf_tensor("kq", [128, 2048], F32)
    vq = nc.alloc_sbuf_tensor("vq", [128, 2048], F32)
    sc1 = nc.alloc_sbuf_tensor("sc1", [128, 2048], F32)
    sc2 = nc.alloc_sbuf_tensor("sc2", [128, 2048], F32)
    xkf = nc.alloc_sbuf_tensor("xkf", [128, 8 * 2048], BF16)
    xvf = nc.alloc_sbuf_tensor("xvf", [128, 8 * 2048], BF16)
    out_sb = nc.alloc_sbuf_tensor("out_sb", [128, 24], F32)
    mixsc = nc.alloc_sbuf_tensor("mixsc", [128, 2048], F32)

    s = Sched(nc)
    for (dst, src) in ((c.ident, ident_in), (c.td, td_in), (c.tmk, tmk_in),
                       (c.tmv, tmv_in), (c.trrev, trrev_in)):
        s.add("sync", lambda e, dst=dst, src=src: e.dma_start(
            out=dst[:, :], in_=src.ap()), dma=True)
    emit_consts(s, c)
    emit_front_end(s, c, x_in.ap(), want_acc=False)
    emit_wkv_prep(s, c, with_r=False)
    emit_timemix(s, c, 0, 2048, c.tmk, c.tmk1, xkf, mixsc)
    emit_timemix(s, c, 0, 2048, c.tmv, c.tmv1, xvf, mixsc)
    emit_wstream(s, c, wkT_in.ap(), dst=c.wbuf)
    emit_wstream(s, c, wvT_in.ap(), dst=wbuf2)
    for j in range(8):
        for tc in range(4):
            emit_mm8(s, c, c.pM, j,
                     lambda kc, tc=tc: xkf[:, 2048 * kc + 512 * tc: 2048 * kc + 512 * (tc + 1)],
                     512, wbuf=c.wbuf)
            s.add("vector", lambda e, tc=tc: e.tensor_copy(
                kq[:, 512 * tc:512 * (tc + 1)], c.pM[:, :]))
            emit_mm8(s, c, c.pM2, j,
                     lambda kc, tc=tc: xvf[:, 2048 * kc + 512 * tc: 2048 * kc + 512 * (tc + 1)],
                     512, wbuf=wbuf2)
            s.add("vector", lambda e, tc=tc: e.tensor_copy(
                vq[:, 512 * tc:512 * (tc + 1)], c.pM2[:, :]))
        s.add("vector", lambda e, j=j: e.scalar_tensor_tensor(
            sc1[:, :], c.trrev[:, :], c.w8[:, j:j + 1], kq[:, :], ALU.mult, ALU.add))
        s.add("vector", lambda e, j=j: e.reduce_max(out_sb[:, j:j + 1], sc1[:, :], AX))
        s.add("vector", lambda e, j=j: e.tensor_scalar_sub(
            sc1[:, :], sc1[:, :], out_sb[:, j:j + 1]))
        s.add("scalar", lambda e: e.activation(sc2[:, :], sc1[:, :], AF.Exp))
        s.add("vector", lambda e, j=j: e.reduce_sum(
            out_sb[:, 16 + j:16 + j + 1], sc2[:, :], AX))
        s.add("vector", lambda e: e.tensor_mul(sc1[:, :], sc2[:, :], vq[:, :]))
        s.add("vector", lambda e, j=j: e.reduce_sum(
            out_sb[:, 8 + j:8 + j + 1], sc1[:, :], AX))
    s.add("sync", lambda e: e.dma_start(out=wsum_out.ap(), in_=out_sb[:, :]), dma=True)
    s.emit("L0")
    return nc


def host_inputs_launch0(inputs, consts):
    x = np.asarray(inputs["L0_attn_x"], np.float32)
    wkT = _f32(np.asarray(inputs["L0_attn_Wk"]).T)
    wvT = _f32(np.asarray(inputs["L0_attn_Wv"]).T)
    td = ctile(inputs["L0_attn_time_decay"])
    tmk = ctile(np.asarray(inputs["L0_attn_time_mix_k"]).reshape(-1))
    tmv = ctile(np.asarray(inputs["L0_attn_time_mix_v"]).reshape(-1))
    maps = []
    for core in range(8):
        b, sq = core // 4, core % 4
        xe = np.zeros((2049, 1024), np.float32)
        xe[1:] = x[b, sq * SEG:(sq + 1) * SEG]
        if sq > 0:
            xe[0] = x[b, sq * SEG - 1]
        maps.append(dict(x=xe, wkT=wkT, wvT=wvT, td_ct=td, tmk_ct=tmk,
                         tmv_ct=tmv, c_ident=consts["c_ident"],
                         c_trrev=consts["c_trrev"]))
    return maps



CH = 512
NQ = 4


def build_launch1(debug=False):
    nc = _mknc()
    inp = {}
    def I(name, shape):
        inp[name] = nc.dram_tensor(name, shape, F32, kind="ExternalInput")
        return inp[name]
    x_in = I("x", [2049, 1024])
    wkT_in = I("wkT", [1024, 1024]); wvT_in = I("wvT", [1024, 1024])
    wrT_in = I("wrT", [1024, 1024]); woT_in = I("woT", [1024, 1024])
    td_in = I("td_ct", [128, 8]); tf_in = I("tf_ct", [128, 8])
    tmk_in = I("tmk_ct", [128, 8]); tmv_in = I("tmv_ct", [128, 8])
    tmr_in = I("tmr_ct", [128, 8])
    carry_in = I("carry", [128, 72])
    wg_in = I("wg", [1024, 8])
    w1_in = I("w1", [8, 1024, 4096]); b1_in = I("b1ct", [128, 288])
    w2_in = I("w2", [8, 4096, 1024])
    b2_in = I("b2", [8, 1024]); bs2_in = I("bs2", [1, 1024])
    ws1_in = I("ws1", [1024, 4096]); ws2_in = I("ws2", [4096, 1024])
    wrkv_in = I("wrkv", [1024, 3072]); brkv_in = I("brkv_ct", [128, 24])
    nul_in = I("nul_ct", [128, 8])
    ident_in = I("c_ident", [128, 128]); trrev_in = I("c_trrev", [128, 2048])
    iota_in = I("c_iota8", [128, 8])
    sel_in = nc.dram_tensor("c_sel", [8, 1024], BF16, kind="ExternalInput")
    x1_out = nc.dram_tensor("x1", [2048, 1024], F32, kind="ExternalOutput")
    usum_out = nc.dram_tensor("usum", [128, 16], F32, kind="ExternalOutput")
    x1_acc = nc.dram_tensor("x1_acc", [128, 16384], F32, kind="Internal")
    if debug:
        dbg_x = nc.dram_tensor("dbg_x", [128, 16384], F32, kind="ExternalOutput")
        dbg_p2 = nc.dram_tensor("dbg_p2", [4, 128, 8192], F32, kind="ExternalOutput")
        dbg_kvr = nc.dram_tensor("dbg_kvr", [4, 4, 128, 4096], F32, kind="ExternalOutput")
        dbg_st = nc.dram_tensor("dbg_st", [4, 128, 24], F32, kind="ExternalOutput")
    act_hbm = nc.dram_tensor("act_hbm", [9, 8, 4, 128, 2048], BF16, kind="Internal")

    AR = nc.alloc_sbuf_tensor("AR", [128, 20480], F32)
    ARB = nc.alloc_sbuf_tensor("ARB", [128, 41088], BF16)
    def Fv(a, n):
        return AR[:, a:a + n]
    def Bv(a, n):
        return ARB[:, a:a + n]

    c = Ctx(); c.nc = nc; c.XS = 2064
    c.ident = nc.alloc_sbuf_tensor("ident", [128, 128], F32)
    c.cst = nc.alloc_sbuf_tensor("cst", [128, 4], F32)
    c.xnF = ARB[:, 0:16512]
    c.xtm = AR[:, 0:2048]
    c.lnt = AR[:, 2048:3072]
    c.lns = AR[:, 3072:4096]
    c.lnsB = AR[:, 12288:13312]
    c.lnm = nc.alloc_sbuf_tensor("lnm", [128, 8], F32)
    c.rowm = nc.alloc_sbuf_tensor("rowm", [1, 4], F32)
    c.row1 = AR[0:1, 4096:5120]
    c.pT = nc.alloc_psum_tensor("pT", [128, 128], F32)
    c.pT2 = nc.alloc_psum_tensor("pT2", [128, 128], F32)
    c.pM = nc.alloc_psum_tensor("pM", [128, 512], F32)
    c.pM2 = nc.alloc_psum_tensor("pM2", [128, 512], F32)
    c.pCW = nc.alloc_psum_tensor("pCW", [128, 512], F32)
    c.td = nc.alloc_sbuf_tensor("td", [128, 8], F32)
    c.w8 = nc.alloc_sbuf_tensor("w8", [128, 8], F32)
    c.tf = nc.alloc_sbuf_tensor("tf", [128, 8], F32)
    c.tmk = nc.alloc_sbuf_tensor("tmk", [128, 8], F32)
    c.tmv = nc.alloc_sbuf_tensor("tmv", [128, 8], F32)
    c.tmr = nc.alloc_sbuf_tensor("tmr", [128, 8], F32)
    c.tmk1 = nc.alloc_sbuf_tensor("tmk1", [128, 8], F32)
    c.tmv1 = nc.alloc_sbuf_tensor("tmv1", [128, 8], F32)
    c.tmr1 = nc.alloc_sbuf_tensor("tmr1", [128, 8], F32)
    c.trrev = AR[:, 17408:19456]
    c.wbuf = ARB[:, 16512:24704]

    STG = Fv(4096, 1024)
    STG2 = Fv(13312, 1024)
    KQ = Fv(5120, 4096)
    VQ = Fv(9216, 4096)
    PE_ = Fv(13312, 513); SN_ = Fv(13825, 513); SD_ = Fv(14338, 513)
    T1 = Fv(14851, 512); T2 = Fv(15363, 512); T3 = Fv(15875, 512)
    T4 = Fv(16387, 512); AL = Fv(16899, 512); CD = Fv(17411, 512)
    CN = Fv(17923, 512); E1B = Fv(18435, 512); E2B = Fv(18947, 512)
    WREP = Fv(19459, 512)
    XQ = Bv(24704, 4096)
    RQ = Bv(28800, 4096)
    RWQ = Bv(32896, 4096)
    st = nc.alloc_sbuf_tensor("st", [128, 24], F32)
    w2048 = nc.alloc_sbuf_tensor("w2048", [128, 8], F32)
    fold8 = nc.alloc_sbuf_tensor("fold8", [128, 40], F32)
    carry_sb = nc.alloc_sbuf_tensor("carry_sb", [128, 72], F32)
    lgt = AR[0:8, 9216:11264]
    cw_tm = nc.alloc_sbuf_tensor("cw_tm", [128, 128], F32)
    cm = nc.alloc_sbuf_tensor("cm", [128, 8], F32)
    ptile = nc.alloc_sbuf_tensor("ptile", [128, 8], F32)
    pt2_ = nc.alloc_sbuf_tensor("pt2_", [128, 8], F32)
    eqt = nc.alloc_sbuf_tensor("eqt", [128, 8], F32)
    eq2t = nc.alloc_sbuf_tensor("eq2t", [128, 8], F32)
    p2t = nc.alloc_sbuf_tensor("p2t", [128, 8], F32)
    iota8 = nc.alloc_sbuf_tensor("iota8", [128, 8], F32)
    wgb = nc.alloc_sbuf_tensor("wgb", [128, 64], BF16)
    cwT8 = nc.alloc_sbuf_tensor("cwT8", [8, 2048], BF16)
    selb = nc.alloc_sbuf_tensor("selb", [8, 1024], BF16)
    cwB4 = nc.alloc_sbuf_tensor("cwB4", [128, 2048], BF16)
    ones_row_bf = nc.alloc_sbuf_tensor("ones_row_bf", [1, 128], BF16)
    ones_col = nc.alloc_sbuf_tensor("ones_col", [128, 1], F32)
    ones_rowf = nc.alloc_sbuf_tensor("ones_rowf", [1, 128], F32)
    b2b = nc.alloc_sbuf_tensor("b2b", [8, 1024], BF16)
    bs2b = nc.alloc_sbuf_tensor("bs2b", [1, 1024], BF16)
    row01 = nc.alloc_sbuf_tensor("row01", [1, 512], BF16)
    b1sb = nc.alloc_sbuf_tensor("b1sb", [128, 288], F32)
    nu8 = nc.alloc_sbuf_tensor("nu8", [128, 8], F32)
    nun8 = nc.alloc_sbuf_tensor("nun8", [128, 8], F32)
    usum_sb = nc.alloc_sbuf_tensor("usum_sb", [128, 16], F32)
    srow = nc.alloc_sbuf_tensor("srow", [1, 1536], F32)
    brkv = nc.alloc_sbuf_tensor("brkv", [128, 24], F32)

    s = Sched(nc)
    for (dst, src) in ((c.ident, ident_in), (c.td, td_in), (c.tf, tf_in),
                       (c.tmk, tmk_in), (c.tmv, tmv_in), (c.tmr, tmr_in),
                       (carry_sb, carry_in), (iota8, iota_in)):
        s.add("sync", lambda e, dst=dst, src=src: e.dma_start(
            out=dst[:, :], in_=src.ap()), dma=True, stream="sm")
    s.add("gpsimd", lambda e: e.dma_start(out=b2b[:, :], in_=b2_in.ap()), dma=True, stream="sm")
    s.add("sync", lambda e: e.dma_start(out=b1sb[:, :], in_=b1_in.ap()), dma=True, stream="sm")
    s.add("gpsimd", lambda e: e.dma_start(out=bs2b[:, :], in_=bs2_in.ap()), dma=True, stream="sm")
    s.add("sync", lambda e: e.dma_start(out=selb[:, :], in_=sel_in.ap()), dma=True, stream="sm")
    emit_consts(s, c)
    s.add("vector", lambda e: e.memset(ones_row_bf[:, :], 1.0))
    s.add("vector", lambda e: e.memset(ones_col[:, :], 1.0))
    s.add("vector", lambda e: e.memset(ones_rowf[:, :], 1.0))
    s.add("vector", lambda e: e.memset(row01[:, :], 0.1))
    tok_smalls_s = s.cur("sync", "d:sm")
    tok_smalls_g = s.cur("gpsimd", "d:sm")
    tok_smalls_v = s.cur("vector")

    x1a = x1_acc.ap().rearrange("p (j t) -> p j t", t=2048)
    ld_tok = [None, None]
    tp_done = [None, None]
    stg_dma = [None, None]
    init_toks = []
    for i in range(16):
        par = i % 2
        xv = c.xtm[:, par * 1024:par * 1024 + 1024]
        lnsv = c.lns if par == 0 else c.lnsB
        ld_tok[par] = s.add("sync", lambda e, i=i, xv=xv: e.dma_start(
            out=xv, in_=x_in.ap()[1 + 128 * i:1 + 128 * (i + 1), :]),
            dma=True, deps=[tp_done[par], tok_smalls_s], stream=f"ld{par}")
        ch = Chain(s, deps=[ld_tok[par], tp_done[par], tok_smalls_v])
        mc = c.lnm[:, 4 * par:4 * par + 4]
        ch.step("vector", lambda e, xv=xv, mc=mc: e.reduce_sum(mc[:, 0:1], xv, AX))
        ch.step("vector", lambda e, mc=mc: e.tensor_scalar_mul(mc[:, 0:1], mc[:, 0:1], 1.0 / 1024.0))
        ch.step("vector", lambda e, xv=xv, mc=mc: e.tensor_scalar_sub(c.lnt[:, :], xv, mc[:, 0:1]))
        ch.step("scalar", lambda e: e.square(KQ[:, 0:1024], c.lnt[:, :]))
        ch.step("vector", lambda e, mc=mc: e.reduce_sum(mc[:, 1:2], KQ[:, 0:1024], AX))
        ch.step("vector", lambda e, mc=mc: e.tensor_scalar_mul(mc[:, 1:2], mc[:, 1:2], 1.0 / 1024.0))
        ch.step("scalar", lambda e, mc=mc: e.activation(mc[:, 2:3], mc[:, 1:2], AF.Ln,
                                                        bias=c.cst[:, 0:1]))
        ch.step("scalar", lambda e, mc=mc: e.activation(mc[:, 3:4], mc[:, 2:3], AF.Exp, scale=-0.5))
        ln_tok = ch.step("vector", lambda e, lnsv=lnsv, mc=mc: e.tensor_scalar_mul(
            lnsv[:, :], c.lnt[:, :], mc[:, 3:4]))
        prev_s = None
        prev_v = None
        for j in range(8):
            tt = s.add("tensor", lambda e, j=j, lnsv=lnsv: e.transpose(
                c.pT[:, :], lnsv[:, 128 * j:128 * (j + 1)], c.ident[:, :]),
                deps=[ln_tok, prev_s])
            prev_s = s.add("scalar", lambda e, i=i, j=j: e.copy(
                c.xnF[:, xn_col(c, j, 128 * i): xn_col(c, j, 128 * (i + 1))],
                c.pT[:, :]), deps=[tt])
            tt2 = s.add("tensor", lambda e, j=j, xv=xv: e.transpose(
                c.pT2[:, :], xv[:, 128 * j:128 * (j + 1)], c.ident[:, :]),
                deps=[ld_tok[par], prev_v])
            prev_v = s.add("vector", lambda e, j=j, par=par: e.tensor_copy(
                (STG if par == 0 else STG2)[:, 128 * j:128 * (j + 1)], c.pT2[:, :]),
                deps=[tt2, stg_dma[par]])
        tp_done[par] = s.add("vector", lambda e: e.engine_nop(), deps=[prev_s, prev_v])
        stg_dma[par] = s.add("sync", lambda e, i=i, par=par: e.dma_start(
            out=x1a[:, :, 128 * i:128 * (i + 1)],
            in_=(STG if par == 0 else STG2).rearrange("p (j t) -> p j t", t=128)),
            dma=True, deps=[prev_v], stream=f"st{par}")
        init_toks.append(stg_dma[par])
    hl = s.add("sync", lambda e: e.dma_start(out=c.row1[:, :], in_=x_in.ap()[0:1, :]),
               dma=True, deps=[tp_done[0], tp_done[1], stg_dma[0], stg_dma[1],
                               tok_smalls_s], stream="sm")
    ch = Chain(s, deps=[hl, tok_smalls_v])
    ch.step("vector", lambda e: e.reduce_sum(c.rowm[:, 0:1], c.row1[:, :], AX))
    ch.step("vector", lambda e: e.tensor_scalar_mul(c.rowm[:, 0:1], c.rowm[:, 0:1], 1.0 / 1024.0))
    ch.step("vector", lambda e: e.tensor_scalar_sub(c.row1[:, :], c.row1[:, :], c.rowm[:, 0:1]))
    ch.step("scalar", lambda e: e.square(c.lnt[0:1, :], c.row1[:, :]))
    ch.step("vector", lambda e: e.reduce_sum(c.rowm[:, 1:2], c.lnt[0:1, :], AX))
    ch.step("vector", lambda e: e.tensor_scalar_mul(c.rowm[:, 1:2], c.rowm[:, 1:2], 1.0 / 1024.0))
    ch.step("scalar", lambda e: e.activation(c.rowm[:, 2:3], c.rowm[:, 1:2], AF.Ln,
                                             bias=c.cst[0:1, 0:1]))
    ch.step("scalar", lambda e: e.activation(c.rowm[:, 3:4], c.rowm[:, 2:3], AF.Exp, scale=-0.5))
    hn = ch.step("vector", lambda e: e.tensor_scalar_mul(c.row1[:, :], c.row1[:, :], c.rowm[:, 3:4]))
    pvh = None
    for j in range(8):
        tt = s.add("tensor", lambda e, j=j: e.transpose(
            c.pT[:, 0:1], c.row1[0:1, 128 * j:128 * (j + 1)], c.ident[0:1, 0:1]),
            deps=[hn, pvh])
        pvh = s.add("scalar", lambda e, j=j: e.copy(
            c.xnF[:, c.XS * j: c.XS * j + 1], c.pT[:, 0:1]), deps=[tt])
    xn_ready = pvh

    emit_wkv_prep(s, c, with_r=True)
    s.add("vector", lambda e: e.tensor_scalar_mul(w2048[:, :], c.w8[:, :], 2048.0))
    s.add("vector", lambda e: e.memset(st[:, 0:8], NEG))
    s.add("vector", lambda e: e.memset(st[:, 8:24], 0.0))
    TM = fold8[:, 0:8]; MM_ = fold8[:, 8:16]; M1 = fold8[:, 16:24]
    M2 = fold8[:, 24:32]; M3 = fold8[:, 32:40]
    for kslot in range(3):
        sp = carry_sb[:, 24 * kslot + 0:24 * kslot + 8]
        sa = carry_sb[:, 24 * kslot + 8:24 * kslot + 16]
        sb_ = carry_sb[:, 24 * kslot + 16:24 * kslot + 24]
        s.add("vector", lambda e: e.tensor_add(TM, st[:, 0:8], w2048[:, :]))
        s.add("vector", lambda e, sp=sp: e.tensor_max(MM_, TM, sp))
        s.add("vector", lambda e: e.tensor_sub(M1, TM, MM_))
        s.add("scalar", lambda e: e.activation(M1, M1, AF.Exp))
        s.add("vector", lambda e, sp=sp: e.tensor_sub(M2, sp, MM_))
        s.add("scalar", lambda e: e.activation(M2, M2, AF.Exp))
        s.add("vector", lambda e: e.tensor_mul(M3, M1, st[:, 8:16]))
        s.add("vector", lambda e, sa=sa: e.tensor_mul(TM, M2, sa))
        s.add("vector", lambda e: e.tensor_add(st[:, 8:16], M3, TM))
        s.add("vector", lambda e: e.tensor_mul(M3, M1, st[:, 16:24]))
        s.add("vector", lambda e, sb_=sb_: e.tensor_mul(TM, M2, sb_))
        s.add("vector", lambda e: e.tensor_add(st[:, 16:24], M3, TM))
        s.add("vector", lambda e: e.tensor_copy(st[:, 0:8], MM_))
    tok_prep_v = s.cur("vector")
    tok_prep_a = s.cur("scalar")

    wkv_w_tok = None
    xq_mm_tok = None
    kq_ev = {}
    vq_ev = {}
    rq_ev = {}
    rwq_toks = [None] * 8
    wo_mm_last = None
    psA = [c.pM, c.pM2]
    ps_free = {0: None, 1: None}
    acc_last = {}
    for h in range(8):
        for tc in range(4):
            acc_last[(h, tc)] = [init_toks[4 * tc + 2], init_toks[4 * tc + 3]]
    acc_last[("stg", 0)] = stg_dma[0]
    acc_last[("stg", 1)] = stg_dma[1]
    if debug:
        dbgx_tok = s.add("sync", lambda e: e.dma_start(
            out=dbg_x.ap(), in_=x1_acc.ap()), dma=True,
            deps=init_toks, stream="dbgx")
        for h in range(8):
            for tc in range(4):
                acc_last[(h, tc)] = acc_last[(h, tc)] + [dbgx_tok]
    mmi = 0
    dbg_toks = []
    for q in range(4):
        wk_dma = emit_wstream(s, c, wkT_in.ap(),
                              deps=[wkv_w_tok, tok_smalls_g] + dbg_toks, stream="wst")
        tmx = None
        for j in range(8):
            xn_v = c.xnF[:, xn_col(c, j, CH * q): xn_col(c, j, CH * q + CH)]
            xx_v = c.xnF[:, xn_col(c, j, CH * q - 1): xn_col(c, j, CH * q - 1 + CH)]
            t1 = s.add("vector", lambda e, xx_v=xx_v, j=j: e.tensor_scalar_mul(
                c.lnt[:, 0:CH], xx_v, c.tmk1[:, j:j + 1]),
                deps=[xn_ready, tok_prep_v, xq_mm_tok, tmx])
            tmx = s.add("vector", lambda e, xn_v=xn_v, j=j: e.scalar_tensor_tensor(
                XQ[:, CH * j:CH * (j + 1)], xn_v, c.tmk[:, j:j + 1], c.lnt[:, 0:CH],
                ALU.mult, ALU.add), deps=[])
        xq_tok = tmx
        for j in range(8):
            ps = psA[mmi % 2]
            mm = emit_mm8(s, c, ps, j, lambda kc: XQ[:, CH * kc:CH * (kc + 1)], CH,
                          deps=[wk_dma, xq_tok, ps_free[mmi % 2]])
            kq_ev[j] = s.add("vector", lambda e, j=j, ps=ps: e.tensor_copy(
                KQ[:, CH * j:CH * (j + 1)], ps[:, 0:CH]), deps=[mm])
            ps_free[mmi % 2] = kq_ev[j]
            mmi += 1
        wkv_w_tok = mm
        wv_dma = emit_wstream(s, c, wvT_in.ap(), deps=[wkv_w_tok], stream="wst")
        tmx = None
        for j in range(8):
            xn_v = c.xnF[:, xn_col(c, j, CH * q): xn_col(c, j, CH * q + CH)]
            xx_v = c.xnF[:, xn_col(c, j, CH * q - 1): xn_col(c, j, CH * q - 1 + CH)]
            t1 = s.add("vector", lambda e, xx_v=xx_v, j=j: e.tensor_scalar_mul(
                c.lnt[:, 0:CH], xx_v, c.tmv1[:, j:j + 1]), deps=[mm, tmx])
            tmx = s.add("vector", lambda e, xn_v=xn_v, j=j: e.scalar_tensor_tensor(
                XQ[:, CH * j:CH * (j + 1)], xn_v, c.tmv[:, j:j + 1], c.lnt[:, 0:CH],
                ALU.mult, ALU.add), deps=[])
        xq_tok = tmx
        for j in range(8):
            ps = psA[mmi % 2]
            mm = emit_mm8(s, c, ps, j, lambda kc: XQ[:, CH * kc:CH * (kc + 1)], CH,
                          deps=[wv_dma, xq_tok, ps_free[mmi % 2]])
            vq_ev[j] = s.add("vector", lambda e, j=j, ps=ps: e.tensor_copy(
                VQ[:, CH * j:CH * (j + 1)], ps[:, 0:CH]), deps=[mm])
            ps_free[mmi % 2] = vq_ev[j]
            mmi += 1
        wkv_w_tok = mm
        wr_dma = emit_wstream(s, c, wrT_in.ap(), deps=[wkv_w_tok], stream="wst")
        tmx = None
        for j in range(8):
            xn_v = c.xnF[:, xn_col(c, j, CH * q): xn_col(c, j, CH * q + CH)]
            xx_v = c.xnF[:, xn_col(c, j, CH * q - 1): xn_col(c, j, CH * q - 1 + CH)]
            t1 = s.add("vector", lambda e, xx_v=xx_v, j=j: e.tensor_scalar_mul(
                c.lnt[:, 0:CH], xx_v, c.tmr1[:, j:j + 1]), deps=[mm, tmx])
            tmx = s.add("vector", lambda e, xn_v=xn_v, j=j: e.scalar_tensor_tensor(
                XQ[:, CH * j:CH * (j + 1)], xn_v, c.tmr[:, j:j + 1], c.lnt[:, 0:CH],
                ALU.mult, ALU.add), deps=[])
        xq_tok = tmx
        for j in range(8):
            ps = psA[mmi % 2]
            mm = emit_mm8(s, c, ps, j, lambda kc: XQ[:, CH * kc:CH * (kc + 1)], CH,
                          deps=[wr_dma, xq_tok, ps_free[mmi % 2], rwq_toks[j]])
            rq_ev[j] = s.add("scalar", lambda e, j=j, ps=ps: e.activation(
                RQ[:, CH * j:CH * (j + 1)], ps[:, 0:CH], AF.Sigmoid), deps=[mm])
            ps_free[mmi % 2] = rq_ev[j]
            mmi += 1
        wkv_w_tok = mm
        st_fin = {}
        for j in range(8):
            kt = KQ[:, CH * j:CH * (j + 1)]
            vt = VQ[:, CH * j:CH * (j + 1)]
            ch = Chain(s, deps=[kq_ev[j], vq_ev[j], rq_ev[j], tok_prep_a,
                                acc_last[("stg", 0)], acc_last[("stg", 1)]])
            ch.step("vector", lambda e, j=j: e.tensor_copy(PE_[:, 0:1], st[:, j:j + 1]))
            ch.step("vector", lambda e: e.memset(WREP, 1.0))
            ch.step("vector", lambda e, j=j: e.tensor_scalar_mul(WREP, WREP, c.w8[:, j:j + 1]))
            ch.step("vector", lambda e, j=j, kt=kt: e.tensor_tensor_scan(
                PE_[:, 1:CH + 1], WREP, kt, st[:, j:j + 1], ALU.add, ALU.max))
            ch.step("vector", lambda e: e.tensor_sub(T1, PE_[:, 0:CH], PE_[:, 1:CH + 1]))
            ch.step("scalar", lambda e, j=j: e.activation(AL, T1, AF.Exp, bias=c.w8[:, j:j + 1]))
            ch.step("vector", lambda e, kt=kt: e.tensor_sub(T2, kt, PE_[:, 1:CH + 1]))
            ch.step("scalar", lambda e: e.activation(CD, T2, AF.Exp))
            ch.step("vector", lambda e, vt=vt: e.tensor_mul(CN, CD, vt))
            ch.step("vector", lambda e, j=j: e.tensor_copy(SN_[:, 0:1], st[:, 8 + j:9 + j]))
            ch.step("vector", lambda e, j=j: e.tensor_tensor_scan(
                SN_[:, 1:CH + 1], AL, CN, st[:, 8 + j:9 + j], ALU.mult, ALU.add))
            ch.step("vector", lambda e, j=j: e.tensor_copy(SD_[:, 0:1], st[:, 16 + j:17 + j]))
            ch.step("vector", lambda e, j=j: e.tensor_tensor_scan(
                SD_[:, 1:CH + 1], AL, CD, st[:, 16 + j:17 + j], ALU.mult, ALU.add))
            ch.step("vector", lambda e, j=j, kt=kt: e.tensor_scalar_add(T1, kt, c.tf[:, j:j + 1]))
            ch.step("vector", lambda e: e.tensor_max(T2, PE_[:, 0:CH], T1))
            ch.step("vector", lambda e: e.tensor_sub(T3, PE_[:, 0:CH], T2))
            ch.step("scalar", lambda e: e.activation(E1B, T3, AF.Exp))
            ch.step("vector", lambda e: e.tensor_sub(T3, T1, T2))
            ch.step("scalar", lambda e: e.activation(E2B, T3, AF.Exp))
            ch.step("vector", lambda e: e.tensor_mul(T4, E1B, SN_[:, 0:CH]))
            ch.step("vector", lambda e, vt=vt: e.tensor_mul(T3, E2B, vt))
            ch.step("vector", lambda e: e.tensor_add(T4, T4, T3))
            ch.step("vector", lambda e: e.tensor_mul(T3, E1B, SD_[:, 0:CH]))
            ch.step("vector", lambda e: e.tensor_add(T3, T3, E2B))
            ch.step("scalar", lambda e: e.activation(T1, T3, AF.Ln))
            ch.step("scalar", lambda e: e.activation(T2, T1, AF.Exp, scale=-1.0))
            ch.step("vector", lambda e: e.tensor_mul(T4, T4, T2))
            rwq_toks[j] = ch.step("vector", lambda e, j=j: e.tensor_mul(
                RWQ[:, CH * j:CH * (j + 1)], T4, RQ[:, CH * j:CH * (j + 1)]),
                extra=[wo_mm_last])
            ch.step("vector", lambda e, j=j: e.tensor_copy(st[:, j:j + 1], PE_[:, CH:CH + 1]))
            ch.step("vector", lambda e, j=j: e.tensor_copy(st[:, 8 + j:9 + j], SN_[:, CH:CH + 1]))
            st_fin[j] = ch.step("vector", lambda e, j=j: e.tensor_copy(st[:, 16 + j:17 + j], SD_[:, CH:CH + 1]))
        if debug:
            last_ch = st_fin[7]
            dbg_toks = [
                s.add("sync", lambda e, q=q: e.dma_start(
                    out=dbg_kvr.ap()[q, 0], in_=KQ[:, :]), dma=True,
                    deps=[kq_ev[7], last_ch], stream="dbg0"),
                s.add("sync", lambda e, q=q: e.dma_start(
                    out=dbg_kvr.ap()[q, 1], in_=VQ[:, :]), dma=True,
                    deps=[vq_ev[7], last_ch], stream="dbg1"),
                s.add("gpsimd", lambda e, q=q: e.dma_start(
                    out=dbg_kvr.ap()[q, 2], in_=RQ[:, :]), dma=True,
                    deps=[rq_ev[7], last_ch], stream="dbg2"),
                s.add("gpsimd", lambda e, q=q: e.dma_start(
                    out=dbg_kvr.ap()[q, 3], in_=RWQ[:, :]), dma=True,
                    deps=[last_ch], stream="dbg3"),
                s.add("sync", lambda e, q=q: e.dma_start(
                    out=dbg_st.ap()[q], in_=st[:, :]), dma=True,
                    deps=[last_ch], stream="dbg4"),
            ]
        wo_dma = emit_wstream(s, c, woT_in.ap(), deps=[wkv_w_tok], stream="wst")
        for cout in range(8):
            ps = psA[mmi % 2]
            mm = emit_mm8(s, c, ps, cout, lambda kc: RWQ[:, CH * kc:CH * (kc + 1)], CH,
                          deps=[wo_dma, rwq_toks[7], ps_free[mmi % 2]])
            ev = s.add("vector", lambda e, ps=ps, cout=cout: e.tensor_copy(
                STG[:, 0:CH] if cout % 2 == 0 else STG2[:, 0:CH], ps[:, 0:CH]),
                deps=[mm, acc_last.get(("stg", cout % 2))])
            dmat = s.add("gpsimd", lambda e, cout=cout, q=q: e.dma_start(
                out=x1_acc.ap()[:, 2048 * cout + CH * q: 2048 * cout + CH * (q + 1)],
                in_=(STG if cout % 2 == 0 else STG2)[:, 0:CH], accum_op=ALU.add),
                dma=True, deps=[ev] + acc_last[(cout, q)], stream=f"st{cout % 2}")
            acc_last[(cout, q)] = [dmat]
            acc_last[("stg", cout % 2)] = dmat
            ps_free[mmi % 2] = ev
            mmi += 1
            wo_mm_last = mm
            wkv_w_tok = mm

    wg_dma = s.add("gpsimd", lambda e: e.dma_start(
        out=wgb[:, 0:64].rearrange("p (kc n) -> p kc n", n=8),
        in_=wg_in.ap().rearrange("(kc p) n -> p kc n", p=128)), dma=True,
        deps=[tok_smalls_g], stream="sm")
    lg_toks = []
    for tc in range(4):
        def fn(e, tc=tc):
            last = None
            for kc in range(8):
                last = e.matmul(c.pCW[0:8, 0:512], wgb[:, 8 * kc:8 * (kc + 1)],
                                c.xnF[:, xn_col(c, kc, 512 * tc): xn_col(c, kc, 512 * (tc + 1))],
                                start=(kc == 0), stop=(kc == 7))
            return last
        prev_cp = lg_toks[-1] if lg_toks else None
        mm = s.add("tensor", fn, deps=[wg_dma, xn_ready, prev_cp, wo_mm_last] + dbg_toks)
        lg_toks.append(s.add("vector", lambda e, tc=tc: e.tensor_copy(
            lgt[:, 512 * tc:512 * (tc + 1)], c.pCW[0:8, 0:512]), deps=[mm]))
    cw_toks = []
    prev_t = None
    for i in range(16):
        ch = Chain(s, deps=[lg_toks[i // 4], prev_t])
        tt = ch.step("tensor", lambda e, i=i: e.transpose(
            c.pT[:, 0:8], lgt[:, 128 * i:128 * (i + 1)], c.ident[0:8, 0:8]))
        ch.step("vector", lambda e: e.tensor_copy(ptile[:, :], c.pT[:, 0:8]))
        ch.step("vector", lambda e: e.reduce_max(cm[:, 0:1], ptile[:, :], AX))
        ch.step("vector", lambda e: e.tensor_scalar_mul(cm[:, 1:2], cm[:, 0:1], -1.0))
        ch.step("scalar", lambda e: e.activation(pt2_[:, :], ptile[:, :], AF.Exp, bias=cm[:, 1:2]))
        ch.step("vector", lambda e: e.reduce_sum(cm[:, 2:3], pt2_[:, :], AX))
        ch.step("scalar", lambda e: e.activation(cm[:, 3:4], cm[:, 2:3], AF.Ln))
        ch.step("scalar", lambda e: e.activation(cm[:, 3:4], cm[:, 3:4], AF.Exp, scale=-1.0))
        ch.step("vector", lambda e: e.tensor_scalar_mul(pt2_[:, :], pt2_[:, :], cm[:, 3:4]))
        ch.step("vector", lambda e: e.reduce_max(cm[:, 0:1], pt2_[:, :], AX))
        ch.step("vector", lambda e: e.tensor_single_scalar(eqt[:, :], pt2_[:, :], cm[:, 0:1], ALU.is_equal))
        ch.step("vector", lambda e: e.scalar_tensor_tensor(p2t[:, :], eqt[:, :], -1.0e30, pt2_[:, :], ALU.mult, ALU.add))
        ch.step("vector", lambda e: e.reduce_max(cm[:, 2:3], p2t[:, :], AX))
        ch.step("vector", lambda e: e.tensor_single_scalar(eq2t[:, :], pt2_[:, :], cm[:, 2:3], ALU.is_equal))
        ch.step("vector", lambda e: e.tensor_add(eqt[:, :], eqt[:, :], eq2t[:, :]))
        ch.step("vector", lambda e: e.tensor_add(cm[:, 5:6], cm[:, 0:1], cm[:, 2:3]))
        ch.step("scalar", lambda e: e.activation(cm[:, 6:7], cm[:, 5:6], AF.Ln, bias=c.cst[:, 1:2]))
        ch.step("scalar", lambda e: e.activation(cm[:, 6:7], cm[:, 6:7], AF.Exp, scale=-1.0))
        ch.step("vector", lambda e: e.tensor_mul(ptile[:, :], pt2_[:, :], eqt[:, :]))
        ch.step("vector", lambda e: e.tensor_scalar_mul(ptile[:, :], ptile[:, :], cm[:, 6:7]))
        prev_t = ch.step("vector", lambda e, i=i: e.tensor_copy(cw_tm[:, 8 * i:8 * (i + 1)], ptile[:, :]))
        cw_toks.append(prev_t)
    cwt_toks = []
    pv = None
    for i in range(16):
        tt = s.add("tensor", lambda e, i=i: e.transpose(
            c.pT2[0:8, 0:128], cw_tm[:, 8 * i:8 * (i + 1)], c.ident[:, :]),
            deps=[cw_toks[i], pv])
        pv = s.add("scalar", lambda e, i=i: e.copy(
            cwT8[:, 128 * i:128 * (i + 1)], c.pT2[0:8, 0:128]), deps=[tt])
        cwt_toks.append(pv)
    cw_ready = cwt_toks[-1]

    PNL = [Bv(16512, 4096), Bv(20608, 4096)]
    ACT1 = Bv(24704, 2048)
    pnl_dma = [None, None]
    pnl_last_mm = [wo_mm_last, wo_mm_last]
    cwb_toks = {}
    act_dma_slot = [None] * 4
    ev_tok = None
    cwm_tok = None
    gi = 0
    for ee in range(9):
        for tc in range(4):
            if ee < 8:
                prev_cwb = cwb_toks.get((ee, tc - 1), cwb_toks.get((ee - 1, 3)))
                mmb = s.add("tensor", lambda e, ee=ee, tc=tc: e.matmul(
                    c.pCW[:, 0:512], selb[:, 128 * ee:128 * (ee + 1)],
                    cwT8[:, 512 * tc:512 * (tc + 1)], start=True, stop=True),
                    deps=[cw_ready, tok_smalls_s, cwb_toks.get((ee - 1, tc)),
                          prev_cwb, cwm_tok])
                cwb_toks[(ee, tc)] = s.add("scalar", lambda e, tc=tc: e.copy(
                    cwB4[:, 512 * tc:512 * (tc + 1)], c.pCW[:, 0:512]),
                    deps=[mmb, cwm_tok])
            else:
                cwb_toks[(ee, tc)] = s.add("vector", lambda e, tc=tc: e.memset(
                    cwB4[:, 512 * tc:512 * (tc + 1)], 0.1),
                    deps=[cwb_toks.get((ee - 1, tc)), cwm_tok])
        for g in range(8):
            buf = gi % 2
            src_ap = (w1_in.ap()[ee].rearrange("(kc p) f -> p kc f", p=128)[:, :, 512 * g:512 * (g + 1)]
                      if ee < 8 else
                      ws1_in.ap().rearrange("(kc p) f -> p kc f", p=128)[:, :, 512 * g:512 * (g + 1)])
            pnl_dma[buf] = s.add("gpsimd", lambda e, buf=buf, src_ap=src_ap: e.dma_start(
                out=PNL[buf].rearrange("p (kc n) -> p kc n", n=512), in_=src_ap),
                dma=True, deps=[pnl_last_mm[buf], tok_smalls_g], stream=f"p{buf}")
            for fb in range(4):
                fc = 4 * g + fb
                for tc in range(4):
                    ps = psA[mmi % 2]
                    mm = emit_mm8(s, c, ps, fb,
                                  lambda kc, tc=tc: c.xnF[:, xn_col(c, kc, 512 * tc): xn_col(c, kc, 512 * (tc + 1))],
                                  512, wbuf=PNL[buf], wcols=512,
                                  deps=[pnl_dma[buf], xn_ready, ps_free[mmi % 2]])
                    ev_tok = s.add("scalar", lambda e, ee=ee, fc=fc, fb=fb, ps=ps: e.activation(
                        ACT1[:, 512 * fb:512 * (fb + 1)], ps[:, 0:512],
                        AF.Silu, bias=b1sb[:, 32 * ee + fc:32 * ee + fc + 1]),
                        deps=[mm, act_dma_slot[fb]])
                    cwm_tok = s.add("vector", lambda e, fb=fb, tc=tc: e.tensor_mul(
                        ACT1[:, 512 * fb:512 * (fb + 1)],
                        ACT1[:, 512 * fb:512 * (fb + 1)],
                        cwB4[:, 512 * tc:512 * (tc + 1)]),
                        deps=[ev_tok, cwb_toks[(ee, tc)]])
                    act_dma_slot[fb] = s.add("gpsimd", lambda e, ee=ee, g=g, tc=tc, fb=fb: e.dma_start(
                        out=act_hbm.ap()[ee, g, tc, :, 512 * fb:512 * (fb + 1)],
                        in_=ACT1[:, 512 * fb:512 * (fb + 1)]), dma=True, deps=[cwm_tok], stream=f"a{fb}")
                    ps_free[mmi % 2] = ev_tok
                    mmi += 1
                    pnl_last_mm[buf] = mm
            gi += 1

    W2H = [Bv(0, 8192), Bv(8192, 8192)]
    ACTF = [Bv(24704, 8192), Bv(32896, 8192)]
    p1_done = s.cur("tensor")
    p2_dbg = []
    w2h_dma = [None, None]
    w2h_last_mm = [p1_done, p1_done]
    actf_dma = [None, None]
    actf_last_mm = [None, None]
    last_ps2_ev = {("sdma", 0): acc_last.get(("stg", 0)),
                   ("sdma", 1): acc_last.get(("stg", 1))}
    for ee in range(9):
        for hh in range(2):
            for half in range(2):
                src_ap = (w2_in.ap()[ee].rearrange("(kc p) n -> p kc n", p=128)[:, 16 * half:16 * (half + 1), 512 * hh:512 * (hh + 1)]
                          if ee < 8 else
                          ws2_in.ap().rearrange("(kc p) n -> p kc n", p=128)[:, 16 * half:16 * (half + 1), 512 * hh:512 * (hh + 1)])
                w2h_dma[half] = s.add("gpsimd", lambda e, half=half, src_ap=src_ap: e.dma_start(
                    out=W2H[half].rearrange("p (kc n) -> p kc n", n=512), in_=src_ap),
                    dma=True, deps=[w2h_last_mm[half]] + p2_dbg, stream=f"w2{half}")
            for tc in range(4):
                for half in range(2):
                    actf_dma[half] = s.add("gpsimd", lambda e, ee=ee, tc=tc, half=half: e.dma_start(
                        out=ACTF[half].rearrange("p (g n) -> p g n", n=2048),
                        in_=act_hbm.ap()[ee, 4 * half:4 * (half + 1), tc, :, :]
                        .rearrange("g p n -> p g n")),
                        dma=True, stream=f"af{half}",
                        deps=[actf_last_mm[half]] + p2_dbg
                             + [act_dma_slot[fb] for fb in range(4)])
                if debug and ee == 2 and hh == 1 and tc == 2:
                    p2_toks = [
                        s.add("gpsimd", lambda e: e.dma_start(
                            out=dbg_p2.ap()[0], in_=W2H[0][:, :]), dma=True,
                            deps=[w2h_dma[0]], stream="dp0"),
                        s.add("gpsimd", lambda e: e.dma_start(
                            out=dbg_p2.ap()[1], in_=W2H[1][:, :]), dma=True,
                            deps=[w2h_dma[1]], stream="dp1"),
                        s.add("gpsimd", lambda e: e.dma_start(
                            out=dbg_p2.ap()[2], in_=ACTF[0][:, :]), dma=True,
                            deps=[actf_dma[0]], stream="dp2"),
                        s.add("gpsimd", lambda e: e.dma_start(
                            out=dbg_p2.ap()[3], in_=ACTF[1][:, :]), dma=True,
                            deps=[actf_dma[1]], stream="dp3"),
                    ]
                    p2_dbg.extend(p2_toks)
                for h4 in range(4):
                    h = 4 * hh + h4
                    ps = psA[mmi % 2]
                    def fn2(e, ee=ee, tc=tc, h4=h4, h=h, ps=ps):
                        last = None
                        extra = (ee == 0)
                        for fc in range(32):
                            half = fc // 16
                            fcl = fc % 16
                            is_last = (fc == 31) and not extra
                            last = e.matmul(
                                ps[:, 0:512],
                                W2H[half][:, 512 * fcl + 128 * h4: 512 * fcl + 128 * (h4 + 1)],
                                ACTF[half][:, 512 * fcl:512 * (fcl + 1)],
                                start=(fc == 0), stop=is_last)
                        if extra:
                            last = e.matmul(ps[:, 0:512], b2b[:, 128 * h:128 * (h + 1)],
                                            cwT8[:, 512 * tc:512 * (tc + 1)],
                                            start=False, stop=False)
                            last = e.matmul(ps[:, 0:512], bs2b[:, 128 * h:128 * (h + 1)],
                                            row01[:, :], start=False, stop=True)
                        return last
                    mm = s.add("tensor", fn2,
                               deps=[w2h_dma[0], w2h_dma[1], actf_dma[0], actf_dma[1],
                                     ps_free[mmi % 2], cw_ready, tok_smalls_g])
                    w2h_last_mm[0] = w2h_last_mm[1] = mm
                    actf_last_mm[0] = actf_last_mm[1] = mm
                    ev = s.add("vector", lambda e, ps=ps, h4=h4: e.tensor_copy(
                        (STG if h4 % 2 == 0 else STG2)[:, 0:512], ps[:, 0:512]),
                        deps=[mm, last_ps2_ev.get(("sdma", h4 % 2))])
                    dmat = s.add("gpsimd", lambda e, h=h, tc=tc, h4=h4: e.dma_start(
                        out=x1_acc.ap()[:, 2048 * h + 512 * tc: 2048 * h + 512 * (tc + 1)],
                        in_=(STG if h4 % 2 == 0 else STG2)[:, 0:512], accum_op=ALU.add),
                        dma=True, deps=[ev] + acc_last[(h, tc)], stream=f"st{h4 % 2}")
                    acc_last[(h, tc)] = [dmat]
                    last_ps2_ev[("sdma", h4 % 2)] = dmat
                    ps_free[mmi % 2] = ev
                    mmi += 1
    moe_done = [t for h in range(8) for tc in range(4) for t in acc_last[(h, tc)]]

    x1_rd = [None, None]
    x1_wr = [None, None]
    x1ev = [None, None]
    for i in range(16):
        par = i % 2
        bufv = STG if par == 0 else STG2
        x1_rd[par] = s.add("sync", lambda e, i=i, bufv=bufv: e.dma_start(
            out=bufv.rearrange("p (j t) -> p j t", t=128),
            in_=x1a[:, :, 128 * i:128 * (i + 1)]),
            dma=True, deps=moe_done if i < 2 else [x1ev[par]], stream=f"rd{par}")
        lastv = None
        for j in range(8):
            tt = s.add("tensor", lambda e, j=j, bufv=bufv: e.transpose(
                c.pT[:, :], bufv[:, 128 * j:128 * (j + 1)], c.ident[:, :]),
                deps=[x1_rd[par], lastv])
            lastv = s.add("vector", lambda e, j=j, par=par: e.tensor_copy(
                (c.xtm[:, 0:1024] if par == 0 else c.xtm[:, 1024:2048])[:, 128 * j:128 * (j + 1)],
                c.pT[:, :]), deps=[tt, x1_wr[par]])
        x1ev[par] = lastv
        x1_wr[par] = s.add("sync", lambda e, i=i, par=par: e.dma_start(
            out=x1_out.ap()[128 * i:128 * (i + 1), :],
            in_=c.xtm[:, 0:1024] if par == 0 else c.xtm[:, 1024:2048]),
            dma=True, deps=[lastv], stream=f"wr{par}")

    X1T = Fv(5120, 4096)
    s.add("sync", lambda e: e.dma_start(out=c.trrev, in_=trrev_in.ap()), dma=True)
    s.add("sync", lambda e: e.dma_start(out=nu8[:, :], in_=nul_in.ap()), dma=True)
    s.add("sync", lambda e: e.dma_start(out=brkv[:, :], in_=brkv_in.ap()), dma=True)
    s.add("scalar", lambda e: e.activation(nu8[:, :], nu8[:, :], AF.Exp))
    s.add("vector", lambda e: e.tensor_scalar_mul(nun8[:, :], nu8[:, :], -1.0))
    for tc in range(4):
        for f in range(8):
            s.add("sync", lambda e, f=f, tc=tc: e.dma_start(
                out=X1T[:, 512 * f:512 * (f + 1)],
                in_=x1_acc.ap()[:, 2048 * f + 512 * tc: 2048 * f + 512 * (tc + 1)]), dma=True)
        def fsum(e, tc=tc):
            last = None
            for f in range(8):
                last = e.matmul(c.pM[0:1, 0:512], ones_col[:, :],
                                X1T[:, 512 * f:512 * (f + 1)],
                                start=(f == 0), stop=(f == 7))
            return last
        s.add("tensor", fsum)
        for f in range(8):
            s.add("scalar", lambda e, f=f: e.square(c.lnt[:, 0:512], X1T[:, 512 * f:512 * (f + 1)]))
            def fsq(e, f=f):
                return e.matmul(c.pM2[0:1, 0:512], ones_col[:, :], c.lnt[:, 0:512],
                                start=(f == 0), stop=(f == 7))
            s.add("tensor", fsq)
        s.add("vector", lambda e: e.tensor_scalar_mul(srow[0:1, 0:512], c.pM[0:1, 0:512], 1.0 / 1024.0))
        s.add("vector", lambda e: e.tensor_scalar_mul(srow[0:1, 512:1024], c.pM2[0:1, 0:512], 1.0 / 1024.0))
        s.add("scalar", lambda e: e.square(srow[0:1, 1024:1536], srow[0:1, 0:512]))
        s.add("vector", lambda e: e.tensor_sub(srow[0:1, 512:1024], srow[0:1, 512:1024], srow[0:1, 1024:1536]))
        s.add("scalar", lambda e: e.activation(srow[0:1, 512:1024], srow[0:1, 512:1024], AF.Ln, bias=c.cst[0:1, 0:1]))
        s.add("scalar", lambda e: e.activation(srow[0:1, 512:1024], srow[0:1, 512:1024], AF.Exp, scale=-0.5))
        s.add("tensor", lambda e: e.matmul(c.pM[:, 0:512], ones_rowf[:, :], srow[0:1, 0:512], start=True, stop=True))
        s.add("vector", lambda e: e.tensor_copy(STG[:, 0:512], c.pM[:, 0:512]))
        s.add("tensor", lambda e: e.matmul(c.pM2[:, 0:512], ones_rowf[:, :], srow[0:1, 512:1024], start=True, stop=True))
        s.add("vector", lambda e: e.tensor_copy(STG[:, 512:1024], c.pM2[:, 0:512]))
        for f in range(8):
            s.add("vector", lambda e, f=f: e.tensor_sub(c.lnt[:, 0:512], X1T[:, 512 * f:512 * (f + 1)], STG[:, 0:512]))
            s.add("vector", lambda e, f=f, tc=tc: e.tensor_mul(
                c.xnF[:, xn_col(c, f, 512 * tc): xn_col(c, f, 512 * (tc + 1))],
                c.lnt[:, 0:512], STG[:, 512:1024]))
    KT = Fv(9216, 2048)
    VT = Fv(11264, 2048)
    SA = Fv(13312, 2048)
    SB2 = Fv(15360, 2048)
    KWB = Bv(16512, 1024)
    VWB = Bv(17536, 1024)
    for j in range(8):
        s.add("gpsimd", lambda e, j=j: e.dma_start(
            out=KWB.rearrange("p (kc n) -> p kc n", n=128),
            in_=wrkv_in.ap().rearrange("(kc p) n -> p kc n", p=128)[:, :, 1024 + 128 * j:1024 + 128 * (j + 1)]), dma=True)
        s.add("gpsimd", lambda e, j=j: e.dma_start(
            out=VWB.rearrange("p (kc n) -> p kc n", n=128),
            in_=wrkv_in.ap().rearrange("(kc p) n -> p kc n", p=128)[:, :, 2048 + 128 * j:2048 + 128 * (j + 1)]), dma=True)
        for tc in range(4):
            emit_mm8(s, c, c.pM, 0,
                     lambda kc, tc=tc: c.xnF[:, xn_col(c, kc, 512 * tc): xn_col(c, kc, 512 * (tc + 1))],
                     512, wbuf=KWB, wcols=128)
            s.add("scalar", lambda e, j=j, tc=tc: e.activation(
                KT[:, 512 * tc:512 * (tc + 1)], c.pM[:, 0:512], AF.Exp,
                bias=brkv[:, 8 + j:9 + j]))
            emit_mm8(s, c, c.pM2, 0,
                     lambda kc, tc=tc: c.xnF[:, xn_col(c, kc, 512 * tc): xn_col(c, kc, 512 * (tc + 1))],
                     512, wbuf=VWB, wcols=128)
            s.add("vector", lambda e, j=j, tc=tc: e.tensor_scalar_add(
                VT[:, 512 * tc:512 * (tc + 1)], c.pM2[:, 0:512], brkv[:, 16 + j:17 + j]))
        s.add("vector", lambda e, j=j: e.tensor_scalar_mul(SA[:, :], c.trrev[:, :], nun8[:, j:j + 1]))
        s.add("scalar", lambda e: e.activation(SA[:, :], SA[:, :], AF.Exp))
        s.add("vector", lambda e: e.tensor_mul(SB2[:, :], KT[:, :], VT[:, :]))
        s.add("vector", lambda e: e.tensor_mul(SB2[:, :], SB2[:, :], SA[:, :]))
        s.add("vector", lambda e, j=j: e.reduce_sum(usum_sb[:, j:j + 1], SB2[:, :], AX))
        s.add("vector", lambda e: e.tensor_mul(SB2[:, :], KT[:, :], SA[:, :]))
        s.add("vector", lambda e, j=j: e.reduce_sum(usum_sb[:, 8 + j:9 + j], SB2[:, :], AX))
    s.add("sync", lambda e: e.dma_start(out=usum_out.ap(), in_=usum_sb[:, :]), dma=True)
    s.emit("L1")
    return nc


def host_inputs_launch1(inputs, consts, wsums):
    x = np.asarray(inputs["L0_attn_x"], np.float32)
    wkT = _f32(np.asarray(inputs["L0_attn_Wk"]).T)
    wvT = _f32(np.asarray(inputs["L0_attn_Wv"]).T)
    wrT = _f32(np.asarray(inputs["L0_attn_Wr"]).T)
    woT = _f32(np.asarray(inputs["L0_attn_Wo"]).T)
    td = ctile(inputs["L0_attn_time_decay"])
    tf = ctile(inputs["L0_attn_time_first"])
    tmk = ctile(np.asarray(inputs["L0_attn_time_mix_k"]).reshape(-1))
    tmv = ctile(np.asarray(inputs["L0_attn_time_mix_v"]).reshape(-1))
    tmr = ctile(np.asarray(inputs["L0_attn_time_mix_r"]).reshape(-1))
    wg = _f32(inputs["L0_ffn_Wg"])
    w1 = _f32(inputs["L0_ffn_W1"])
    w2 = _f32(inputs["L0_ffn_W2"])
    ws1 = _f32(inputs["L0_ffn_Ws1"])
    ws2 = _f32(inputs["L0_ffn_Ws2"])
    b1 = np.asarray(inputs["L0_ffn_b1"], np.float32)
    bs1 = np.asarray(inputs["L0_ffn_bs1"], np.float32)
    b1ct = np.zeros((128, 288), np.float32)
    for e in range(8):
        b1ct[:, 32 * e:32 * (e + 1)] = b1[e].reshape(32, 128).T
    b1ct[:, 256:288] = bs1.reshape(32, 128).T
    b2 = _f32(inputs["L0_ffn_b2"])
    bs2 = _f32(np.asarray(inputs["L0_ffn_bs2"]).reshape(1, 1024))
    wrkv = _f32(inputs["L1_attn_W_rkv"])
    brkv = np.asarray(inputs["L1_attn_b_rkv"], np.float32)
    brkv_ct = np.zeros((128, 24), np.float32)
    for part in range(3):
        brkv_ct[:, 8 * part:8 * (part + 1)] = brkv[1024 * part:1024 * (part + 1)].reshape(8, 128).T
    nul = ctile(inputs["L1_attn_nu_log"])
    sel = np.zeros((8, 1024), np.float32)
    for e in range(8):
        sel[e, 128 * e:128 * (e + 1)] = 1.0
    sel = sel.astype(ml_dtypes.bfloat16)
    ident_p = np.zeros((128, 24), np.float32)
    ident_p[:, 0:8] = NEG
    maps = []
    for core in range(8):
        b, sq = core // 4, core % 4
        xe = np.zeros((2049, 1024), np.float32)
        xe[1:] = x[b, sq * SEG:(sq + 1) * SEG]
        if sq > 0:
            xe[0] = x[b, sq * SEG - 1]
        carry = np.zeros((128, 72), np.float32)
        for k in range(3):
            seg = sq - 3 + k
            if seg >= 0:
                carry[:, 24 * k:24 * (k + 1)] = wsums[b * 4 + seg]
            else:
                carry[:, 24 * k:24 * (k + 1)] = ident_p
        maps.append(dict(
            x=xe, wkT=wkT, wvT=wvT, wrT=wrT, woT=woT, td_ct=td, tf_ct=tf,
            tmk_ct=tmk, tmv_ct=tmv, tmr_ct=tmr, carry=carry, wg=wg,
            w1=w1, b1ct=b1ct, w2=w2, b2=b2, bs2=bs2, ws1=ws1, ws2=ws2,
            wrkv=wrkv, brkv_ct=brkv_ct, nul_ct=nul,
            c_ident=consts["c_ident"], c_trrev=consts["c_trrev"],
            c_iota8=consts["c_iota8"], c_sel=sel))
    return maps



XS2 = 4224
XSA = 3584
XSB = 2560
HID = 2730


def _mlp_tiles():
    out = []
    for i in range(22):
        lo = 128 * i
        hi = min(HID, 128 * (i + 1))
        out.append((lo, hi))
    return out


def w1_perm():
    perm = []
    for (lo, hi) in _mlp_tiles():
        perm.extend(range(lo, hi))
        perm.extend(range(HID + lo, HID + hi))
    return np.array(perm)


def build_launch2():
    nc = _mknc()
    def I(name, shape, dt=F32):
        return nc.dram_tensor(name, shape, dt, kind="ExternalInput")
    x_in = I("x1e", [4224, 1024])
    usum_in = I("usum_pred", [128, 48])
    wqkv_in = I("wqkv", [2, 1024, 1536])
    wo_in = I("wo", [2, 512, 1024])
    w1p_in = I("w1p", [2, 1024, 5460])
    b1p_in = I("b1p_ct", [128, 88])
    w2_in = I("w2", [2, 2730, 1024])
    b2_in = I("b2_ct", [128, 16])
    lng_in = I("lng_ct", [128, 16])
    lnb_in = I("lnb_ct", [128, 16])
    wrkv_in = I("wrkv", [1024, 3072])
    brkv_in = I("brkv_ct", [128, 24])
    wo2_in = I("wo2", [1024, 1024])
    bo_in = I("bo_ct", [128, 8])
    nul_in = I("nul_ct", [128, 8])
    gl_in = I("gl_ct", [128, 8])
    qcos_in = I("t_qcos", [128, 4096]); qsin_in = I("t_qsin", [128, 4096])
    kcos_in = I("t_kcos", [128, 4096]); ksin_in = I("t_ksin", [128, 4096])
    maskS_in = I("mask_std", [512, 1024], BF16)
    maskF_in = I("mask_first", [512, 1024], BF16)
    e2c_in = I("c_e2", [128, 2])
    e2v_in = I("c_e2v", [128, 8])
    flag_in = I("haloflag", [128, 1])
    ident_in = I("c_ident", [128, 128])
    x2_out = nc.dram_tensor("x2seg", [2048, 1024], F32, kind="ExternalOutput")
    x2p = nc.dram_tensor("x2p", [128, 16384], F32, kind="Internal")
    actA_hbm = nc.dram_tensor("actA_hbm", [7, 128, 11264], BF16, kind="Internal")
    actB_hbm = nc.dram_tensor("actB_hbm", [4, 128, 11264], BF16, kind="Internal")

    AR = nc.alloc_sbuf_tensor("AR", [128, 6144], F32)
    XN1 = nc.alloc_sbuf_tensor("XN1", [128, 8 * XS2], BF16)
    XFF = nc.alloc_sbuf_tensor("XFF", [128, 8 * XSA], BF16)
    WQB = nc.alloc_sbuf_tensor("WQB", [128, 12288], BF16)
    WOB = nc.alloc_sbuf_tensor("WOB", [128, 4096], BF16)
    KB0 = nc.alloc_sbuf_tensor("KB0", [128, 2048], BF16)
    KB1 = nc.alloc_sbuf_tensor("KB1", [128, 2048], BF16)
    VB0 = nc.alloc_sbuf_tensor("VB0", [128, 2048], BF16)
    VB1 = nc.alloc_sbuf_tensor("VB1", [128, 2048], BF16)
    QBF = nc.alloc_sbuf_tensor("QBF", [128, 2048], BF16)
    OF = nc.alloc_sbuf_tensor("OF", [128, 2048], BF16)
    ASTORE = nc.alloc_sbuf_tensor("ASTORE", [128, 2048], BF16)
    MSK = ASTORE[:, 0:1024]
    ATT = ASTORE[:, 1024:2048]

    c = Ctx(); c.nc = nc; c.XS = XS2
    c.ident = nc.alloc_sbuf_tensor("ident", [128, 128], F32)
    c.cst = nc.alloc_sbuf_tensor("cst", [128, 4], F32)
    c.xnF = XN1
    c.xtm = AR[:, 0:1024]
    c.lnt = AR[:, 1024:2048]
    c.lns = AR[:, 2048:3072]
    c.lnm = nc.alloc_sbuf_tensor("lnm", [128, 8], F32)
    c.row1 = AR[0:1, 3072:4096]
    c.rowm = nc.alloc_sbuf_tensor("rowm", [1, 4], F32)
    c.pT = nc.alloc_psum_tensor("pT", [128, 128], F32)
    c.pM = nc.alloc_psum_tensor("pM", [128, 512], F32)
    c.pM2 = nc.alloc_psum_tensor("pM2", [128, 512], F32)
    c.pM3 = nc.alloc_psum_tensor("pM3", [128, 512], F32)
    c.pSIM = nc.alloc_psum_tensor("pSIM", [128, 1024], F32)
    c.pSIM2 = nc.alloc_psum_tensor("pSIM2", [128, 1024], F32)

    usum_sb = AR[:, 3072:3120]
    brkv = AR[:, 3120:3144]
    lam8 = AR[:, 3152:3160]
    nu8 = AR[:, 3160:3168]
    gam8 = AR[:, 3168:3176]
    lam16 = AR[:, 3176:3200]
    st16 = AR[:, 3200:3216]
    bo8 = nc.alloc_sbuf_tensor("bo8", [128, 8], F32)
    e2c = nc.alloc_sbuf_tensor("e2c", [128, 2], F32)
    e2v = nc.alloc_sbuf_tensor("e2v", [128, 8], F32)
    lng = nc.alloc_sbuf_tensor("lng", [128, 16], F32)
    lnb = nc.alloc_sbuf_tensor("lnb", [128, 16], F32)
    b2c = nc.alloc_sbuf_tensor("b2c", [128, 16], F32)
    b1pc = nc.alloc_sbuf_tensor("b1pc", [128, 88], F32)
    flagc = nc.alloc_sbuf_tensor("flagc", [128, 1], F32)

    s = Sched(nc)
    for (dst, src) in ((c.ident, ident_in),
                       (e2c, e2c_in), (e2v, e2v_in), (lng, lng_in),
                       (lnb, lnb_in), (b2c, b2_in), (b1pc, b1p_in),
                       (flagc, flag_in)):
        s.add("sync", lambda e, dst=dst, src=src: e.dma_start(
            out=dst[:, :], in_=src.ap()), dma=True)
    emit_consts(s, c)

    x2pa = x2p.ap().rearrange("p (j t) -> p j t", t=2048)
    STG = AR[:, 3072:4096]
    for i in range(33):
        xv = c.xtm
        s.add("sync", lambda e, i=i, xv=xv: e.dma_start(
            out=xv, in_=x_in.ap()[128 * i:128 * (i + 1), :]), dma=True)
        emit_ln_tile(s, c, xv, c.lns[:, :], c.lnm)
        for j in range(8):
            s.add("tensor", lambda e, j=j: e.transpose(
                c.pT[:, :], c.lns[:, 128 * j:128 * (j + 1)], c.ident[:, :]))
            s.add("scalar", lambda e, i=i, j=j: e.copy(
                c.xnF[:, XS2 * j + 128 * i: XS2 * j + 128 * (i + 1)], c.pT[:, :]))
        if i >= 17:
            for j in range(8):
                s.add("tensor", lambda e, j=j, xv=xv: e.transpose(
                    c.pT[:, :], xv[:, 128 * j:128 * (j + 1)], c.ident[:, :]))
                s.add("vector", lambda e, j=j: e.tensor_copy(
                    STG[:, 128 * j:128 * (j + 1)], c.pT[:, :]))
            s.add("sync", lambda e, i=i: e.dma_start(
                out=x2pa[:, :, 128 * (i - 17):128 * (i - 16)],
                in_=STG.rearrange("p (j t) -> p j t", t=128)), dma=True)

    for (dst, src) in ((usum_sb, usum_in), (brkv, brkv_in), (bo8, bo_in),
                       (nu8, nul_in), (gam8, gl_in)):
        s.add("sync", lambda e, dst=dst, src=src: e.dma_start(
            out=dst[:, :], in_=src.ap()), dma=True)
    s.add("scalar", lambda e: e.activation(lam8[:, :], nu8[:, :], AF.Exp))
    s.add("vector", lambda e: e.tensor_scalar_mul(lam8[:, :], lam8[:, :], -1.0))
    s.add("scalar", lambda e: e.activation(lam8[:, :], lam8[:, :], AF.Exp))
    s.add("vector", lambda e: e.tensor_add(gam8[:, :], nu8[:, :], gam8[:, :]))
    s.add("scalar", lambda e: e.activation(gam8[:, :], gam8[:, :], AF.Exp))
    s.add("vector", lambda e: e.tensor_scalar(gam8[:, :], gam8[:, :], 1.0, -1.0, ALU.mult, ALU.add))
    s.add("vector", lambda e: e.tensor_copy(lam16[:, 16:24], lam8[:, :]))
    for _ in range(11):
        s.add("vector", lambda e: e.tensor_mul(lam16[:, 16:24], lam16[:, 16:24], lam16[:, 16:24]))
    s.add("vector", lambda e: e.tensor_copy(lam16[:, 0:8], lam16[:, 16:24]))
    s.add("vector", lambda e: e.tensor_copy(lam16[:, 8:16], lam16[:, 16:24]))
    s.add("vector", lambda e: e.memset(st16[:, :], 0.0))
    for kslot in range(3):
        s.add("vector", lambda e: e.tensor_mul(st16[:, :], st16[:, :], lam16[:, 0:16]))
        s.add("vector", lambda e, kslot=kslot: e.tensor_add(
            st16[:, :], st16[:, :], usum_sb[:, 16 * kslot:16 * (kslot + 1)]))

    KT = AR[:, 0:512]; VT = AR[:, 512:1024]; KVT = AR[:, 1024:1536]
    Y1 = AR[:, 1536:2048]; Y2 = AR[:, 2048:2560]; LREP = AR[:, 2560:3072]
    XL1 = XFF[:, 0:16384]
    RT = MSK
    KWB = WQB[:, 0:1024]; VWB = WQB[:, 1024:2048]; RWB = WQB[:, 2048:3072]
    wrkv3 = wrkv_in.ap().rearrange("(kc p) n -> p kc n", p=128)
    for j in range(8):
        for (dstb, coff) in ((RWB, 0), (KWB, 1024), (VWB, 2048)):
            s.add("gpsimd", lambda e, dstb=dstb, coff=coff, j=j: e.dma_start(
                out=dstb.rearrange("p (kc n) -> p kc n", n=128),
                in_=wrkv3[:, :, coff + 128 * j:coff + 128 * (j + 1)]), dma=True)
        s.add("vector", lambda e: e.memset(LREP, 1.0))
        s.add("vector", lambda e, j=j: e.tensor_scalar_mul(LREP, LREP, lam8[:, j:j + 1]))
        for th in range(4):
            t0 = 512 * th
            emit_mm8(s, c, c.pM, 0,
                     lambda kc, t0=t0: c.xnF[:, XS2 * kc + 2176 + t0: XS2 * kc + 2176 + t0 + 512],
                     512, wbuf=KWB, wcols=128)
            s.add("scalar", lambda e, j=j: e.activation(
                KT[:, :], c.pM[:, 0:512], AF.Exp, bias=brkv[:, 8 + j:9 + j]))
            emit_mm8(s, c, c.pM2, 0,
                     lambda kc, t0=t0: c.xnF[:, XS2 * kc + 2176 + t0: XS2 * kc + 2176 + t0 + 512],
                     512, wbuf=VWB, wcols=128)
            s.add("vector", lambda e, j=j: e.tensor_scalar_add(
                VT[:, :], c.pM2[:, 0:512], brkv[:, 16 + j:17 + j]))
            emit_mm8(s, c, c.pM, 0,
                     lambda kc, t0=t0: c.xnF[:, XS2 * kc + 2176 + t0: XS2 * kc + 2176 + t0 + 512],
                     512, wbuf=RWB, wcols=128)
            s.add("scalar", lambda e, j=j: e.activation(
                RT[:, 0:512], c.pM[:, 0:512], AF.Sigmoid, bias=brkv[:, j:j + 1]))
            s.add("vector", lambda e: e.tensor_mul(KVT[:, :], KT[:, :], VT[:, :]))
            s.add("vector", lambda e, j=j: e.tensor_tensor_scan(
                Y1[:, :], LREP, KVT[:, :], st16[:, j:j + 1], ALU.mult, ALU.add))
            s.add("vector", lambda e, j=j: e.tensor_tensor_scan(
                Y2[:, :], LREP, KT[:, :], st16[:, 8 + j:9 + j], ALU.mult, ALU.add))
            s.add("vector", lambda e, j=j: e.tensor_copy(st16[:, j:j + 1], Y1[:, 511:512]))
            s.add("vector", lambda e, j=j: e.tensor_copy(st16[:, 8 + j:9 + j], Y2[:, 511:512]))
            s.add("vector", lambda e, j=j: e.scalar_tensor_tensor(
                KVT[:, :], KVT[:, :], gam8[:, j:j + 1], Y1[:, :], ALU.mult, ALU.add))
            s.add("vector", lambda e, j=j: e.scalar_tensor_tensor(
                KT[:, :], KT[:, :], gam8[:, j:j + 1], Y2[:, :], ALU.mult, ALU.add))
            s.add("scalar", lambda e: e.activation(KT[:, :], KT[:, :], AF.Ln, bias=c.cst[:, 2:3]))
            s.add("scalar", lambda e: e.activation(KT[:, :], KT[:, :], AF.Exp, scale=-1.0))
            s.add("vector", lambda e: e.tensor_mul(KVT[:, :], KVT[:, :], KT[:, :]))
            s.add("vector", lambda e, j=j, th=th: e.tensor_mul(
                XL1[:, 2048 * j + 512 * th: 2048 * j + 512 * (th + 1)],
                KVT[:, :], RT[:, 0:512]))
    WO2B = WQB[:, 3072:11264]
    for kc in range(8):
        s.add("gpsimd", lambda e, kc=kc: e.dma_start(
            out=WO2B[:, 1024 * kc:1024 * (kc + 1)],
            in_=wo2_in.ap()[128 * kc:128 * (kc + 1), :]), dma=True)
    for cout in range(8):
        for tc in range(4):
            def fwo(e, cout=cout, tc=tc):
                last = None
                for kc in range(8):
                    last = e.matmul(c.pM[:, 0:512],
                                    WO2B[:, 1024 * kc + 128 * cout: 1024 * kc + 128 * (cout + 1)],
                                    XL1[:, 2048 * kc + 512 * tc: 2048 * kc + 512 * (tc + 1)],
                                    start=(kc == 0), stop=(kc == 7))
                return last
            s.add("tensor", fwo)
            s.add("vector", lambda e, cout=cout: e.tensor_scalar_add(
                STG[:, 0:512], c.pM[:, 0:512], bo8[:, cout:cout + 1]))
            s.add("gpsimd", lambda e, cout=cout, tc=tc: e.dma_start(
                out=x2p.ap()[:, 2048 * cout + 512 * tc: 2048 * cout + 512 * (tc + 1)],
                in_=STG[:, 0:512], accum_op=ALU.add), dma=True)

    if os.environ.get("MK_L2STOP") != "1":
        emit_ffn_layers(s, c, locals())
    s.emit("L2")
    return nc


def emit_ffn_layers(s, c, L):
    nc = c.nc
    AR = L["AR"]; XN1 = L["XN1"]; XFF = L["XFF"]; WQB = L["WQB"]
    WOB = L["WOB"]; MSK = L["MSK"]; QBF = L["QBF"]; ATT = L["ATT"]; OF = L["OF"]
    ASTORE = L["ASTORE"]
    KB = (L["KB0"], L["KB1"]); VB = (L["VB0"], L["VB1"])
    wqkv_in = L["wqkv_in"]; wo_in = L["wo_in"]; w1p_in = L["w1p_in"]
    w2_in = L["w2_in"]; maskS_in = L["maskS_in"]; maskF_in = L["maskF_in"]
    qcos_in = L["qcos_in"]; qsin_in = L["qsin_in"]
    kcos_in = L["kcos_in"]; ksin_in = L["ksin_in"]
    actA_hbm = L["actA_hbm"]; actB_hbm = L["actB_hbm"]
    x2p = L["x2p"]; x2_out = L["x2_out"]; x2pa = L["x2pa"]
    e2c = L["e2c"]; e2v = L["e2v"]; lng = L["lng"]; lnb = L["lnb"]
    b2c = L["b2c"]; b1pc = L["b1pc"]; flagc = L["flagc"]; STG = L["STG"]

    ones_colb = nc.alloc_sbuf_tensor("ones_colb", [128, 1], BF16)
    ones_colf = nc.alloc_sbuf_tensor("ones_colf", [128, 1], F32)
    ones_rowf = nc.alloc_sbuf_tensor("ones_rowf_a", [1, 128], F32)
    s.add("vector", lambda e: e.memset(ones_colb[:, :], 1.0))
    s.add("vector", lambda e: e.memset(ones_colf[:, :], 1.0))
    s.add("vector", lambda e: e.memset(ones_rowf[:, :], 1.0))
    mskS_src = maskS_in.ap()
    mskF_src = maskF_in.ap()

    QFp_ = [AR[:, 0:512], AR[:, 0:512]]
    SQp_ = [AR[:, 512:1024], AR[:, 512:1024]]
    BCp_ = [AR[:, 1024:1536], AR[:, 1024:1536]]
    ROTp_ = [AR[:, 1536:2048], AR[:, 1536:2048]]
    W1p_ = [AR[:, 2048:2560], AR[:, 2048:2560]]
    W2p_ = [AR[:, 2560:3072], AR[:, 2560:3072]]
    TCOk = AR[:, 3072:3584]; TSIk = AR[:, 3584:4096]
    TCOq = TCOk; TSIq = TSIk
    R2a = AR[0:1, 4096:4608]; R2b = AR[0:1, 4608:5120]
    R2full = AR[0:1, 4096:5120]
    SMp_ = [AR[:, 0:1024], AR[:, 4096:5120]]
    EFp_ = [AR[:, 1024:2048], AR[:, 5120:6144]]
    cps_ = [c.pM2, c.pSIM2]
    pQK_ = [c.pSIM, c.pSIM2]
    pKV_ = [c.pM, c.pM3]

    tiles = _mlp_tiles()

    def stream_view(l, j, t, n, shifted):
        off = -1 if (shifted and j >= 4) else 0
        if l == 0:
            return XN1[:, XS2 * j + 2176 + t + off: XS2 * j + 2176 + t + off + n]
        else:
            return XFF[:, XSA * j + 1536 + t + off: XSA * j + 1536 + t + off + n]

    def dst_view(l, j, t, n):
        if l == 0:
            return XFF[:, XSA * j + 1536 + t: XSA * j + 1536 + t + n]
        else:
            return XN1[:, XSB * j + 512 + t: XSB * j + 512 + t + n]

    bar0 = [(k2, v2) for k2, v2 in s.src_cnt.items()]
    l2dbg = os.environ.get("MK_L2DBG") == "1"
    if l2dbg:
        dbg_xa = nc.dram_tensor("dbg_xa", [128, 28672], BF16, kind="ExternalOutput")
        dbg_hn = nc.dram_tensor("dbg_hn", [128, 33792], BF16, kind="ExternalOutput")
        dbg_as = nc.dram_tensor("dbg_as", [2, 128, 2048], BF16, kind="ExternalOutput")
        dbg_pnl = nc.dram_tensor("dbg_pnl", [128, 1024], BF16, kind="ExternalOutput")

    chain_prev = [None, None]
    tp_lastp = [None, None]
    exp_prev = [None, None]
    tp_ev = None
    qk_prev = None
    fo_prev = None
    ofcopy_prev = [None, None]
    woadd_prev = None
    wo_mm_prev = None
    mlp2_last = None
    p1_last = None
    qk_layer_last = None

    for l in range(2):
        kv_lo = -4 if l == 0 else -2
        q_lo = kv_lo + 1
        wdep = bar0 if l == 0 else [p1_last, mlp2_last]
        wq_tok = s.add("gpsimd", lambda e, l=l: e.dma_start(
            out=WQB[:, 0:12288].rearrange("p (kc n) -> p kc n", n=1536),
            in_=wqkv_in.ap()[l].rearrange("(kc p) n -> p kc n", p=128)),
            dma=True, deps=wdep, stream="wq")
        wol_tok = s.add("gpsimd", lambda e, l=l: e.dma_start(
            out=WOB[:, 0:4096].rearrange("p (icn n) -> p icn n", n=1024),
            in_=wo_in.ap()[l].rearrange("(icn p) n -> p icn n", p=128)),
            dma=True, deps=wdep + [wo_mm_prev], stream="wol")

        def proj_post(p, mmtok, tco, tsi, evcol, out_view, extra=None):
            ch = Chain(s, deps=[mmtok, chain_prev[0], chain_prev[1],
                                tp_lastp[0], tp_lastp[1]] + (extra or []))
            QF, SQ, BC, ROT, W1_, W2_ = (QFp_[p], SQp_[p], BCp_[p], ROTp_[p],
                                         W1p_[p], W2p_[p])
            cps = cps_[p]; src_ps = pKV_[p]
            ch.step("vector", lambda e: e.tensor_copy(QF, src_ps[:, 0:512]))
            ch.step("scalar", lambda e: e.square(SQ, QF))
            ch.step("tensor", lambda e: e.matmul(cps[0:1, 0:512], e2c[:, 0:1], SQ,
                                                 start=True, stop=True))
            ch.step("scalar", lambda e: e.sqrt(R2a, cps[0:1, 0:512]))
            ch.step("tensor", lambda e: e.matmul(cps[0:1, 0:512], e2c[:, 1:2], SQ,
                                                 start=True, stop=True))
            ch.step("scalar", lambda e: e.sqrt(R2b, cps[0:1, 0:512]))
            ch.step("vector", lambda e: e.tensor_single_scalar(R2full, R2full, 1e-12, ALU.max))
            ch.step("scalar", lambda e: e.activation(R2full, R2full, AF.Ln))
            ch.step("scalar", lambda e: e.activation(R2full, R2full, AF.Exp, scale=-1.0))
            ch.step("tensor", lambda e: e.matmul(cps[:, 0:512], ones_rowf[:, :], R2a,
                                                 start=True, stop=True))
            ch.step("vector", lambda e, evcol=evcol: e.tensor_scalar_mul(
                BC, cps[:, 0:512], e2v[:, evcol:evcol + 1]))
            ch.step("tensor", lambda e: e.matmul(cps[:, 0:512], ones_rowf[:, :], R2b,
                                                 start=True, stop=True))
            ch.step("vector", lambda e, evcol=evcol: e.scalar_tensor_tensor(
                BC, cps[:, 0:512], e2v[:, evcol + 1:evcol + 2], BC,
                ALU.mult, ALU.add))
            ch.step("vector", lambda e: e.tensor_mul(W1_, QF, BC))
            ch.step("vector", lambda e: e.tensor_scalar_mul(ROT[0:32, :], W1_[32:64, :], -1.0))
            ch.step("vector", lambda e: e.tensor_copy(ROT[32:64, :], W1_[0:32, :]))
            ch.step("vector", lambda e: e.tensor_scalar_mul(ROT[64:96, :], W1_[96:128, :], -1.0))
            ch.step("vector", lambda e: e.tensor_copy(ROT[96:128, :], W1_[64:96, :]))
            ch.step("vector", lambda e, tco=tco: e.tensor_mul(W2_, W1_, tco))
            ch.step("vector", lambda e, tsi=tsi: e.tensor_mul(ROT, ROT, tsi))
            tok = ch.step("vector", lambda e, out_view=out_view: e.tensor_add(out_view, W2_, ROT))
            chain_prev[p] = tok
            return tok

        msk_tok = None
        sm_last = None
        tco_k_tok = None
        tco_q_tok = None
        vc_free = [None, None]
        for w in range(kv_lo, 4):
            par = (w - kv_lo) % 2
            KBc, VBc = KB[par], VB[par]
            KBp, VBp = KB[1 - par], VB[1 - par]
            t0 = 512 * w
            tco_k_tok = s.add("sync", lambda e, w=w: e.dma_start(
                out=TCOk, in_=kcos_in.ap()[:, 2048 + 512 * w:2048 + 512 * (w + 1)]),
                dma=True, deps=wdep + [chain_prev[0], chain_prev[1]], stream="tk")
            tsi_k_tok = s.add("sync", lambda e, w=w: e.dma_start(
                out=TSIk, in_=ksin_in.ap()[:, 2048 + 512 * w:2048 + 512 * (w + 1)]),
                dma=True, deps=wdep + [chain_prev[0], chain_prev[1]], stream="tk")
            kadd_last = None
            for ot in range(4):
                p = ot % 2
                def fk(e, l=l, ot=ot, t0=t0):
                    last = None
                    for kc in range(8):
                        last = e.matmul(pKV_[ot % 2][:, 0:512],
                                        WQB[:, 1536 * kc + 512 + 128 * ot: 1536 * kc + 512 + 128 * (ot + 1)],
                                        stream_view(l, kc, t0, 512, True),
                                        start=(kc == 0), stop=(kc == 7))
                    return last
                mmk = s.add("tensor", fk, deps=wdep + [
                    wq_tok, chain_prev[p], ofcopy_prev[0], ofcopy_prev[1],
                    woadd_prev, qk_prev, vc_free[ot % 2]])
                kadd_last = proj_post(p, mmk, TCOk, TSIk, 4 * l + 2,
                                      KBc[:, 512 * ot:512 * (ot + 1)],
                                      extra=[tco_k_tok, tsi_k_tok, qk_prev])
            vcopy_last = None
            for tt in range(4):
                def fv(e, l=l, tt=tt, t0=t0):
                    last = None
                    for kc in range(8):
                        last = e.matmul(pKV_[tt % 2][:, 0:512],
                                        stream_view(l, kc, t0 + 128 * tt, 128, True),
                                        WQB[:, 1536 * kc + 1024:1536 * kc + 1536],
                                        start=(kc == 0), stop=(kc == 7))
                    return last
                mmv = s.add("tensor", fv, deps=[wq_tok, chain_prev[tt % 2],
                                                fo_prev, vc_free[tt % 2]])
                vcopy_last = s.add("vector", lambda e, tt=tt, VBc=VBc: e.tensor_copy(
                    VBc[:, 512 * tt:512 * (tt + 1)], pKV_[tt % 2][:, 0:512]),
                    deps=[mmv, fo_prev])
                vc_free[tt % 2] = vcopy_last
            if w < q_lo:
                qk_prev = mmv
                continue
            tco_q_tok = s.add("sync", lambda e, w=w: e.dma_start(
                out=TCOq, in_=qcos_in.ap()[:, 2048 + 512 * w:2048 + 512 * (w + 1)]),
                dma=True, deps=[kadd_last, chain_prev[0], chain_prev[1]], stream="tk")
            tsi_q_tok = s.add("sync", lambda e, w=w: e.dma_start(
                out=TSIq, in_=qsin_in.ap()[:, 2048 + 512 * w:2048 + 512 * (w + 1)]),
                dma=True, deps=[kadd_last, chain_prev[0], chain_prev[1]], stream="tk")
            qadd = [None] * 4
            for ot in range(4):
                p = ot % 2
                def fq(e, l=l, ot=ot, t0=t0):
                    last = None
                    for kc in range(8):
                        last = e.matmul(pKV_[ot % 2][:, 0:512],
                                        WQB[:, 1536 * kc + 128 * ot: 1536 * kc + 128 * (ot + 1)],
                                        stream_view(l, kc, t0, 512, True),
                                        start=(kc == 0), stop=(kc == 7))
                    return last
                mmq = s.add("tensor", fq, deps=[wq_tok, chain_prev[p], qk_prev,
                                                vc_free[ot % 2]])
                qadd[ot] = proj_post(p, mmq, TCOq, TSIq, 4 * l,
                                     QBF[:, 512 * ot:512 * (ot + 1)],
                                     extra=[tco_q_tok, tsi_q_tok, qk_prev])
            msrc = mskF_src if w == 0 else mskS_src
            of_toks = []

            def emit_tpfo(unit):
                nonlocal tp_ev, fo_prev
                p, EF, h, ic, mul_tok = unit
                ev_toks = []
                for jc in range(8):
                    tt2 = s.add("tensor", lambda e, jc=jc, EF=EF: e.transpose(
                        c.pT[:, :], EF[:, 128 * jc:128 * (jc + 1)], c.ident[:, :]),
                        deps=[mul_tok, tp_ev])
                    tp_lastp[p] = tt2
                    if jc < 4:
                        tp_ev = s.add("scalar", lambda e, jc=jc: e.copy(
                            ATT[:, 128 * jc:128 * (jc + 1)], c.pT[:, :]), deps=[tt2])
                    else:
                        tp_ev = s.add("vector", lambda e, jc=jc: e.tensor_copy(
                            ATT[:, 128 * jc:128 * (jc + 1)], c.pT[:, :]), deps=[tt2])
                    ev_toks.append(tp_ev)
                def fo(e, h=h, VBp=VBp, VBc=VBc, p=p):
                    last = None
                    for jc in range(8):
                        vb = VBp if jc < 4 else VBc
                        tt = jc % 4
                        last = e.matmul(c.pM[0:64, 128 * p:128 * (p + 1)],
                                        vb[:, 512 * tt + 64 * h: 512 * tt + 64 * (h + 1)],
                                        ATT[:, 128 * jc:128 * (jc + 1)],
                                        start=(jc == 0), stop=(jc == 7))
                    return last
                fo_prev = s.add("tensor", fo,
                                deps=ev_toks + [vcopy_last, ofcopy_prev[p]])
                ofcopy_prev[p] = s.add("scalar", lambda e, h=h, ic=ic, p=p: e.copy(
                    OF[(h % 2) * 64:(h % 2) * 64 + 64,
                       512 * (h // 2) + 128 * ic:512 * (h // 2) + 128 * (ic + 1)],
                    c.pM[0:64, 128 * p:128 * (p + 1)]), deps=[fo_prev])
                of_toks.append(ofcopy_prev[p])

            pend = None
            for ic in range(4):
                msk_tok = s.add("sync", lambda e, ic=ic, msrc=msrc: e.dma_start(
                    out=MSK[:, :], in_=msrc[128 * ic:128 * (ic + 1), :]),
                    dma=True, stream="msk",
                    deps=wdep + (bar0 if msk_tok is None else [sm_last, msk_tok]))
                for h in range(8):
                    p = h % 2
                    rows = (h % 2) * 64
                    tl = h // 2
                    SM = SMp_[p]; EF = EFp_[p]; pQK = pQK_[p]
                    qsl = QBF[rows:rows + 64, 512 * tl + 128 * ic: 512 * tl + 128 * (ic + 1)]
                    mmA = s.add("tensor", lambda e, qsl=qsl, KBp=KBp, rows=rows, tl=tl, pQK=pQK: e.matmul(
                        pQK[:, 0:512], qsl, KBp[rows:rows + 64, 512 * tl:512 * (tl + 1)],
                        start=True, stop=True),
                        deps=[qadd[3], kadd_last, vcopy_last, exp_prev[p]])
                    mmB = s.add("tensor", lambda e, qsl=qsl, KBc=KBc, rows=rows, tl=tl, pQK=pQK: e.matmul(
                        pQK[:, 512:1024], qsl, KBc[rows:rows + 64, 512 * tl:512 * (tl + 1)],
                        start=True, stop=True), deps=[])
                    sm_last = s.add("vector", lambda e, SM=SM, pQK=pQK: e.tensor_tensor(
                        SM, pQK[:, :], MSK[:, :], ALU.add),
                        deps=[mmA, mmB, msk_tok, exp_prev[p]])
                    ch = Chain(s, deps=[sm_last])
                    ch.step("vector", lambda e, SM=SM, p=p: e.reduce_max(
                        c.lnm[:, 4 * p:4 * p + 1], SM, AX))
                    ch.step("vector", lambda e, p=p: e.tensor_scalar_mul(
                        c.lnm[:, 4 * p + 1:4 * p + 2], c.lnm[:, 4 * p:4 * p + 1], -1.0))
                    exp_prev[p] = ch.step("scalar", lambda e, SM=SM, EF=EF, p=p: e.activation(
                        EF, SM, AF.Exp, bias=c.lnm[:, 4 * p + 1:4 * p + 2],
                        accum_out=c.lnm[:, 4 * p + 2:4 * p + 3]),
                        extra=[tp_lastp[p]])
                    ch.step("scalar", lambda e, p=p: e.activation(
                        c.lnm[:, 4 * p + 3:4 * p + 4], c.lnm[:, 4 * p + 2:4 * p + 3], AF.Ln))
                    ch.step("scalar", lambda e, p=p: e.activation(
                        c.lnm[:, 4 * p + 3:4 * p + 4], c.lnm[:, 4 * p + 3:4 * p + 4], AF.Exp, scale=-1.0))
                    mul_tok = ch.step("vector", lambda e, EF=EF, p=p: e.tensor_scalar_mul(
                        EF, EF, c.lnm[:, 4 * p + 3:4 * p + 4]))
                    unit = (p, EF, h, ic, mul_tok)
                    if pend is not None:
                        emit_tpfo(pend)
                    pend = unit
            emit_tpfo(pend)
            woadds = [None, None]
            for dm in range(8):
                p = dm % 2
                pW = c.pM2 if p == 0 else c.pM3
                def fwo(e, dm=dm, pW=pW):
                    last = None
                    for icn in range(4):
                        last = e.matmul(pW[:, 0:512],
                                        WOB[:, 1024 * icn + 128 * dm: 1024 * icn + 128 * (dm + 1)],
                                        OF[:, 512 * icn:512 * (icn + 1)],
                                        start=(icn == 0), stop=(icn == 3))
                    return last
                wo_mm_prev = s.add("tensor", fwo,
                                   deps=[ofcopy_prev[0], ofcopy_prev[1], wol_tok,
                                         woadds[p], chain_prev[p]])
                woadds[p] = s.add("vector", lambda e, l=l, dm=dm, t0=t0, pW=pW: e.tensor_tensor(
                    dst_view(l, dm, t0, 512), pW[:, 0:512],
                    stream_view(l, dm, t0, 512, False), ALU.add),
                    deps=[wo_mm_prev] + (bar0 if l == 0 else [mlp2_last]))
                woadd_prev = woadds[p]
            qk_prev = wo_mm_prev
        qk_layer_last = qk_prev
        halo_last = None
        for j in range(8):
            if l == 0:
                halo_last = s.add("vector", lambda e, j=j: e.tensor_scalar_mul(
                    XFF[:, XSA * j:XSA * j + 1536], XFF[:, XSA * j:XSA * j + 1536],
                    flagc[:, 0:1]), deps=[woadd_prev])
            else:
                halo_last = s.add("vector", lambda e, j=j: e.tensor_scalar_mul(
                    XN1[:, XSB * j:XSB * j + 512], XN1[:, XSB * j:XSB * j + 512],
                    flagc[:, 0:1]), deps=[woadd_prev])
        if l2dbg and l == 0:
            dxa_tok = s.add("sync", lambda e: e.dma_start(
                out=dbg_xa.ap(), in_=XFF[:, :]), dma=True,
                deps=[halo_last], stream="dbgxa")
        NTC = 7 if l == 0 else 4
        tb = -1536 if l == 0 else 0

        def hn_view(l_, j, t, n):
            if l_ == 0:
                return XN1[:, 3584 * j + 1536 + t: 3584 * j + 1536 + t + n]
            return XFF[:, 2048 * j + t: 2048 * j + t + n]

        def mlp_src(l_, j, t, n, shifted):
            off = -1 if (shifted and j >= 4) else 0
            if l_ == 0:
                return XFF[:, XSA * j + 1536 + t + off: XSA * j + 1536 + t + off + n]
            else:
                return XN1[:, XSB * j + 512 + t + off: XSB * j + 512 + t + off + n]

        SQf_ = [AR[:, 0:512], AR[:, 4096:4608]]
        Tf_ = [AR[:, 512:1024], AR[:, 4608:5120]]
        mB_ = [AR[:, 1024:1536], AR[:, 5120:5632]]
        rB_ = [AR[:, 1536:2048], AR[:, 5632:6144]]
        mrow_ = [AR[0:1, 2048:2560], AR[0:1, 3072:3584]]
        vrow_ = [AR[0:1, 2560:3072], AR[0:1, 3584:4096]]
        qrow_ = [Tf_[0][0:1, :], Tf_[1][0:1, :]]
        hn_toks = {}
        stat_prev = [None, None]
        for tc in range(NTC):
            p = tc % 2
            t0 = tb + 512 * tc
            pS = pQK_[p]
            ch = Chain(s, deps=[halo_last, stat_prev[p]])
            def fsum(e, l=l, t0=t0, pS=pS):
                last = None
                for f in range(8):
                    last = e.matmul(pS[0:1, 0:512], ones_colb[:, :],
                                    mlp_src(l, f, t0, 512, True),
                                    start=(f == 0), stop=(f == 7))
                return last
            ch.step("tensor", fsum)
            for f in range(8):
                ch.step("scalar", lambda e, l=l, f=f, t0=t0, p=p: e.square(
                    SQf_[p], mlp_src(l, f, t0, 512, True)))
                ch.step("tensor", lambda e, f=f, pS=pS, p=p: e.matmul(
                    pS[0:1, 512:1024], ones_colf[:, :], SQf_[p],
                    start=(f == 0), stop=(f == 7)))
            ch.step("vector", lambda e, p=p, pS=pS: e.tensor_scalar_mul(
                mrow_[p], pS[0:1, 0:512], 1.0 / 1024.0))
            ch.step("vector", lambda e, p=p, pS=pS: e.tensor_scalar_mul(
                vrow_[p], pS[0:1, 512:1024], 1.0 / 1024.0))
            ch.step("scalar", lambda e, p=p: e.square(qrow_[p], mrow_[p]))
            ch.step("vector", lambda e, p=p: e.tensor_sub(vrow_[p], vrow_[p], qrow_[p]))
            ch.step("scalar", lambda e, p=p: e.activation(vrow_[p], vrow_[p], AF.Ln,
                                                          bias=c.cst[0:1, 0:1]))
            ch.step("scalar", lambda e, p=p: e.activation(vrow_[p], vrow_[p], AF.Exp, scale=-0.5))
            ch.step("tensor", lambda e, p=p, pS=pS: e.matmul(
                pS[:, 0:512], ones_rowf[:, :], mrow_[p], start=True, stop=True))
            ch.step("vector", lambda e, p=p, pS=pS: e.tensor_copy(mB_[p], pS[:, 0:512]))
            ch.step("tensor", lambda e, p=p, pS=pS: e.matmul(
                pS[:, 512:1024], ones_rowf[:, :], vrow_[p], start=True, stop=True))
            ch.step("vector", lambda e, p=p, pS=pS: e.tensor_copy(rB_[p], pS[0:128, 512:1024]))
            for f in range(8):
                ch.step("vector", lambda e, l=l, f=f, t0=t0, p=p: e.tensor_sub(
                    Tf_[p], mlp_src(l, f, t0, 512, True), mB_[p]))
                ch.step("vector", lambda e, p=p: e.tensor_mul(Tf_[p], Tf_[p], rB_[p]))
                tok = ch.step("vector", lambda e, l=l, f=f, t0=t0, p=p: e.tensor_scalar(
                    hn_view(l, f, t0, 512),
                    Tf_[p], lng[:, 8 * l + f:8 * l + f + 1], lnb[:, 8 * l + f:8 * l + f + 1],
                    ALU.mult, ALU.add))
            hn_toks[tc] = tok
            stat_prev[p] = tok
        if l2dbg and l == 0:
            dhn_tok = s.add("sync", lambda e: e.dma_start(
                out=dbg_hn.ap(), in_=XN1[:, :]), dma=True,
                deps=[hn_toks[tc2] for tc2 in range(NTC)], stream="dbghn")
        PNL_ = [WQB[:, 0:1024], WQB[:, 1024:2048]]
        act_hbm = actA_hbm if l == 0 else actB_hbm
        tc_groups = [(0, 4), (4, 7)] if l == 0 else [(0, 4)]
        w1p3 = w1p_in.ap()[l].rearrange("(kc p) n -> p kc n", p=128)
        pnl_rd = [qk_layer_last, qk_layer_last]
        p1_free = [None, None]
        spill_prev = [None, None]
        spill_by_tc = {}
        gelu_prev = qk_layer_last
        mul_prev = None
        gi = 0
        mmi1 = 0
        p1ser = os.environ.get("MK_P1SER") == "1"
        def dd(lst):
            return None if p1ser else lst
        for i, (lo, hi) in enumerate(tiles):
            rn = hi - lo
            for (g0, g1) in tc_groups:
                buf = gi % 2
                ld_a = s.add("gpsimd", lambda e, lo=lo, rn=rn, buf=buf, w1p3=w1p3: e.dma_start(
                    out=PNL_[buf].rearrange("p (kc n) -> p kc n", n=128)[:, :, 0:rn],
                    in_=w1p3[:, :, 2 * lo:2 * lo + rn]), dma=True,
                    deps=dd([pnl_rd[buf]]), stream=f"p{buf}")
                gi += 1
                a_toks = {}
                for tc in range(g0, g1):
                    pP = pKV_[mmi1 % 2]
                    mm = s.add("tensor", lambda e, l=l, rn=rn, tc=tc, tb=tb, buf=buf, pP=pP: [
                        e.matmul(pP[0:rn, 0:512],
                                 PNL_[buf][:, 128 * kc:128 * kc + rn],
                                 hn_view(l, kc, tb + 512 * tc, 512),
                                 start=(kc == 0), stop=(kc == 7)) for kc in range(8)][-1],
                        deps=dd([ld_a, hn_toks[tc], p1_free[mmi1 % 2]]))
                    pnl_rd[buf] = mm
                    a_toks[tc] = s.add("vector", lambda e, l=l, i=i, rn=rn, tc=tc, g0=g0, pP=pP: e.tensor_scalar_add(
                        ASTORE[0:rn, 512 * (tc - g0):512 * (tc - g0) + 512], pP[0:rn, 0:512],
                        b1pc[0:rn, 44 * l + 2 * i:44 * l + 2 * i + 1]),
                        deps=dd([mm, mul_prev]))
                    p1_free[mmi1 % 2] = a_toks[tc]
                    mmi1 += 1
                buf = gi % 2
                ld_g = s.add("gpsimd", lambda e, lo=lo, rn=rn, buf=buf, w1p3=w1p3: e.dma_start(
                    out=PNL_[buf].rearrange("p (kc n) -> p kc n", n=128)[:, :, 0:rn],
                    in_=w1p3[:, :, 2 * lo + rn:2 * lo + 2 * rn]), dma=True,
                    deps=dd([pnl_rd[buf]]), stream=f"p{buf}")
                gi += 1
                for tc in range(g0, g1):
                    sp = (i + tc) % 2
                    pP = pKV_[mmi1 % 2]
                    mm = s.add("tensor", lambda e, l=l, rn=rn, tc=tc, tb=tb, buf=buf, pP=pP: [
                        e.matmul(pP[0:rn, 0:512],
                                 PNL_[buf][:, 128 * kc:128 * kc + rn],
                                 hn_view(l, kc, tb + 512 * tc, 512),
                                 start=(kc == 0), stop=(kc == 7)) for kc in range(8)][-1],
                        deps=dd([ld_g, hn_toks[tc], p1_free[mmi1 % 2]]))
                    pnl_rd[buf] = mm
                    p1_last = mm
                    gelu_prev = s.add("scalar", lambda e, l=l, i=i, rn=rn, pP=pP: e.activation(
                        QBF[0:rn, 0:512], pP[0:rn, 0:512], AF.Gelu,
                        bias=b1pc[0:rn, 44 * l + 2 * i + 1:44 * l + 2 * i + 2]),
                        deps=dd([mm, gelu_prev, mul_prev]))
                    p1_free[mmi1 % 2] = gelu_prev
                    mmi1 += 1
                    stg = QBF[:, 512 + 512 * sp:1024 + 512 * sp]
                    if rn < 128:
                        mz = s.add("vector", lambda e, stg=stg: e.memset(stg, 0.0),
                                   deps=dd([gelu_prev, spill_prev[sp]]))
                    else:
                        mz = None
                    mul_prev = s.add("vector", lambda e, rn=rn, tc=tc, g0=g0, stg=stg: e.tensor_mul(
                        stg[0:rn, :], ASTORE[0:rn, 512 * (tc - g0):512 * (tc - g0) + 512],
                        QBF[0:rn, 0:512]),
                        deps=dd([gelu_prev, a_toks[tc], spill_prev[sp], mz]))
                    spill_prev[sp] = s.add("gpsimd", lambda e, tc=tc, i=i, act_hbm=act_hbm, stg=stg: e.dma_start(
                        out=act_hbm.ap()[tc, :, 512 * i:512 * (i + 1)],
                        in_=stg), dma=True, deps=dd([mul_prev]), stream=f"sp{sp}")
                    spill_by_tc.setdefault(tc, []).append(spill_prev[sp])
                if l2dbg and l == 0 and i == 0 and g0 == 0:
                    dtok = s.add("sync", lambda e: e.dma_start(
                        out=dbg_as.ap()[0], in_=ASTORE[:, :]), dma=True,
                        deps=[a_toks[3]], stream="dbgas")
                    dptok = s.add("sync", lambda e: e.dma_start(
                        out=dbg_pnl.ap(), in_=PNL_[0][:, :]), dma=True,
                        deps=[a_toks[3]], stream="dbgpnl")
                    mul_prev = dptok
                    dtok2 = s.add("sync", lambda e: e.dma_start(
                        out=dbg_as.ap()[1], in_=QBF[:, 0:2048]), dma=True,
                        deps=[gelu_prev, mul_prev], stream="dbgas2")
                    mul_prev = dtok
                    gelu_prev = dtok2
        if l == 0:
            W2L = XN1[:, 0:22528]
            ACTL = XN1[:, 22528:33792]
        else:
            W2L = XFF[:, 0:22528]
            ACTL = XN1[:, 20480:31744]
        w2l_tok = None
        for kc, (lo, hi) in enumerate(tiles):
            rn = hi - lo
            w2l_tok = s.add("gpsimd", lambda e, l=l, kc=kc, lo=lo, rn=rn, W2L=W2L: e.dma_start(
                out=W2L[0:rn, 1024 * kc:1024 * (kc + 1)],
                in_=w2_in.ap()[l, lo:lo + rn, :]), dma=True,
                deps=[p1_last] + ([dhn_tok, dxa_tok] if l2dbg and l == 0 else []),
                stream="w2l")
        fw2_prev = None
        fw2_free = [None, None]
        actl_tok = None
        for tc in range(NTC):
            actl_tok = s.add("sync", lambda e, tc=tc, act_hbm=act_hbm, ACTL=ACTL: e.dma_start(
                out=ACTL[:, 0:11264], in_=act_hbm.ap()[tc, :, :]), dma=True,
                deps=[p1_last, fw2_prev] + spill_by_tc[tc]
                     + ([dhn_tok, dxa_tok] if l2dbg and l == 0 else []),
                stream="actl")
            for h in range(8):
                p = h % 2
                pP = pKV_[p]
                def fw2(e, h=h, W2L=W2L, ACTL=ACTL, pP=pP):
                    last = None
                    for kc, (lo, hi) in enumerate(tiles):
                        rn = hi - lo
                        last = e.matmul(pP[:, 0:512],
                                        W2L[0:rn, 1024 * kc + 128 * h: 1024 * kc + 128 * (h + 1)],
                                        ACTL[0:rn, 512 * kc:512 * (kc + 1)],
                                        start=(kc == 0), stop=(kc == 21))
                    return last
                fw2_prev = s.add("tensor", fw2,
                                 deps=[actl_tok, w2l_tok, fw2_free[p]])
                mlp2_last = s.add("vector", lambda e, l=l, h=h, tc=tc, tb=tb, pP=pP: e.scalar_tensor_tensor(
                    mlp_src(l, h, tb + 512 * tc, 512, False),
                    pP[:, 0:512], b2c[:, 8 * l + h:8 * l + h + 1],
                    mlp_src(l, h, tb + 512 * tc, 512, False), ALU.add, ALU.add),
                    deps=[fw2_prev])
                fw2_free[p] = mlp2_last

    for i in range(16):
        s.add("sync", lambda e, i=i: e.dma_start(
            out=STG.rearrange("p (j t) -> p j t", t=128),
            in_=x2pa[:, :, 128 * i:128 * (i + 1)]), dma=True)
        for j in range(8):
            s.add("vector", lambda e, i=i, j=j: e.tensor_tensor(
                STG[:, 128 * j:128 * (j + 1)], STG[:, 128 * j:128 * (j + 1)],
                XN1[:, XSB * j + 512 + 128 * i: XSB * j + 512 + 128 * (i + 1)], ALU.add))
            s.add("tensor", lambda e, j=j: e.transpose(
                c.pT[:, :], STG[:, 128 * j:128 * (j + 1)], c.ident[:, :]))
            s.add("vector", lambda e, j=j: e.tensor_copy(
                c.lnt[:, 128 * j:128 * (j + 1)], c.pT[:, :]))
        s.add("sync", lambda e, i=i: e.dma_start(
            out=x2_out.ap()[128 * i:128 * (i + 1), :], in_=c.lnt[:, 0:1024]), dma=True)


def host_consts_l2(s_idx):
    tloc = np.arange(-2048, 2048, dtype=np.float64)
    p = 2048.0 * s_idx + tloc
    half = np.arange(0, 64, 2, dtype=np.float64)
    inv = 1.0 / 10000.0 ** (half / 64)
    fr = p[:, None] * inv[None, :]
    fr = np.concatenate([fr, fr], axis=-1)
    base = (half + 0.4 * 64) / (1.4 * 64)
    power = (p - 4096.0) / 256.0
    sc = base[None, :] ** power[:, None]
    sc = np.concatenate([sc, sc], axis=-1)
    co, si = np.cos(fr), np.sin(fr)
    qcos = (co * sc * 8.0).T
    qsin = (si * sc * 8.0).T
    kcos = (co / sc).T
    ksin = (si / sc).T
    def dup(a):
        return _f32(np.concatenate([a, a], axis=0))
    i = np.arange(512)[:, None]
    j = np.arange(1024)[None, :]
    d = i - (j - 512)
    bad = (d < 0) | (d > 512)
    mstd = np.where(bad, -3.0e38, 0.0).astype(ml_dtypes.bfloat16)
    badf = bad | (j < 512)
    mfirst = np.where(badf, -3.0e38, 0.0).astype(ml_dtypes.bfloat16)
    return dict(t_qcos=dup(qcos), t_qsin=dup(qsin), t_kcos=dup(kcos),
                t_ksin=dup(ksin), mask_std=mstd, mask_first=mfirst)


def host_inputs_launch2(inputs, consts, x1_full, usums):
    x1 = np.asarray(x1_full, np.float32)
    wqkv = _f32(inputs["L1_ffn_wqkv"])
    wo = _f32(inputs["L1_ffn_wo"])
    w1 = np.asarray(inputs["L1_ffn_w1"], np.float32)
    b1 = np.asarray(inputs["L1_ffn_b1"], np.float32)
    perm = w1_perm()
    w1p = _f32(w1[:, :, perm])
    b1p = b1[:, perm]
    tiles = _mlp_tiles()
    b1p_ct = np.zeros((128, 88), np.float32)
    for l in range(2):
        off = 0
        for i, (lo, hi) in enumerate(tiles):
            rn = hi - lo
            b1p_ct[0:rn, 44 * l + 2 * i] = b1p[l, off:off + rn]
            b1p_ct[0:rn, 44 * l + 2 * i + 1] = b1p[l, off + rn:off + 2 * rn]
            off += 2 * rn
    w2 = _f32(inputs["L1_ffn_w2"])
    def ct2(a):
        out = np.zeros((128, 16), np.float32)
        for l in range(2):
            out[:, 8 * l:8 * (l + 1)] = np.asarray(a, np.float32)[l].reshape(8, 128).T
        return out
    b2c = ct2(inputs["L1_ffn_b2"])
    lng = ct2(inputs["L1_ffn_ln_g"])
    lnb = ct2(inputs["L1_ffn_ln_b"])
    wrkv = _f32(inputs["L1_attn_W_rkv"])
    brkv = np.asarray(inputs["L1_attn_b_rkv"], np.float32)
    brkv_ct = np.zeros((128, 24), np.float32)
    for part in range(3):
        brkv_ct[:, 8 * part:8 * (part + 1)] = brkv[1024 * part:1024 * (part + 1)].reshape(8, 128).T
    wo2 = _f32(inputs["L1_attn_W_o"])
    bo_ct = ctile(inputs["L1_attn_b_o"])
    nul = ctile(inputs["L1_attn_nu_log"])
    gl = ctile(inputs["L1_attn_gamma_log"])
    qs = np.asarray(inputs["L1_ffn_q_scale"], np.float32)
    ks = np.asarray(inputs["L1_ffn_k_scale"], np.float32)
    e2vv = np.zeros((128, 8), np.float32)
    for l in range(2):
        e2vv[0:64, 4 * l + 0] = qs[l]
        e2vv[64:128, 4 * l + 1] = qs[l]
        e2vv[0:64, 4 * l + 2] = ks[l]
        e2vv[64:128, 4 * l + 3] = ks[l]
    e2c = np.zeros((128, 2), np.float32)
    e2c[0:64, 0] = 1.0
    e2c[64:128, 1] = 1.0
    maps = []
    for core in range(8):
        b, sq = core // 4, core % 4
        xe = np.zeros((4224, 1024), np.float32)
        xe[2176:4224] = x1[b, sq * SEG:(sq + 1) * SEG]
        lo_g = sq * SEG - 2049
        nvalid = min(2049, 2049 + lo_g) if lo_g < 0 else 2049
        if lo_g >= 0:
            xe[127:2176] = x1[b, lo_g:lo_g + 2049]
        elif nvalid > 0:
            xe[2176 - nvalid:2176] = x1[b, 0:nvalid]
        up = np.zeros((128, 48), np.float32)
        for kslot in range(3):
            seg = sq - 3 + kslot
            if seg >= 0:
                up[:, 16 * kslot:16 * (kslot + 1)] = usums[b * 4 + seg]
        cz = host_consts_l2(sq)
        maps.append(dict(
            x1e=xe, usum_pred=up, wqkv=wqkv, wo=wo, w1p=w1p, b1p_ct=b1p_ct,
            w2=w2, b2_ct=b2c, lng_ct=lng, lnb_ct=lnb, wrkv=wrkv,
            brkv_ct=brkv_ct, wo2=wo2, bo_ct=bo_ct, nul_ct=nul, gl_ct=gl,
            t_qcos=cz["t_qcos"], t_qsin=cz["t_qsin"], t_kcos=cz["t_kcos"],
            t_ksin=cz["t_ksin"], mask_std=cz["mask_std"],
            mask_first=cz["mask_first"] if sq == 0 else cz["mask_std"],
            c_e2=e2c, c_e2v=e2vv,
            haloflag=np.full((128, 1), 0.0 if sq == 0 else 1.0, np.float32),
            c_ident=consts["c_ident"]))
    return maps



_CACHE = {}


def kernel(**inputs):
    consts = host_consts_l01()
    if "nc0" not in _CACHE:
        _CACHE["nc0"] = build_launch0()
        _CACHE["nc1"] = build_launch1()
        _CACHE["nc2"] = build_launch2()
    nc0, nc1, nc2 = _CACHE["nc0"], _CACHE["nc1"], _CACHE["nc2"]

    maps0 = host_inputs_launch0(inputs, consts)
    res0 = run_bass_kernel_spmd(nc0, maps0, core_ids=list(range(8)))
    wsums = [np.asarray(res0.results[c]["wsum"]) for c in range(8)]

    maps1 = host_inputs_launch1(inputs, consts, wsums)
    res1 = run_bass_kernel_spmd(nc1, maps1, core_ids=list(range(8)))
    x1 = np.zeros((2, 8192, 1024), np.float32)
    usums = []
    for c in range(8):
        b, sq = c // 4, c % 4
        x1[b, sq * SEG:(sq + 1) * SEG] = np.asarray(res1.results[c]["x1"])
        usums.append(np.asarray(res1.results[c]["usum"]))

    maps2 = host_inputs_launch2(inputs, consts, x1, usums)
    res2 = run_bass_kernel_spmd(nc2, maps2, core_ids=list(range(8)))
    x2 = np.zeros((2, 8192, 1024), np.float32)
    for c in range(8):
        b, sq = c // 4, c % 4
        x2[b, sq * SEG:(sq + 1) * SEG] = np.asarray(res2.results[c]["x2seg"])
    return x2
